# version 04
# baseline (speedup 1.0000x reference)
import contextlib
import numpy as np
import concourse.bass as bass
import concourse.mybir as mybir
from concourse.bass_utils import run_bass_kernel_spmd

F32 = mybir.dt.float32
BF16 = mybir.dt.bfloat16
AF = mybir.ActivationFunctionType
ALU = mybir.AluOpType
AX = mybir.AxisListType

ENGINES = ("pe", "dve", "act", "pool", "sp")
EM_NSLOT = {"sp": 12, "pool": 6, "act": 2, "dve": 1, "pe": 1}
DM = 1024


class Em:

    def __init__(self, nc):
        self.nc = nc
        self.recs = {e: [] for e in ENGINES}
        self.gid = 0
        self.phase = {}

    def set_phase(self, toks):
        self.phase = {e: list(toks) for e in ENGINES}

    def _tok(self, eng, fn, waits, dma=False, serialize=False):
        extra = self.phase.pop(eng, None)
        if extra:
            waits = list(waits or ()) + extra
        ws = []
        for w in waits or ():
            if w is None:
                continue
            assert w["gid"] < self.gid, "wait on later instruction (cycle)"
            if not w["dma"]:
                w["marked"] = True
            ws.append(w)
        rec = dict(eng=eng, fn=fn, waits=ws, dma=dma, marked=dma,
                   serialize=serialize, gid=self.gid)
        self.gid += 1
        self.recs[eng].append(rec)
        return rec

    def pe(self, fn, waits=()):
        return self._tok("pe", fn, waits)

    def dve(self, fn, waits=()):
        return self._tok("dve", fn, waits)

    def act(self, fn, waits=()):
        return self._tok("act", fn, waits)

    def pool(self, fn, waits=()):
        return self._tok("pool", fn, waits)

    def frontier(self):
        toks = []
        for e in ENGINES:
            rs = self.recs[e]
            dmas = [r for r in rs if r["dma"]]
            toks += dmas[-EM_NSLOT[e]:]
            comp = [r for r in rs if not r["dma"]]
            if comp:
                toks.append(comp[-1])
        return toks

    def dma(self, queue, out, in_, waits=(), serialize=False, **kw):
        if out.dtype != in_.dtype:
            assert queue == "pool", "casting DMA must use gpsimd (SWDGE)"

        def fn(eng, out=out, in_=in_, kw=kw):
            return eng.dma_start(out=out, in_=in_, **kw)

        return self._tok(queue, fn, waits, dma=True, serialize=serialize)

    def build(self, serial=False, max_gid=None):
        nc = self.nc
        if max_gid is not None:
            for e in ENGINES:
                self.recs[e] = [r for r in self.recs[e] if r["gid"] <= max_gid]
        if serial:
            prev = None
            for e in ENGINES:
                pass
            order = sorted((r for e in ENGINES for r in self.recs[e]),
                           key=lambda r: r["gid"])
            for r in order:
                if prev is not None and prev not in r["waits"]:
                    r["waits"] = list(r["waits"]) + [prev]
                    if not prev["dma"]:
                        prev["marked"] = True
                r["marked"] = True
                prev = r
        NSLOT = EM_NSLOT
        for e in ENGINES:
            cc = 0
            dseq = 0
            for r in self.recs[e]:
                if r["dma"]:
                    r["slot"] = dseq % NSLOT[e]
                    r["use"] = dseq // NSLOT[e] + 1
                    r["count"] = 16 * r["use"]
                    dseq += 1
                elif r["marked"]:
                    cc += 1
                    r["count"] = cc
                else:
                    r["count"] = None

        stack = contextlib.ExitStack()
        sems = {}
        for e in ENGINES:
            sems[(e, -1)] = stack.enter_context(nc.semaphore(f"qc_{e}"))
            for s in range(NSLOT[e]):
                sems[(e, s)] = stack.enter_context(nc.semaphore(f"qd_{e}{s}"))

        def replay(eng_obj, e):
            seen = {}
            for r in self.recs[e]:
                if r["dma"] and r["use"] > 1:
                    key = (e, r["slot"])
                    v = 16 * (r["use"] - 1)
                    if seen.get(key, -1) < v:
                        eng_obj.wait_ge(sems[key], v)
                        seen[key] = v
                for w in r["waits"]:
                    key = (w["eng"], w["slot"] if w["dma"] else -1)
                    wv = w["count"]
                    assert wv is not None, "wait on unmarked instruction"
                    if seen.get(key, -1) < wv:
                        eng_obj.wait_ge(sems[key], wv)
                        seen[key] = wv
                inst = r["fn"](eng_obj)
                if r["dma"]:
                    inst.then_inc(sems[(e, r["slot"])], 16)
                elif r["marked"]:
                    inst.then_inc(sems[(e, -1)], 1)
            finals = {}
            for r in self.recs[e]:
                if r["dma"]:
                    finals[r["slot"]] = 16 * r["use"]
            for s, v in sorted(finals.items()):
                if seen.get((e, s), -1) < v:
                    eng_obj.wait_ge(sems[(e, s)], v)

        with nc.Block() as block:
            @block.tensor
            def _(t):
                replay(t, "pe")

            @block.vector
            def _(v):
                replay(v, "dve")

            @block.scalar
            def _(s):
                replay(s, "act")

            @block.gpsimd
            def _(g):
                replay(g, "pool")

            @block.sync
            def _(s):
                replay(s, "sp")
        stack.close()


def last(*toks):
    out = []
    for t in toks:
        if t is None:
            continue
        if isinstance(t, (list, tuple)):
            out.extend(x for x in t if x is not None)
        else:
            out.append(t)
    return out



def build_stage1(nc, em, io, seq=2048, tc=256):
    stS = contextlib.ExitStack()

    def A(name, shape, dt):
        return stS.enter_context(nc.sbuf_tensor(name, shape, dt))
    ntok = 2 * seq
    nch = ntok // tc
    cps = seq // tc

    x = io["x"]
    R1 = io["R1"]

    R0 = nc.dram_tensor("s1_R0", [DM, ntok], F32, kind="Internal").ap()
    xcD = nc.dram_tensor("s1_xc", [2048, ntok], F32, kind="Internal").ap()
    gD = nc.dram_tensor("s1_g", [2048, ntok], F32, kind="Internal").ap()
    bcD = nc.dram_tensor("s1_bc", [nch, 32, tc], F32, kind="Internal").ap()

    ident = A("s1_identS", [128, 128], F32).ap()
    ident_bf = A("ident_bf", [128, 128], BF16).ap()
    ones_col = A("ones_col", [128, 1], F32).ap()
    ones_row = A("ones_row", [1, 128], F32).ap()
    lnw = A("lnw", [128, 8], F32).ap()
    lnb = A("lnb", [128, 8], F32).ap()
    convw = A("convw", [128, 16, 4], F32).ap()
    convb = A("convb", [128, 16], F32).ap()
    dtb = A("dtb", [128, 16], F32).ap()
    Dp = A("Dp", [128, 16], F32).ap()
    Asb = A("Asb", [128, 16, 16], F32).ap()
    xpw = A("xpw", [128, 16, 96], BF16).ap()
    dtw = A("dtw", [64, 2048], BF16).ap()
    opw = A("opw", [128, 16, 1024], BF16).ap()

    t_id = em.dma("sp", ident, io["ident"])
    t_idb = em.dve(lambda v: v.tensor_copy(ident_bf, ident), waits=[t_id])
    t_oc = em.pool(lambda g: g.memset(ones_col, 1.0))
    t_or = em.pool(lambda g: g.memset(ones_row, 1.0))
    w_lnw = em.dma("sp", lnw, io["ln_w"])
    w_lnb = em.dma("sp", lnb, io["ln_b"])
    w_cw = em.dma("sp", convw, io["conv_w"])
    w_cb = em.dma("sp", convb, io["conv_b"])
    w_dtb = em.dma("sp", dtb, io["dt_proj_b"])
    w_D = em.dma("sp", Dp, io["D_param"])
    w_Al = em.dma("sp", Asb, io["A_log"])
    t_Ae = em.act(lambda s: s.activation(out=Asb, in_=Asb, func=AF.Exp),
                  waits=[w_Al])
    t_An = em.dve(lambda v: v.tensor_scalar_mul(Asb, Asb, -1.0), waits=[t_Ae])
    w_xp = em.dma("pool", xpw, io["x_proj_wT"].rearrange("(kc p) m -> p kc m", p=128))
    w_dt = em.dma("pool", dtw, io["dt_proj_wT"])
    w_op = em.dma("pool", opw, io["out_proj_wT"].rearrange("(kc p) m -> p kc m", p=128))

    PS = [stS.enter_context(nc.psum_tensor(f"s1_ps{i}", [128, 512], F32)).ap()
          for i in range(8)]

    stackA = contextlib.ExitStack()
    trT = stackA.enter_context(nc.sbuf_tensor("trT", [128, 1024], F32)).ap()
    trE = stackA.enter_context(nc.sbuf_tensor("trE", [128, 2, 128], F32)).ap()
    ipw = stackA.enter_context(nc.sbuf_tensor("ipw", [128, 8, 4096], BF16)).ap()
    rch = stackA.enter_context(nc.sbuf_tensor("rch", [128, 8, tc], F32)).ap()
    sqA = stackA.enter_context(nc.sbuf_tensor("sqA", [128, tc], F32)).ap()
    xnch = stackA.enter_context(nc.sbuf_tensor("xnch", [128, 8, tc], BF16)).ap()
    xinb = stackA.enter_context(nc.sbuf_tensor("xinb", [128, 16, tc + 3], F32)).ap()
    xcr = stackA.enter_context(nc.sbuf_tensor("xcr", [128, 2, tc], F32)).ap()
    gr = stackA.enter_context(nc.sbuf_tensor("gr", [128, 2, tc], F32)).ap()
    rows = stackA.enter_context(nc.sbuf_tensor("s1rows", [1, 4, tc], F32)).ap()

    xf = x.rearrange("b t d -> (b t) d")
    t_mm_last = None
    tok_evac = [None, None]
    tok_dma = [None, None]
    for tt in range(ntok // 128):
        t_load = em.dma("sp", trT, xf[tt * 128:(tt + 1) * 128, :],
                        waits=last(t_mm_last, t_id))
        for dc in range(8):
            bank = dc % 2
            tmm = em.pe(lambda t, dc=dc, bank=bank: t.matmul(
                PS[bank][:, 0:128], trT[:, dc * 128:(dc + 1) * 128], ident,
                start=True, stop=True),
                waits=last(t_load, tok_evac[bank], t_id))
            t_mm_last = tmm
            tev = em.dve(lambda v, bank=bank: v.tensor_copy(
                trE[:, bank, :], PS[bank][:, 0:128]),
                waits=last(tmm, tok_dma[bank]))
            tok_evac[bank] = tev
            td = em.dma("sp", R0[dc * 128:(dc + 1) * 128, tt * 128:(tt + 1) * 128],
                        trE[:, bank, :], waits=[tev])
            tok_dma[bank] = td

    w_ip = em.dma("pool", ipw, io["in_proj_wT"].rearrange("(kc p) m -> p kc m", p=128))

    phaseA_stores = [tok_dma[0], tok_dma[1]]
    t_hzero = None
    tok_sq_user = None
    tok_xc = [None, None]
    tok_g = [None, None]
    tok_xin = {}
    tok_ps = {}
    t_stats = None
    t_ipmm_last = None
    t_rch_last = None

    for ch in range(nch):
        t0 = ch * tc
        t_r = em.dma("sp", rch,
                     R0.rearrange("(dc p) T -> p dc T", p=128)[:, :, t0:t0 + tc],
                     waits=last(t_rch_last, tok_dma[0], tok_dma[1]))
        if ch % cps == 0:
            t_hzero = em.pool(lambda g: g.memset(xinb[:, :, 0:3], 0.0),
                              waits=last(*[tok_xin.get(("conv", m)) for m in range(16)]))
        t_mm1 = None
        t_mm2 = None
        for dc in range(8):
            t_mm1 = em.pe(lambda t, dc=dc: t.matmul(
                PS[2][0:1, 0:tc], ones_col, rch[:, dc, :],
                start=(dc == 0), stop=(dc == 7)),
                waits=last(t_r, t_oc) + (last(t_stats) if dc == 0 else []))
            t_sqv = em.act(lambda s, dc=dc: s.activation(
                out=sqA, in_=rch[:, dc, :], func=AF.Square),
                waits=last(t_r, tok_sq_user))
            t_mm2 = em.pe(lambda t, dc=dc: t.matmul(
                PS[3][0:1, 0:tc], ones_col, sqA,
                start=(dc == 0), stop=(dc == 7)),
                waits=last(t_sqv) + (last(t_stats) if dc == 0 else []))
            tok_sq_user = t_mm2
        mu = rows[:, 0, :]
        v1 = rows[:, 1, :]
        sc = rows[:, 2, :]
        tq = rows[:, 3, :]
        t_mu = em.dve(lambda v: v.tensor_scalar_mul(mu, PS[2][0:1, 0:tc], 1.0 / DM),
                      waits=last(t_mm1, t_stats))
        t_msq = em.act(lambda s: s.activation(out=tq, in_=mu, func=AF.Square),
                       waits=[t_mu])
        t_v1 = em.dve(lambda v: v.scalar_tensor_tensor(
            out=v1, in0=PS[3][0:1, 0:tc], scalar=1.0 / DM, in1=tq,
            op0=ALU.mult, op1=ALU.subtract), waits=[t_mm2, t_msq])
        t_a = em.dve(lambda v: v.tensor_scalar_add(tq, v1, 1e-5), waits=[t_v1, t_msq])
        t_rcp = em.dve(lambda v: v.reciprocal(sc, tq), waits=[t_a])
        t_b = em.dve(lambda v: v.tensor_tensor(out=sc, in0=v1, in1=sc, op=ALU.mult),
                     waits=[t_rcp])
        t_b2 = em.dve(lambda v: v.tensor_scalar_add(sc, sc, 1e-5), waits=[t_b])
        t_c = em.dve(lambda v: v.tensor_tensor(out=sc, in0=sc, in1=tq, op=ALU.mult),
                     waits=[t_b2])
        t_sd = em.act(lambda s: s.activation(out=sc, in_=sc, func=AF.Sqrt),
                      waits=[t_c])
        t_sc = em.dve(lambda v: v.reciprocal(sc, sc), waits=[t_sd])
        t_nb = em.dve(lambda v: v.tensor_tensor(out=tq, in0=mu, in1=sc, op=ALU.mult),
                      waits=[t_sc])
        t_nb2 = em.dve(lambda v: v.tensor_scalar_mul(tq, tq, -1.0), waits=[t_nb])
        t_stats = t_nb2
        t_srep = em.pe(lambda t: t.matmul(PS[4][:, 0:tc], ones_row, sc,
                                          start=True, stop=True),
                       waits=last(t_sc, t_or, tok_ps.get(4)))
        t_brep = em.pe(lambda t: t.matmul(PS[5][:, 0:tc], ones_row, tq,
                                          start=True, stop=True),
                       waits=last(t_nb2, t_or, tok_ps.get(5)))
        t_xns = []
        for dc in range(8):
            t1 = em.dve(lambda v, dc=dc: v.tensor_tensor(
                out=sqA, in0=rch[:, dc, :], in1=PS[4][:, 0:tc], op=ALU.mult),
                waits=last(t_srep, tok_sq_user))
            t2 = em.dve(lambda v, dc=dc: v.tensor_tensor(
                out=sqA, in0=sqA, in1=PS[5][:, 0:tc], op=ALU.add),
                waits=[t_brep, t1])
            t3 = em.dve(lambda v, dc=dc: v.tensor_scalar(
                out=xnch[:, dc, :], in0=sqA, scalar1=lnw[:, dc:dc + 1],
                scalar2=lnb[:, dc:dc + 1], op0=ALU.mult, op1=ALU.add),
                waits=last(t2, w_lnw, w_lnb, t_ipmm_last))
            tok_sq_user = t3
            t_xns.append(t3)
        tok_ps[4] = t_xns[-1]
        tok_ps[5] = t_xns[-1]
        t_rch_last = t_xns[-1]

        for m in range(32):
            bank = m % 2
            t_mm = None
            for kc in range(8):
                t_mm = em.pe(lambda t, m=m, kc=kc, bank=bank: t.matmul(
                    PS[bank][:, 0:tc], ipw[:, kc, m * 128:(m + 1) * 128],
                    xnch[:, kc, :], start=(kc == 0), stop=(kc == 7)),
                    waits=last(t_xns[kc], w_ip) + (last(tok_ps.get(bank)) if kc == 0 else []))
            if m < 16:
                t_ev = em.dve(lambda v, m=m, bank=bank: v.tensor_copy(
                    xinb[:, m, 3:3 + tc], PS[bank][:, 0:tc]),
                    waits=last(t_mm, t_hzero, tok_xin.get(("conv", m))))
                tok_ps[bank] = t_ev
                tok_xin[("xin", m)] = t_ev
            else:
                bankg = m % 2
                t_g = em.act(lambda s, bankg=bankg, bank=bank: s.activation(
                    out=gr[:, bankg, :], in_=PS[bank][:, 0:tc], func=AF.Silu),
                    waits=last(t_mm, tok_g[bankg]))
                tok_ps[bank] = t_g
                t_gd = em.dma("sp", gD[(m - 16) * 128:(m - 15) * 128, t0:t0 + tc],
                              gr[:, bankg, :], waits=[t_g])
                tok_g[bankg] = t_gd
        t_ipmm_last = t_mm

        for m in range(16):
            bank = m % 2
            tx = tok_xin[("xin", m)]
            tc1 = em.dve(lambda v, m=m: v.tensor_scalar(
                out=sqA, in0=xinb[:, m, 0:tc], scalar1=convw[:, m, 0:1],
                scalar2=None, op0=ALU.mult),
                waits=last(tx, w_cw, tok_sq_user))
            for k in (1, 2, 3):
                tc1 = em.dve(lambda v, m=m, k=k: v.scalar_tensor_tensor(
                    out=sqA, in0=xinb[:, m, k:k + tc], scalar=convw[:, m, k:k + 1],
                    in1=sqA, op0=ALU.mult, op1=ALU.add),
                    waits=[tc1])
            t_halo = em.dve(lambda v, m=m: v.tensor_copy(
                xinb[:, m, 0:3], xinb[:, m, tc:tc + 3]), waits=[tc1])
            tok_xin[("conv", m)] = t_halo
            t_xcv = em.act(lambda s, m=m, bank=bank: s.activation(
                out=xcr[:, bank, :], in_=sqA, func=AF.Silu,
                bias=convb[:, m:m + 1]),
                waits=last(tc1, w_cb, tok_xc[bank]))
            tok_sq_user = t_xcv
            t_xcd = em.dma("sp", xcD[m * 128:(m + 1) * 128, t0:t0 + tc],
                           xcr[:, bank, :], waits=[t_xcv])
            tok_xc[bank] = t_xcd
        phaseA_stores = [tok_xc[0], tok_xc[1], tok_g[0], tok_g[1]]

    stackA.close()

    stackB = contextlib.ExitStack()
    E = stackB.enter_context
    xc2 = E(nc.sbuf_tensor("xc2", [128, 16, tc], F32)).ap()
    xcb2 = E(nc.sbuf_tensor("xcb2", [128, 16, tc], BF16)).ap()
    dbc_bf = E(nc.sbuf_tensor("dbc_bf", [64, tc], BF16)).ap()
    bc_sb = E(nc.sbuf_tensor("bc_sb", [32, tc], F32)).ap()
    dtch = E(nc.sbuf_tensor("dtch", [128, 16, tc], F32)).ap()
    u_rot = E(nc.sbuf_tensor("u_rot", [128, 2, tc], F32)).ap()
    sA = E(nc.sbuf_tensor("sA", [128, tc], F32)).ap()
    sL = E(nc.sbuf_tensor("sL", [128, tc], F32)).ap()
    sR = E(nc.sbuf_tensor("sR", [128, tc], F32)).ap()
    sq = E(nc.sbuf_tensor("sqB", [128, tc], F32)).ap()
    Brep = E(nc.sbuf_tensor("Brep", [128, 16, tc], F32)).ap()
    Crep = E(nc.sbuf_tensor("Crep", [128, 16, tc], F32)).ap()
    a_rot = E(nc.sbuf_tensor("a_rot", [128, 2, tc], F32)).ap()
    w_all = E(nc.sbuf_tensor("w_all", [128, 16, tc], F32)).ap()
    hbuf = E(nc.sbuf_tensor("hbuf", [128, 16, tc], F32)).ap()
    hc_all = E(nc.sbuf_tensor("hc_all", [128, 16, tc], BF16)).ap()
    carry = E(nc.sbuf_tensor("carry", [128, 16, 16], F32)).ap()
    y3 = E(nc.sbuf_tensor("y3", [128, 16, tc], BF16)).ap()
    outE = E(nc.sbuf_tensor("outE", [128, 8, tc], F32)).ap()
    g_rot2 = E(nc.sbuf_tensor("g_rot2", [128, 2, tc], F32)).ap()
    r0ch = E(nc.sbuf_tensor("r0ch", [128, 8, tc], F32)).ap()
    e_rot = E(nc.sbuf_tensor("e_rot", [128, 2, tc], F32)).ap()
    rows2 = E(nc.sbuf_tensor("s1rows2", [1, 8, tc], F32)).ap()

    prevA = last(*phaseA_stores)
    U = {}

    def tu(*k):
        return last(*[U.get(x) for x in k])

    t_carry_w = []
    t_store_last = []
    for ch in range(nch):
        t0 = ch * tc
        t_xc2 = em.dma("sp", xc2,
                       xcD.rearrange("(mc p) T -> p mc T", p=128)[:, :, t0:t0 + tc],
                       waits=prevA + tu("xc2"))
        t_xcb2 = em.dma("pool", xcb2,
                        xcD.rearrange("(mc p) T -> p mc T", p=128)[:, :, t0:t0 + tc],
                        waits=prevA + tu("xcb2"))
        t_mm = None
        for kc in range(16):
            t_mm = em.pe(lambda t, kc=kc: t.matmul(
                PS[0][0:96, 0:tc], xpw[:, kc, :], xcb2[:, kc, :],
                start=(kc == 0), stop=(kc == 15)),
                waits=last(t_xcb2, w_xp) + (tu("ps0") if kc == 0 else []))
        t_dbc1 = em.dve(lambda v: v.tensor_copy(dbc_bf, PS[0][0:64, 0:tc]),
                        waits=last(t_mm) + tu("dbc_bf"))
        t_dbc2 = em.dve(lambda v: v.tensor_copy(bc_sb, PS[0][64:96, 0:tc]),
                        waits=last(t_mm) + tu("bc_sb"))
        U["ps0"] = t_dbc2
        t_bcd = em.dma("sp", bcD[ch], bc_sb, waits=[t_dbc2])
        U["bc_sb"] = t_bcd
        t_brepL = em.dma("sp", Brep,
                         bcD[ch, 0:16, :].unsqueeze(0).partition_broadcast(128)
                         .rearrange("p a s T -> p (a s) T"),
                         waits=[t_bcd] + tu("Brep"))
        t_crepL = em.dma("sp", Crep,
                         bcD[ch, 16:32, :].unsqueeze(0).partition_broadcast(128)
                         .rearrange("p a s T -> p (a s) T"),
                         waits=[t_bcd] + tu("Crep"))
        t_dts = []
        for m in range(16):
            bank = 1 + m % 2
            t_mmd = em.pe(lambda t, m=m, bank=bank: t.matmul(
                PS[bank][:, 0:tc], dtw[:, m * 128:(m + 1) * 128], dbc_bf,
                start=True, stop=True),
                waits=last(t_dbc1, w_dt) + tu(("psdt", bank)))
            t_ab = em.act(lambda s, m=m, bank=bank: s.activation(
                out=sA, in_=PS[bank][:, 0:tc], func=AF.Abs,
                bias=dtb[:, m:m + 1]), waits=last(t_mmd, w_dtb) + tu("sA"))
            t_ex = em.act(lambda s: s.activation(
                out=sA, in_=sA, func=AF.Exp, scale=-1.0), waits=[t_ab])
            t_ln = em.act(lambda s: s.activation(
                out=sL, in_=sA, func=AF.Ln, bias=1.0), waits=last(t_ex) + tu("sL"))
            U["sA"] = t_ln
            t_rl = em.act(lambda s, m=m, bank=bank: s.activation(
                out=sR, in_=PS[bank][:, 0:tc], func=AF.Relu,
                bias=dtb[:, m:m + 1]), waits=last(t_mmd) + tu("sR"))
            U[("psdt", bank)] = t_rl
            t_dt = em.dve(lambda v, m=m: v.tensor_tensor(
                out=dtch[:, m, :], in0=sL, in1=sR, op=ALU.add),
                waits=last(t_ln, t_rl) + tu(("dt", m)))
            U["sL"] = t_dt
            U["sR"] = t_dt
            t_dts.append(t_dt)
        if ch % cps == 0:
            t_cz = em.pool(lambda g: g.memset(carry, 0.0), waits=t_carry_w)
            t_carry_w = [t_cz]
        t_y3s = []
        t_ylast = None
        for m in range(16):
            ub = m % 2
            t_u = em.dve(lambda v, m=m, ub=ub: v.tensor_tensor(
                out=u_rot[:, ub, :], in0=dtch[:, m, :], in1=xc2[:, m, :],
                op=ALU.mult),
                waits=last(t_dts[m], t_xc2) + tu(("u", ub)))
            t_wall = em.dve(lambda v, ub=ub: v.tensor_tensor(
                out=w_all, in0=u_rot[:, ub, :].unsqueeze(1)
                .broadcast_to([128, 16, tc]), in1=Brep, op=ALU.mult),
                waits=last(t_u, t_brepL) + tu("w_all"))
            t_h = None
            for s in range(16):
                ab = s % 2
                t_aa = em.act(lambda sc_, m=m, s=s, ab=ab: sc_.activation(
                    out=a_rot[:, ab, :], in_=dtch[:, m, :], func=AF.Exp,
                    scale=Asb[:, m, s:s + 1]),
                    waits=last(t_dts[m], t_An) + tu(("a", ab)))
                t_h = em.dve(lambda v, m=m, s=s, ab=ab: v.tensor_tensor_scan(
                    out=hbuf[:, s, :], data0=a_rot[:, ab, :],
                    data1=w_all[:, s, :], initial=carry[:, m, s:s + 1],
                    op0=ALU.mult, op1=ALU.add),
                    waits=last(t_aa, t_wall) + tu(("h", s)) + t_carry_w[-1:])
                U[("a", ab)] = t_h
            U["w_all"] = t_h
            t_hcall = em.dve(lambda v: v.tensor_tensor(
                out=hc_all, in0=hbuf, in1=Crep, op=ALU.mult),
                waits=last(t_h, t_crepL) + tu("hc_all"))
            t_cc = em.dve(lambda v, m=m: v.tensor_copy(
                carry[:, m, :], hbuf[:, :, tc - 1]), waits=[t_h])
            t_carry_w.append(t_cc)
            for ss in range(16):
                U[("h", ss)] = t_cc
            t_ylast = None
            for s in range(16):
                t_ylast = em.pe(lambda t, s=s: t.matmul(
                    PS[3][:, 0:tc], ident_bf, hc_all[:, s, :],
                    start=(s == 0), stop=(s == 15)),
                    waits=last(t_hcall, t_idb) + (tu("psy") if s == 0 else []))
            U["hc_all"] = t_ylast
            U[("u", ub)] = t_ylast
            gb = m % 2
            t_g2 = em.dma("sp", g_rot2[:, gb, :],
                          gD[m * 128:(m + 1) * 128, t0:t0 + tc],
                          waits=(prevA if ch == 0 and m < 2 else []) + tu(("g2", gb)))
            t_e1 = em.dve(lambda v, m=m: v.scalar_tensor_tensor(
                out=sq, in0=xc2[:, m, :], scalar=Dp[:, m:m + 1],
                in1=PS[3][:, 0:tc], op0=ALU.mult, op1=ALU.add),
                waits=last(t_ylast, w_D, t_xc2) + tu("sqB"))
            U["psy"] = t_e1
            t_e2 = em.dve(lambda v, m=m, gb=gb: v.tensor_tensor(
                out=y3[:, m, :], in0=sq, in1=g_rot2[:, gb, :], op=ALU.mult),
                waits=last(t_e1, t_g2) + tu(("y3", m)))
            U["sqB"] = t_e2
            U[("g2", gb)] = t_e2
            t_y3s.append(t_e2)
        U["xc2"] = t_y3s[-1]
        U["xcb2"] = t_mm
        U["Brep"] = t_y3s[-1]
        U["Crep"] = t_y3s[-1]
        U["dbc_bf"] = t_dts[-1]
        for m in range(16):
            U[("dt", m)] = t_y3s[-1]

        t_r0 = em.dma("sp", r0ch,
                      R0.rearrange("(dc p) T -> p dc T", p=128)[:, :, t0:t0 + tc],
                      waits=tu("r0ch"))
        t_ms1 = None
        t_ms2 = None
        for dc in range(8):
            t_ms1 = em.pe(lambda t, dc=dc: t.matmul(
                PS[6][0:1, 0:tc], ones_col, r0ch[:, dc, :],
                start=(dc == 0), stop=(dc == 7)),
                waits=last(t_r0, t_oc) + (tu("ps67") if dc == 0 else []))
            t_sq1 = em.act(lambda s, dc=dc: s.activation(
                out=sq, in_=r0ch[:, dc, :], func=AF.Square),
                waits=[t_r0] + tu("sqB"))
            t_ms2 = em.pe(lambda t, dc=dc: t.matmul(
                PS[7][0:1, 0:tc], ones_col, sq,
                start=(dc == 0), stop=(dc == 7)),
                waits=[t_sq1] + (tu("ps67") if dc == 0 else []))
            U["sqB"] = t_ms2
        mu1 = rows2[:, 4, :]
        rs1 = rows2[:, 5, :]
        bb1 = rows2[:, 6, :]
        tq2 = rows2[:, 7, :]
        t_m1 = em.dve(lambda v: v.tensor_scalar_mul(mu1, PS[6][0:1, 0:tc], 1.0 / DM),
                      waits=[t_ms1] + tu("rows2b"))
        t_q1 = em.act(lambda s: s.activation(out=tq2, in_=mu1, func=AF.Square),
                      waits=[t_m1])
        t_v1b = em.dve(lambda v: v.scalar_tensor_tensor(
            out=rs1, in0=PS[7][0:1, 0:tc], scalar=1.0 / DM, in1=tq2,
            op0=ALU.mult, op1=ALU.subtract), waits=[t_ms2, t_q1])
        t_v1c = em.dve(lambda v: v.tensor_scalar_add(rs1, rs1, 1e-5), waits=[t_v1b])
        t_s1b = em.act(lambda s: s.activation(out=rs1, in_=rs1, func=AF.Sqrt),
                       waits=[t_v1c])
        t_rs1 = em.dve(lambda v: v.reciprocal(rs1, rs1), waits=[t_s1b])
        t_bb1 = em.dve(lambda v: v.tensor_tensor(out=bb1, in0=mu1, in1=rs1,
                                                 op=ALU.mult), waits=[t_rs1])
        t_bb1b = em.dve(lambda v: v.tensor_scalar_mul(bb1, bb1, -1.0), waits=[t_bb1])
        U["ps67a"] = [t_v1b]
        t_rs1rep = em.pe(lambda t: t.matmul(PS[2][:, 0:tc], ones_row, rs1,
                                            start=True, stop=True),
                         waits=last(t_rs1, t_or) + tu(("psdt", 2)))
        t_bb1rep = em.pe(lambda t: t.matmul(PS[1][:, 0:tc], ones_row, bb1,
                                            start=True, stop=True),
                         waits=last(t_bb1b, t_or) + tu(("psdt", 1)))
        U["rows2b"] = t_bb1b

        t_mo1 = None
        t_mo2 = None
        t_mmo = None
        t_adds = []
        for mo in range(8):
            bank = 4 + mo % 2
            for kc in range(16):
                t_mmo = em.pe(lambda t, mo=mo, kc=kc, bank=bank: t.matmul(
                    PS[bank][:, 0:tc], opw[:, kc, mo * 128:(mo + 1) * 128],
                    y3[:, kc, :], start=(kc == 0), stop=(kc == 15)),
                    waits=last(t_y3s[kc], w_op) + (tu(("pso", bank)) if kc == 0 else []))
            t_xa = em.dve(lambda v, mo=mo: v.tensor_tensor(
                out=sq, in0=r0ch[:, mo, :], in1=PS[2][:, 0:tc], op=ALU.mult),
                waits=[t_rs1rep] + tu("sqB"))
            t_xb = em.dve(lambda v: v.tensor_tensor(
                out=sq, in0=sq, in1=PS[1][:, 0:tc], op=ALU.add),
                waits=[t_bb1rep, t_xa])
            t_evo = em.dve(lambda v, mo=mo, bank=bank: v.tensor_tensor(
                out=outE[:, mo, :], in0=sq, in1=PS[bank][:, 0:tc], op=ALU.add),
                waits=last(t_mmo, t_xb) + tu(("outE", mo)))
            t_adds.append(t_evo)
            U[("pso", bank)] = t_evo
            t_sqo = em.act(lambda s, mo=mo: s.activation(
                out=sq, in_=outE[:, mo, :], func=AF.Square),
                waits=[t_evo])
            U["sqB"] = t_sqo
            t_mo1 = em.pe(lambda t, mo=mo: t.matmul(
                PS[6][0:1, 0:tc], ones_col, outE[:, mo, :],
                start=(mo == 0), stop=(mo == 7)),
                waits=[t_evo] + (tu("ps67a") if mo == 0 else []))
            t_mo2 = em.pe(lambda t, mo=mo: t.matmul(
                PS[7][0:1, 0:tc], ones_col, sq,
                start=(mo == 0), stop=(mo == 7)),
                waits=[t_sqo] + (tu("ps67a") if mo == 0 else []))
            U["sqB"] = t_mo2
        mu2 = rows2[:, 0, :]
        vv = rows2[:, 1, :]
        rr = rows2[:, 2, :]
        bb = rows2[:, 3, :]
        t_mu = em.dve(lambda v: v.tensor_scalar_mul(mu2, PS[6][0:1, 0:tc], 1.0 / DM),
                      waits=[t_mo1] + tu("rows2"))
        t_ms = em.act(lambda s: s.activation(out=vv, in_=mu2, func=AF.Square),
                      waits=[t_mu])
        t_vv = em.dve(lambda v: v.scalar_tensor_tensor(
            out=vv, in0=PS[7][0:1, 0:tc], scalar=1.0 / DM, in1=vv,
            op0=ALU.mult, op1=ALU.subtract), waits=[t_mo2, t_ms])
        t_ve = em.dve(lambda v: v.tensor_scalar_add(rr, vv, 1e-5), waits=[t_vv])
        t_sd = em.act(lambda s: s.activation(out=rr, in_=rr, func=AF.Sqrt),
                      waits=[t_ve])
        t_rr = em.dve(lambda v: v.reciprocal(rr, rr), waits=[t_sd])
        t_nb = em.dve(lambda v: v.tensor_tensor(out=bb, in0=mu2, in1=rr,
                                                op=ALU.mult), waits=[t_rr])
        t_nb2 = em.dve(lambda v: v.tensor_scalar_mul(bb, bb, -1.0), waits=[t_nb])
        U["ps67"] = [t_vv, t_nb2]
        U["rows2"] = t_nb2
        t_rrep = em.pe(lambda t: t.matmul(PS[0][:, 0:tc], ones_row, rr,
                                          start=True, stop=True),
                       waits=last(t_rr, t_or) + tu("ps0"))
        t_brep2 = em.pe(lambda t: t.matmul(PS[3][:, 0:tc], ones_row, bb,
                                           start=True, stop=True),
                        waits=last(t_nb2, t_or) + tu("psy"))
        t_a1 = None
        t_a2 = None
        for mo in range(8):
            rb = mo % 2
            t_a1 = em.dve(lambda v, mo=mo, rb=rb: v.tensor_tensor(
                out=e_rot[:, rb, :], in0=outE[:, mo, :], in1=PS[0][:, 0:tc],
                op=ALU.mult), waits=[t_rrep, t_adds[mo]] + tu(("er", rb)))
            t_a2 = em.dve(lambda v, rb=rb: v.tensor_tensor(
                out=e_rot[:, rb, :], in0=e_rot[:, rb, :], in1=PS[3][:, 0:tc],
                op=ALU.add), waits=[t_brep2, t_a1])
            t_a3 = em.dve(lambda v, mo=mo, rb=rb: v.tensor_tensor(
                out=e_rot[:, rb, :], in0=e_rot[:, rb, :], in1=r0ch[:, mo, :],
                op=ALU.add), waits=[t_a2])
            U[("outE", mo)] = [t_a1, t_mo1, t_mo2]
            t_st = em.dma("sp", R1[mo * 128:(mo + 1) * 128, t0:t0 + tc],
                          e_rot[:, rb, :], waits=[t_a3])
            U[("er", rb)] = t_st
            t_store_last.append(t_st)
        U["r0ch"] = t_a3
        U["ps0"] = t_a1
        U["psy"] = t_a2
        U[("psdt", 1)] = t_xb
        U[("psdt", 2)] = t_xa
        prevA = []
        t_store_last = t_store_last[-2:]
    stackB.close()
    stS.close()
    return t_store_last




def build_stage2(nc, em, io, seq=2048, tc=512):
    stS = contextlib.ExitStack()

    def A(name, shape, dt):
        return stS.enter_context(nc.sbuf_tensor(name, shape, dt))
    ntok = 2 * seq
    nch = ntok // tc
    nst = seq // 128
    cw = min(512, seq)
    ncw = seq // cw

    R1 = io["R1"]
    R2 = io["R2"]

    xnB = nc.dram_tensor("s2_xnB", [DM, ntok], BF16, kind="Internal").ap()
    xnF = nc.dram_tensor("s2_xnF", [DM, ntok], F32, kind="Internal").ap()
    qD = nc.dram_tensor("s2_q", [DM, ntok], BF16, kind="Internal").ap()
    kD = nc.dram_tensor("s2_k", [DM, ntok], BF16, kind="Internal").ap()
    vD = nc.dram_tensor("s2_v", [DM, ntok], BF16, kind="Internal").ap()
    yB = nc.dram_tensor("s2_yB", [DM, ntok], BF16, kind="Internal").ap()
    yF = nc.dram_tensor("s2_yF", [DM, ntok], F32, kind="Internal").ap()
    p8D = nc.dram_tensor("s2_p8", [nch, 256, tc], BF16, kind="Internal").ap()
    hD = nc.dram_tensor("s2_h", [nch, 16, tc], BF16, kind="Internal").ap()

    PSALL = stS.enter_context(nc.psum_tensor("s2_psall", [128, 4, 512], F32)).ap()
    P0 = PSALL[:, 0, :]
    P1 = PSALL[:, 1, :]
    P2 = PSALL[:, 2, :]
    P3 = PSALL[:, 3, :]
    pT0 = stS.enter_context(nc.psum_tensor("s2_pT0", [128, 1024], BF16)).ap()
    pT1 = stS.enter_context(nc.psum_tensor("s2_pT1", [128, 1024], BF16)).ap()
    P6 = stS.enter_context(nc.psum_tensor("s2_p6", [128, 512], F32)).ap()
    P7 = stS.enter_context(nc.psum_tensor("s2_p7", [128, 512], F32)).ap()
    SF = PSALL.rearrange("p a b -> p (a b)")

    ones_col = A("s2_onesc", [128, 1], F32).ap()
    ones_row = A("s2_onesr", [1, 128], F32).ap()
    ones_bf = A("s2_onesb", [1, 512], BF16).ap()
    ident_bf = A("s2_identb", [128, 128], BF16).ap()
    identf = A("s2_identf", [128, 128], F32).ap()
    Mg = A("s2_Mg", [128, 16, 128], BF16).ap()
    sel16T = A("s2_sel16T", [16, 128], BF16).ap()
    sel128 = A("s2_sel128", [128, 16], BF16).ap()
    t_oc = em.pool(lambda g: g.memset(ones_col, 1.0))
    t_or = em.pool(lambda g: g.memset(ones_row, 1.0))
    t_ob = em.pool(lambda g: g.memset(ones_bf, 1.0))
    t_i1 = em.dma("sp", identf, io["ident"])
    t_ib = em.dve(lambda v: v.tensor_copy(ident_bf, identf), waits=[t_i1])
    t_cs = [t_ib, t_i1, t_oc, t_or, t_ob]
    t_cs.append(em.dma("pool", Mg, io["Mg"]))
    t_cs.append(em.dma("pool", sel16T, io["sel16T"]))
    t_cs.append(em.dma("pool", sel128, io["sel128"]))

    ones_f = A("s2_onesf", [1, 512], F32).ap()
    b1T = A("s2_b1T", [128, 4, 2, 16], BF16).ap()
    b2T = A("s2_b2T", [128, 4, 2, 1024], BF16).ap()
    evq = A("s2_evq", [128, 2, tc], BF16).ap()
    stW = contextlib.ExitStack()
    scm = stW.enter_context(nc.sbuf_tensor("s2_scm", [128, 8, 2, 256], BF16)).ap()
    nwT = stW.enter_context(nc.sbuf_tensor("s2_nwT", [128, 4, 8, 255], F32)).ap()
    nbT = stW.enter_context(nc.sbuf_tensor("s2_nbT", [1, 4, 255], F32)).ap()
    t_cs.append(em.dma("pool", scm, io["scm"]))
    w_nw = em.dma("sp", nwT, io["nwT4"].rearrange("(kc p) f n -> p f kc n", p=128))
    w_nb = em.dma("sp", nbT, io["nb4"])
    t_of = em.pool(lambda g: g.memset(ones_f, 1.0))
    w_b1 = em.dma("pool", b1T, io["b14"].rearrange("f (c p) l -> p f c l", p=128))
    w_b2 = em.dma("pool", b2T, io["b24"].rearrange("f (c p) m -> p f c m", p=128))

    U = {}

    def tu(*k):
        return last(*[U.get(x) for x in k])

    st0 = contextlib.ExitStack()
    E0 = st0.enter_context
    rch = E0(nc.sbuf_tensor("s2_rch", [128, 8, tc], F32)).ap()
    sq = E0(nc.sbuf_tensor("s2_sq", [128, tc], F32)).ap()
    xnc = E0(nc.sbuf_tensor("s2_xnc", [128, 8, tc], BF16)).ap()
    xfc = E0(nc.sbuf_tensor("s2_xfc", [128, 2, tc], F32)).ap()
    rows = E0(nc.sbuf_tensor("s2_rows", [1, 4, tc], F32)).ap()
    t_done_xn = []
    for ch in range(nch):
        t0 = ch * tc
        t_r = em.dma("sp", rch,
                     R1.rearrange("(dc p) T -> p dc T", p=128)[:, :, t0:t0 + tc],
                     waits=tu("rch"))
        t_m1 = None
        t_m2 = None
        for dc in range(8):
            t_m1 = em.pe(lambda t, dc=dc: t.matmul(
                P6[0:1, 0:tc], ones_col, rch[:, dc, :],
                start=(dc == 0), stop=(dc == 7)),
                waits=last(t_r, t_oc) + (tu("ps67") if dc == 0 else []))
            t_sq = em.act(lambda s, dc=dc: s.activation(
                out=sq, in_=rch[:, dc, :], func=AF.Square),
                waits=[t_r] + tu("sq"))
            t_m2 = em.pe(lambda t, dc=dc: t.matmul(
                P7[0:1, 0:tc], ones_col, sq,
                start=(dc == 0), stop=(dc == 7)),
                waits=[t_sq] + (tu("ps67") if dc == 0 else []))
            U["sq"] = t_m2
        mu = rows[:, 0, :]
        v1 = rows[:, 1, :]
        sc = rows[:, 2, :]
        tq = rows[:, 3, :]
        t_mu = em.dve(lambda v: v.tensor_scalar_mul(mu, P6[0:1, 0:tc], 1.0 / DM),
                      waits=[t_m1] + tu("rows"))
        t_q = em.act(lambda s: s.activation(out=tq, in_=mu, func=AF.Square),
                     waits=[t_mu])
        t_v = em.dve(lambda v: v.scalar_tensor_tensor(
            out=v1, in0=P7[0:1, 0:tc], scalar=1.0 / DM, in1=tq,
            op0=ALU.mult, op1=ALU.subtract), waits=[t_m2, t_q])
        U["ps67"] = [t_v, t_mu]
        t_1 = em.dve(lambda v: v.tensor_scalar_add(tq, v1, 1e-5), waits=[t_v, t_q])
        t_2 = em.dve(lambda v: v.reciprocal(sc, tq), waits=[t_1])
        t_3 = em.dve(lambda v: v.scalar_tensor_tensor(
            out=tq, in0=v1, scalar=float(DM) / (DM - 1), in1=sc,
            op0=ALU.mult, op1=ALU.mult), waits=[t_2])
        t_4 = em.act(lambda s: s.activation(out=tq, in_=tq, func=AF.Sqrt),
                     waits=[t_3])
        t_5 = em.dve(lambda v: v.tensor_scalar_add(tq, tq, 1e-5), waits=[t_4])
        t_6 = em.act(lambda s: s.activation(out=tq, in_=tq, func=AF.Sqrt),
                     waits=[t_5])
        t_7 = em.act(lambda s: s.activation(out=sc, in_=sc, func=AF.Sqrt),
                     waits=[t_3])
        t_8 = em.dve(lambda v: v.reciprocal(tq, tq), waits=[t_6])
        t_9 = em.dve(lambda v: v.tensor_tensor(out=sc, in0=sc, in1=tq, op=ALU.mult),
                     waits=[t_7, t_8])
        t_10 = em.dve(lambda v: v.tensor_tensor(out=tq, in0=mu, in1=sc, op=ALU.mult),
                      waits=[t_9])
        t_11 = em.dve(lambda v: v.tensor_scalar_mul(tq, tq, -1.0), waits=[t_10])
        U["rows"] = t_11
        t_sr = em.pe(lambda t: t.matmul(P2[:, 0:tc], ones_row, sc,
                                        start=True, stop=True),
                     waits=last(t_9, t_or) + tu("ps2"))
        t_br = em.pe(lambda t: t.matmul(P3[:, 0:tc], ones_row, tq,
                                        start=True, stop=True),
                     waits=last(t_11, t_or) + tu("ps3"))
        t_l = []
        ta = None
        tb = None
        for dc in range(8):
            fb = dc % 2
            ta = em.dve(lambda v, dc=dc, fb=fb: v.tensor_tensor(
                out=xfc[:, fb, :], in0=rch[:, dc, :], in1=P2[:, 0:tc],
                op=ALU.mult),
                waits=[t_sr] + tu(("xf", fb)))
            tb = em.dve(lambda v, fb=fb: v.tensor_tensor(
                out=xfc[:, fb, :], in0=xfc[:, fb, :], in1=P3[:, 0:tc],
                op=ALU.add), waits=[t_br, ta])
            tcv = em.dve(lambda v, dc=dc, fb=fb: v.tensor_copy(
                xnc[:, dc, :], xfc[:, fb, :]), waits=[tb] + tu("xnc"))
            t_d1 = em.dma("sp", xnF[dc * 128:(dc + 1) * 128, t0:t0 + tc],
                          xfc[:, fb, :], waits=[tcv])
            U[("xf", fb)] = t_d1
            t_l.append(tcv)
        t_d2 = em.dma("sp", xnB.rearrange("(dc p) T -> p dc T", p=128)[:, :, t0:t0 + tc],
                      xnc, waits=t_l)
        U["xnc"] = t_d2
        U["rch"] = t_l + [t_m1]
        U["ps2"] = [ta]
        U["ps3"] = [tb]
        t_done_xn = [t_d2] + tu(("xf", 0)) + tu(("xf", 1))
    st0.close()

    def tree_L1(tree_idx, src_bf, src_f32, act_fn, wname, t_pre):
        st = contextlib.ExitStack()
        E = st.enter_context
        sfx = f"t{tree_idx}a"
        w1 = E(nc.sbuf_tensor("s2_w1_"+sfx, [128, 8, 4096], BF16)).ap()
        xb = E(nc.sbuf_tensor("s2_xb_"+sfx, [128, 8, tc], BF16)).ap()
        xf = E(nc.sbuf_tensor("s2_xf_"+sfx, [128, 8, tc], F32)).ap()
        lgk = E(nc.sbuf_tensor("s2_lgk_"+sfx, [128, 2, tc], F32)).ap()
        p_cur = E(nc.sbuf_tensor("s2_pcur_"+sfx, [128, tc], BF16)).ap()
        g0 = E(nc.sbuf_tensor("s2_g0_"+sfx, [128, tc], BF16)).ap()
        g1 = E(nc.sbuf_tensor("s2_g1_"+sfx, [128, tc], BF16)).ap()
        p8 = E(nc.sbuf_tensor("s2_p8sb_"+sfx, [128, 2, tc], BF16)).ap()
        prep = E(nc.sbuf_tensor("s2_prep_"+sfx, [128, 2, tc], BF16)).ap()
        pF = E(nc.sbuf_tensor("s2_pF_"+sfx, [128, 2, tc], BF16)).ap()
        hsb = E(nc.sbuf_tensor("s2_hsb_"+sfx, [16, tc], BF16)).ap()
        w_w1 = em.dma("pool", w1, io[wname].rearrange("(kc p) m -> p kc m", p=128),
                      waits=t_pre)
        V = {}

        def tv(*k):
            return last(*[V.get(x) for x in k])

        t_ret = []
        for ch in range(nch):
            t0 = ch * tc
            t_x = em.dma("sp", xb,
                         src_bf.rearrange("(dc p) T -> p dc T", p=128)[:, :, t0:t0 + tc],
                         waits=(t_pre if ch == 0 else []) + tv("xb"))
            t_x2 = em.dma("sp", xf,
                          src_f32.rearrange("(dc p) T -> p dc T", p=128)[:, :, t0:t0 + tc],
                          waits=(t_pre if ch == 0 else []) + tv("xf"))
            t_pc = em.dve(lambda v: v.memset(p_cur[0:1, :], 1.0),
                          waits=tv("p_cur") + t_cs + (t_pre if ch == 0 else []))
            t_p8 = []
            for k in range(8):
                wk = 2 ** k
                off = wk - 1
                lb = k % 2
                t_lg = None
                for kc in range(8):
                    t_lg = em.pe(lambda t, kc=kc, wk=wk, off=off: t.matmul(
                        P0[0:wk, 0:tc], nwT[:, tree_idx, kc, off:off + wk],
                        xf[:, kc, :], start=(kc == 0), stop=False),
                        waits=last(t_x2, w_nw) + (tv("ps0") if kc == 0 else []))
                t_lg = em.pe(lambda t, wk=wk, off=off: t.matmul(
                    P0[0:wk, 0:tc], nbT[0:1, tree_idx, off:off + wk],
                    ones_f[0:1, 0:tc], start=False, stop=True),
                    waits=last(w_nb, t_of))
                t_le = em.dve(lambda v, wk=wk, lb=lb: v.tensor_copy(
                    lgk[0:wk, lb, :], P0[0:wk, 0:tc]),
                    waits=[t_lg] + tv(("lgk", lb)))
                V["ps0"] = t_le
                t_g1 = em.dve(lambda v, wk=wk, lb=lb: v.scalar_tensor_tensor(
                    out=g1[0:wk, :], in0=lgk[0:wk, lb, :], scalar=0.0,
                    in1=p_cur[0:wk, :], op0=ALU.is_gt, op1=ALU.mult),
                    waits=[t_le, t_pc] + tv("g1"))
                V[("lgk", lb)] = t_g1
                t_g0 = em.dve(lambda v, wk=wk: v.tensor_tensor(
                    out=g0[0:wk, :], in0=p_cur[0:wk, :], in1=g1[0:wk, :],
                    op=ALU.subtract), waits=[t_g1] + tv("g0"))
                if k < 7:
                    w2 = 2 ** (k + 1)
                    em.pe(lambda t, k=k, wk=wk, w2=w2: t.matmul(
                        P1[0:w2, 0:tc], scm[0:wk, k, 0, 0:w2], g0[0:wk, :],
                        start=True, stop=False),
                        waits=[t_g0] + tv("ps1"))
                    t_c1 = em.pe(lambda t, k=k, wk=wk, w2=w2: t.matmul(
                        P1[0:w2, 0:tc], scm[0:wk, k, 1, 0:w2], g1[0:wk, :],
                        start=False, stop=True), waits=[t_g1])
                    t_pc = em.dve(lambda v, w2=w2: v.tensor_copy(
                        p_cur[0:w2, :], P1[0:w2, 0:tc]), waits=[t_c1, t_g0])
                    V["ps1"] = t_pc
                    V["g0"] = t_pc
                    V["g1"] = t_pc
                else:
                    for half in range(2):
                        em.pe(lambda t, half=half: t.matmul(
                            P1[:, 0:tc], scm[:, 7, 0, half * 128:(half + 1) * 128],
                            g0, start=True, stop=False),
                            waits=[t_g0] + tv("ps1"))
                        t_c1 = em.pe(lambda t, half=half: t.matmul(
                            P1[:, 0:tc], scm[:, 7, 1, half * 128:(half + 1) * 128],
                            g1, start=False, stop=True), waits=[t_g1])
                        t_pe = em.dve(lambda v, half=half: v.tensor_copy(
                            p8[:, half, :], P1[:, 0:tc]),
                            waits=[t_c1] + tv(("p8", half)))
                        V["ps1"] = t_pe
                        t_p8.append(t_pe)
                    V["g0"] = t_p8[-1]
                    V["g1"] = t_p8[-1]
            t_p8d = em.dma("sp", p8D[ch].rearrange("(h p) T -> p h T", p=128),
                           p8, waits=t_p8)
            t_hmm = None
            t_pf_prev = None
            fb_prev = 0
            t_pf_last_odd = None
            for gg in range(32):
                half = gg // 16
                gi = gg % 16
                fb = gg % 2
                bank3 = P3 if fb == 0 else P1
                bkey = "ps3" if fb == 0 else "ps1"
                t_pr = em.pe(lambda t, gi=gi, half=half: t.matmul(
                    P2[:, 0:tc], Mg[:, gi, :], p8[:, half, :],
                    start=True, stop=True),
                    waits=[t_p8[half]] + tv("ps2"))
                t_pv = em.act(lambda s, fb=fb: s.activation(
                    out=prep[:, fb, :], in_=P2[:, 0:tc], func=AF.Copy),
                    waits=[t_pr] + tv(("prep", fb)))
                V["ps2"] = t_pv
                t_F = None
                for kc in range(8):
                    t_F = em.pe(lambda t, gg=gg, kc=kc, bank3=bank3: t.matmul(
                        bank3[:, 0:tc], w1[:, kc, gg * 128:(gg + 1) * 128],
                        xb[:, kc, :], start=(kc == 0), stop=(kc == 7)),
                        waits=last(t_x, w_w1) + (tv(bkey) if kc == 0 else []))
                if t_pf_prev is not None:
                    t_hmm = em.pe(lambda t, fb_prev=fb_prev, gg=gg: t.matmul(
                        P6[0:16, 0:tc], sel128, pF[:, fb_prev, :],
                        start=(gg == 1), stop=False),
                        waits=[t_pf_prev] + (tv("ps6") if gg == 1 else []))
                    V[("pF", fb_prev)] = t_hmm
                t_pf = em.dve(lambda v, fb=fb, bank3=bank3: v.tensor_tensor(
                    out=pF[:, fb, :], in0=bank3[:, 0:tc], in1=prep[:, fb, :],
                    op=ALU.mult), waits=[t_F, t_pv] + tv(("pF", fb)))
                V[bkey] = t_pf
                if fb == 1:
                    t_pf_last_odd = t_pf
                V[("prep", fb)] = t_pf
                t_pf_prev = t_pf
                fb_prev = fb
            t_hmm = em.pe(lambda t: t.matmul(
                P6[0:16, 0:tc], sel128, pF[:, fb_prev, :],
                start=False, stop=False), waits=[t_pf_prev])
            V[("pF", fb_prev)] = t_hmm
            V["ps1"] = t_pf_last_odd
            for half in range(2):
                t_hmm = em.pe(lambda t, half=half: t.matmul(
                    P6[0:16, 0:tc], b1T[:, tree_idx, half, :], p8[:, half, :],
                    start=False, stop=(half == 1)), waits=last(w_b1))
            t_h = em.act(lambda s: s.activation(
                out=hsb, in_=P6[0:16, 0:tc], func=act_fn),
                waits=[t_hmm] + tv("hsb"))
            V["ps6"] = t_h
            t_hd = em.dma("sp", hD[ch], hsb, waits=[t_h])
            V["hsb"] = t_hd
            V["xb"] = [t_hmm]
            V["xf"] = [t_lg]
            V["p_cur"] = t_hmm
            V[("p8", 0)] = [t_p8d, t_hmm]
            V[("p8", 1)] = [t_p8d, t_hmm]
            t_ret = [t_hd, t_p8d]
        st.close()
        return t_ret

    def tree_L2(tree_idx, wname, t_pre, out_cb):
        st = contextlib.ExitStack()
        E = st.enter_context
        sfx = f"t{tree_idx}b"
        w2t = E(nc.sbuf_tensor("s2_w2_"+sfx, [128, 32, 1024], BF16)).ap()
        p8 = E(nc.sbuf_tensor("s2_p8b_"+sfx, [128, 2, tc], BF16)).ap()
        hsb = E(nc.sbuf_tensor("s2_hsb2_"+sfx, [16, tc], BF16)).ap()
        hrep = E(nc.sbuf_tensor("s2_hrep_"+sfx, [128, tc], BF16)).ap()
        Hbig = E(nc.sbuf_tensor("s2_Hbig_"+sfx, [128, 32, tc], BF16)).ap()
        w_w2 = em.dma("pool", w2t, io[wname].rearrange("(kc p) m -> p kc m", p=128),
                      waits=t_pre)
        V = {}

        def tv(*k):
            return last(*[V.get(x) for x in k])

        t_out = []
        for ch in range(nch):
            t_p8 = em.dma("sp", p8, p8D[ch].rearrange("(h p) T -> p h T", p=128),
                          waits=(t_pre if ch == 0 else []) + tv("p8"))
            t_hl = em.dma("sp", hsb, hD[ch],
                          waits=(t_pre if ch == 0 else []) + tv("hsb"))
            t_hr = em.pe(lambda t: t.matmul(P0[:, 0:tc], sel16T, hsb,
                                            start=True, stop=True),
                         waits=last(t_hl, *t_cs) + tv("ps0b"))
            t_hv = em.act(lambda s: s.activation(
                out=hrep, in_=P0[:, 0:tc], func=AF.Copy),
                waits=[t_hr] + tv("hrep"))
            V["ps0b"] = t_hv
            t_hbs = []
            t_hb = None
            for gg in range(32):
                half = gg // 16
                gi = gg % 16
                t_pr = em.pe(lambda t, gi=gi, half=half: t.matmul(
                    P1[:, 0:tc], Mg[:, gi, :], p8[:, half, :],
                    start=True, stop=True),
                    waits=[t_p8] + tv("ps1b") + t_cs)
                t_hb = em.dve(lambda v, gg=gg: v.tensor_tensor(
                    out=Hbig[:, gg, :], in0=P1[:, 0:tc], in1=hrep,
                    op=ALU.mult), waits=[t_pr, t_hv] + tv(("Hb", gg)))
                V["ps1b"] = t_hb
                t_hbs.append(t_hb)
            V["hrep"] = t_hb
            t_fin = None
            t_mm = None
            for mo in range(8):
                bank = P6 if mo % 2 == 0 else P7
                bkey = ("psy", mo % 2)
                for gg in range(32):
                    t_mm = em.pe(lambda t, mo=mo, gg=gg, bank=bank: t.matmul(
                        bank[:, 0:tc], w2t[:, gg, mo * 128:(mo + 1) * 128],
                        Hbig[:, gg, :], start=(gg == 0), stop=False),
                        waits=last(t_hbs[gg], w_w2) + (tv(bkey) if gg == 0 else []))
                for half in range(2):
                    t_mm = em.pe(lambda t, mo=mo, half=half, bank=bank: t.matmul(
                        bank[:, 0:tc],
                        b2T[:, tree_idx, half, mo * 128:(mo + 1) * 128],
                        p8[:, half, :], start=False, stop=(half == 1)),
                        waits=last(w_b2))
                t_fin = out_cb(ch, mo, bank[:, 0:tc], t_mm)
                V[bkey] = t_fin
                t_out.append(t_fin)
            for gg in range(32):
                V[("Hb", gg)] = t_mm
            V["p8"] = t_mm
            V["hsb"] = t_hr
        st.close()
        return t_out[-2:]


    qkv_st = []

    def mk_cb(dst, scale=None):
        def cb(ch, mo, ps, t_mm, dst=dst, scale=scale):
            fb = mo % 2
            if scale is None:
                t_e = em.act(lambda s, ch=ch, mo=mo, fb=fb: s.activation(
                    out=evq[:, fb, :], in_=ps, func=AF.Copy),
                    waits=[t_mm] + tu(("evq", fb)))
            else:
                t_e = em.dve(lambda v, fb=fb, ps=ps: v.tensor_scalar_mul(
                    evq[:, fb, :], ps, scale),
                    waits=[t_mm] + tu(("evq", fb)))
            t_d = em.dma("sp", dst[mo * 128:(mo + 1) * 128, ch * tc:(ch + 1) * tc],
                         evq[:, fb, :], waits=[t_e])
            U[("evq", fb)] = t_d
            qkv_st.append(t_d)
            return t_e
        return cb

    t_qh = tree_L1(0, xnB, xnF, AF.Relu, "q_w1T", t_done_xn + t_cs)
    t_q2 = tree_L2(0, "q_w2T", t_qh, mk_cb(qD, scale=1.0 / 32.0))
    t_kh = tree_L1(1, xnB, xnF, AF.Relu, "k_w1T", t_q2)
    t_k2 = tree_L2(1, "k_w2T", t_kh, mk_cb(kD))
    t_vh = tree_L1(2, xnB, xnF, AF.Relu, "v_w1T", t_k2)
    t_v2 = tree_L2(2, "v_w2T", t_vh, mk_cb(vD))

    stB = contextlib.ExitStack()
    E = stB.enter_context
    qsb = E(nc.sbuf_tensor("s2_qsb", [128, 8, cw], BF16)).ap()
    ksb = E(nc.sbuf_tensor("s2_ksb", [128, 8, seq], BF16)).ap()
    Pq = E(nc.sbuf_tensor("s2_Pq", [128, seq], BF16)).ap()
    PT = E(nc.sbuf_tensor("s2_PT", [128, nst, cw], BF16)).ap()
    vtm = E(nc.sbuf_tensor("s2_vtm", [128, nst, 1024], BF16)).ap()
    ncol = E(nc.sbuf_tensor("s2_ncol", [128, 4], F32)).ap()
    lcol = E(nc.sbuf_tensor("s2_lcol", [128, 4], F32)).ap()
    nlT = E(nc.sbuf_tensor("s2_nlT", [64, 128], F32)).ap()
    lrow = E(nc.sbuf_tensor("s2_lrow", [1, cw], F32)).ap()
    rl = E(nc.sbuf_tensor("s2_rl", [128, 2, cw], F32)).ap()
    xfl = E(nc.sbuf_tensor("s2_xfl", [128, 2, cw], F32)).ap()
    yfb = E(nc.sbuf_tensor("s2_yfb", [128, 2, cw], F32)).ap()
    ybb = E(nc.sbuf_tensor("s2_ybb", [128, 2, cw], BF16)).ap()
    t_att_done = []
    prevB = last(t_v2) + qkv_st
    nqt = cw // 128
    for b in range(2):
        tb0 = b * seq
        t_kl = em.dma("sp", ksb,
                      kD.rearrange("(dc p) T -> p dc T", p=128)[:, :, tb0:tb0 + seq],
                      waits=prevB + tu("ksb"))
        t_vts = []
        for dc in range(8):
            t_vl = em.dma("sp", Pq[:, 0:seq],
                          vD[dc * 128:(dc + 1) * 128, tb0:tb0 + seq],
                          waits=prevB + tu("Pq") + tu("vload"))
            for stt2 in range(nst):
                pT = pT0 if stt2 % 2 == 0 else pT1
                t_tr = em.pe(lambda t, stt2=stt2, pT=pT: t.transpose(
                    pT[:, 0:128], Pq[:, stt2 * 128:(stt2 + 1) * 128], ident_bf),
                    waits=[t_vl, t_ib] + tu(("psT", stt2 % 2)))
                t_ev = em.dve(lambda v, stt2=stt2, dc=dc, pT=pT: v.tensor_copy(
                    vtm[:, stt2, dc * 128:(dc + 1) * 128], pT[:, 0:128]),
                    waits=[t_tr] + tu("vtmw"))
                U[("psT", stt2 % 2)] = t_ev
                t_vts.append(t_ev)
            U["vload"] = t_vts[-nst:]
            U["Pq"] = t_vts[-nst:]
        prevB = []
        for qc in range(ncw):
            t_ql = em.dma("sp", qsb,
                          qD.rearrange("(dc p) T -> p dc T", p=128)[:, :, tb0 + qc * cw:tb0 + (qc + 1) * cw],
                          waits=tu("qsb"))
            t_trs = []
            t_ex = None
            for qt in range(nqt):
                t_s = None
                for stt in range(ncw):
                    for kc in range(8):
                        t_s = em.pe(lambda t, qt=qt, stt=stt, kc=kc: t.matmul(
                            SF[:, stt * cw:(stt + 1) * cw],
                            qsb[:, kc, qt * 128:(qt + 1) * 128],
                            ksb[:, kc, stt * cw:(stt + 1) * cw],
                            start=(kc == 0), stop=(kc == 7)),
                            waits=last(t_ql, t_kl) + (tu("psS") if kc == 0 and stt == 0 else []))
                t_nm = em.dve(lambda v, qt=qt: v.reduce_max(
                    out=ncol[:, qt:qt + 1], in_=SF[:, 0:seq],
                    axis=AX.X, negate=True),
                    waits=[t_s] + tu("ncol"))
                t_ex = em.act(lambda s, qt=qt: s.activation(
                    out=Pq[:, 0:seq], in_=SF[:, 0:seq],
                    func=AF.Exp, bias=ncol[:, qt:qt + 1],
                    accum_out=lcol[:, qt:qt + 1]),
                    waits=[t_nm] + tu("Pq"))
                U["psS"] = t_ex
                U["ncol"] = t_ex
                for stt2 in range(nst):
                    pT = pT0 if stt2 % 2 == 0 else pT1
                    t_tr = em.pe(lambda t, stt2=stt2, pT=pT: t.transpose(
                        pT[:, 0:128], Pq[:, stt2 * 128:(stt2 + 1) * 128], ident_bf),
                        waits=[t_ex, t_ib] + tu(("psT", stt2 % 2)))
                    t_ev = em.dve(lambda v, qt=qt, stt2=stt2, pT=pT: v.tensor_copy(
                        PT[:, stt2, qt * 128:(qt + 1) * 128], pT[:, 0:128]),
                        waits=[t_tr] + tu("PTw"))
                    U[("psT", stt2 % 2)] = t_ev
                    t_trs.append(t_ev)
                U["Pq"] = t_trs[-nst:]
            U["qsb"] = t_ex
            t_t2 = em.pe(lambda t: t.matmul(P1[0:nqt, 0:128], lcol[:, 0:nqt], identf,
                                            start=True, stop=True),
                         waits=[t_ex, t_i1] + tu("psS1"))
            t_e2 = em.dve(lambda v: v.tensor_copy(nlT[32:32 + nqt, :], P1[0:nqt, 0:128]),
                          waits=[t_t2] + tu("nlT"))
            U["psS1"] = t_e2
            t_r2 = em.dma("sp", lrow[0:1, 0:cw], nlT[32:32 + nqt, :],
                          waits=[t_e2] + tu("lrow"))
            t_rc = em.dve(lambda v: v.reciprocal(lrow[0:1, 0:cw], lrow[0:1, 0:cw]),
                          waits=[t_r2])
            U["nlT"] = [t_rc]
            fb = qc % 2
            t_rlrep = em.pe(lambda t: t.matmul(
                P2[:, 0:cw], ones_row, lrow[0:1, 0:cw],
                start=True, stop=True), waits=[t_rc, t_or] + tu("ps2c"))
            t_rle = em.act(lambda s, fb=fb: s.activation(
                out=rl[:, fb, :], in_=P2[:, 0:cw], func=AF.Copy),
                waits=[t_rlrep] + tu(("rl", fb)))
            U["ps2c"] = t_rle
            U["lrow"] = t_rle
            t_av = None
            for vd in range(8):
                bank = P6 if vd % 2 == 0 else P7
                bkey = ("psav", vd % 2)
                for stt2 in range(nst):
                    t_av = em.pe(lambda t, vd=vd, stt2=stt2, bank=bank: t.matmul(
                        bank[:, 0:cw], vtm[:, stt2, vd * 128:(vd + 1) * 128],
                        PT[:, stt2, 0:cw],
                        start=(stt2 == 0), stop=(stt2 == nst - 1)),
                        waits=last(t_vts[-1], t_trs[-1]) + (tu(bkey) if stt2 == 0 else []))
                t_xl = em.dma("sp", xfl[:, fb, :],
                              xnF[vd * 128:(vd + 1) * 128, tb0 + qc * cw:tb0 + (qc + 1) * cw],
                              waits=tu(("xfl", fb)))
                t_m = em.dve(lambda v, fb=fb, bank=bank: v.tensor_tensor(
                    out=yfb[:, fb, :], in0=bank[:, 0:cw], in1=rl[:, fb, :],
                    op=ALU.mult), waits=[t_av, t_rle] + tu(("yfb", fb)))
                U[bkey] = t_m
                t_y = em.dve(lambda v, fb=fb: v.tensor_tensor(
                    out=yfb[:, fb, :], in0=yfb[:, fb, :], in1=xfl[:, fb, :],
                    op=ALU.add), waits=[t_m, t_xl])
                U[("rl", fb)] = t_y
                U[("xfl", fb)] = t_y
                t_yc = em.dve(lambda v, fb=fb: v.tensor_copy(
                    ybb[:, fb, :], yfb[:, fb, :]), waits=[t_y] + tu(("ybb", fb)))
                t_d1 = em.dma("sp", yF[vd * 128:(vd + 1) * 128, tb0 + qc * cw:tb0 + (qc + 1) * cw],
                              yfb[:, fb, :], waits=[t_yc])
                t_d2 = em.dma("sp", yB[vd * 128:(vd + 1) * 128, tb0 + qc * cw:tb0 + (qc + 1) * cw],
                              ybb[:, fb, :], waits=[t_yc])
                U[("yfb", fb)] = t_d1
                U[("ybb", fb)] = t_d2
                t_att_done += [t_d1, t_d2]
            U["PTw"] = t_av
        U["vtmw"] = t_av
        U["ksb"] = t_av
        t_att_done = t_att_done[-8:]
    stB.close()

    t_fh = tree_L1(3, yB, yF, AF.Gelu, "f_w1T", t_att_done + t_cs)
    stW.close()

    stC = contextlib.ExitStack()
    E = stC.enter_context
    outE = E(nc.sbuf_tensor("s2_outE", [128, 8, tc], F32)).ap()
    yfc = E(nc.sbuf_tensor("s2_yfc", [128, 8, tc], F32)).ap()
    r1c = E(nc.sbuf_tensor("s2_r1c", [128, 8, tc], F32)).ap()
    sqc = E(nc.sbuf_tensor("s2_sqc", [128, tc], F32)).ap()
    rowc = E(nc.sbuf_tensor("s2_rowc", [1, 4, tc], F32)).ap()
    erot = E(nc.sbuf_tensor("s2_erot", [128, 2, tc], F32)).ap()
    tail_state = {}

    def tail_cb(ch, mo, ps, t_mm):
        t0 = ch * tc
        S = tail_state
        if mo == 0:
            S["yl"] = em.dma("sp", yfc,
                             yF.rearrange("(dc p) T -> p dc T", p=128)[:, :, t0:t0 + tc],
                             waits=tu("yfc"))
            S["rl"] = em.dma("sp", r1c,
                             R1.rearrange("(dc p) T -> p dc T", p=128)[:, :, t0:t0 + tc],
                             waits=tu("r1c"))
            S["m"] = {}
        t_o = em.dve(lambda v, mo=mo: v.tensor_tensor(
            out=outE[:, mo, :], in0=ps, in1=yfc[:, mo, :], op=ALU.add),
            waits=[t_mm, S["yl"]] + tu(("oE", mo)))
        t_sq = em.act(lambda s, mo=mo: s.activation(
            out=sqc, in_=outE[:, mo, :], func=AF.Square),
            waits=[t_o] + tu("sqc"))
        t_m1 = em.pe(lambda t, mo=mo: t.matmul(
            P2[0:1, 0:tc], ones_col, outE[:, mo, :],
            start=(mo == 0), stop=(mo == 7)),
            waits=[t_o, t_oc] + (tu("psr2") if mo == 0 else []))
        t_m2 = em.pe(lambda t, mo=mo: t.matmul(
            P3[0:1, 0:tc], ones_col, sqc,
            start=(mo == 0), stop=(mo == 7)),
            waits=[t_sq] + (tu("psr3") if mo == 0 else []))
        U["sqc"] = t_m2
        S["m"][mo] = (t_m1, t_m2, t_o)
        if mo < 7:
            return t_o
        mu = rowc[:, 0, :]
        vv = rowc[:, 1, :]
        rr = rowc[:, 2, :]
        bb = rowc[:, 3, :]
        t_mu = em.dve(lambda v: v.tensor_scalar_mul(mu, P2[0:1, 0:tc], 1.0 / DM),
                      waits=[t_m1] + tu("rowc"))
        t_q = em.act(lambda s: s.activation(out=vv, in_=mu, func=AF.Square),
                     waits=[t_mu])
        t_v = em.dve(lambda v: v.scalar_tensor_tensor(
            out=vv, in0=P3[0:1, 0:tc], scalar=1.0 / DM, in1=vv,
            op0=ALU.mult, op1=ALU.subtract), waits=[t_m2, t_q])
        t_1 = em.dve(lambda v: v.tensor_scalar_add(rr, vv, 1e-5), waits=[t_v])
        t_2 = em.act(lambda s: s.activation(out=rr, in_=rr, func=AF.Sqrt),
                     waits=[t_1])
        t_3 = em.dve(lambda v: v.reciprocal(rr, rr), waits=[t_2])
        t_4 = em.dve(lambda v: v.tensor_tensor(out=bb, in0=mu, in1=rr, op=ALU.mult),
                     waits=[t_3])
        t_5 = em.dve(lambda v: v.tensor_scalar_mul(bb, bb, -1.0), waits=[t_4])
        U["psr2"] = [t_v, t_mu]
        U["psr3"] = [t_v]
        U["rowc"] = t_5
        t_rr = em.pe(lambda t: t.matmul(P0[:, 0:tc], ones_row, rr,
                                        start=True, stop=True),
                     waits=last(t_3, t_or) + tu("psr0"))
        t_br = em.pe(lambda t: t.matmul(P1[:, 0:tc], ones_row, bb,
                                        start=True, stop=True),
                     waits=last(t_5, t_or) + tu("psr1"))
        t_last = None
        ta = None
        tb = None
        for mo2 in range(8):
            rb = mo2 % 2
            ta = em.dve(lambda v, mo2=mo2, rb=rb: v.tensor_tensor(
                out=erot[:, rb, :], in0=outE[:, mo2, :], in1=P0[:, 0:tc],
                op=ALU.mult),
                waits=[t_rr, S["m"][mo2][2]] + tu(("ero", rb)))
            tb = em.dve(lambda v, rb=rb: v.tensor_tensor(
                out=erot[:, rb, :], in0=erot[:, rb, :], in1=P1[:, 0:tc],
                op=ALU.add), waits=[t_br, ta])
            tcv = em.dve(lambda v, mo2=mo2, rb=rb: v.tensor_tensor(
                out=erot[:, rb, :], in0=erot[:, rb, :], in1=r1c[:, mo2, :],
                op=ALU.add), waits=[tb, S["rl"]])
            U[("oE", mo2)] = [ta, S["m"][mo2][0], S["m"][mo2][1]]
            t_st = em.dma("sp", R2[mo2 * 128:(mo2 + 1) * 128, t0:t0 + tc],
                          erot[:, rb, :], waits=[tcv])
            U[("ero", rb)] = t_st
            t_last = t_st
        U["psr0"] = ta
        U["psr1"] = tb
        U["yfc"] = ta
        U["r1c"] = tcv
        return t_last

    t_f2 = tree_L2(3, "f_w2T", t_fh, tail_cb)
    stC.close()
    stS.close()
    return t_f2


def host_consts_stage2():
    scm = np.zeros((128, 8, 2, 256), np.float32)
    for k in range(8):
        for c in (0, 1):
            for j in range(2 ** k):
                scm[j, k, c, 2 * j + c] = 1.0
    Mg = np.zeros((128, 16, 128), np.float32)
    for gi in range(16):
        for j in range(8):
            for l in range(16):
                Mg[8 * gi + j, gi, 16 * j + l] = 1.0
    sel16T = np.zeros((16, 128), np.float32)
    sel128 = np.zeros((128, 16), np.float32)
    for j in range(8):
        for l in range(16):
            sel16T[l, 16 * j + l] = 1.0
            sel128[16 * j + l, l] = 1.0
    return {"ident": np.eye(128, dtype=np.float32), "scm": scm, "Mg": Mg,
            "sel16T": sel16T, "sel128": sel128}


def host_prep_stage2(z, core, seq=2048):
    d = dict(host_consts_stage2())
    trees = ["q", "k", "v", "f"]
    d["nwT4"] = np.ascontiguousarray(np.stack(
        [z[f"L0_ffn_{t}_nw"].T for t in trees], axis=1).astype(np.float32))
    d["nb4"] = np.ascontiguousarray(np.stack(
        [z[f"L0_ffn_{t}_nb"] for t in trees], axis=0)[None].astype(np.float32))
    d["b14"] = np.ascontiguousarray(np.stack(
        [z[f"L0_ffn_{t}_b1"] for t in trees], axis=0).astype(np.float32))
    d["b24"] = np.ascontiguousarray(np.stack(
        [z[f"L0_ffn_{t}_b2"] for t in trees], axis=0).astype(np.float32))
    for t in trees:
        d[f"{t}_w1T"] = np.ascontiguousarray(
            z[f"L0_ffn_{t}_w1"].transpose(1, 0, 2).reshape(1024, 4096).astype(np.float32))
        d[f"{t}_w2T"] = np.ascontiguousarray(
            z[f"L0_ffn_{t}_w2"].reshape(4096, 1024).astype(np.float32))
    return d


def build_stage2_program(seq=2048, tc=512):
    nc = bass.Bass("TRN2", target_bir_lowering=False)
    ntok = 2 * seq
    io = {}
    io["R1"] = nc.dram_tensor("R1", [DM, ntok], F32, kind="ExternalInput").ap()
    io["ident"] = nc.dram_tensor("ident", [128, 128], F32, kind="ExternalInput").ap()
    io["scm"] = nc.dram_tensor("scm", [128, 8, 2, 256], F32, kind="ExternalInput").ap()
    io["Mg"] = nc.dram_tensor("Mg", [128, 16, 128], F32, kind="ExternalInput").ap()
    io["sel16T"] = nc.dram_tensor("sel16T", [16, 128], F32, kind="ExternalInput").ap()
    io["sel128"] = nc.dram_tensor("sel128", [128, 16], F32, kind="ExternalInput").ap()
    io["nwT4"] = nc.dram_tensor("nwT4", [1024, 4, 255], F32, kind="ExternalInput").ap()
    io["nb4"] = nc.dram_tensor("nb4", [1, 4, 255], F32, kind="ExternalInput").ap()
    io["b14"] = nc.dram_tensor("b14", [4, 256, 16], F32, kind="ExternalInput").ap()
    io["b24"] = nc.dram_tensor("b24", [4, 256, 1024], F32, kind="ExternalInput").ap()
    for t in ["q", "k", "v", "f"]:
        io[f"{t}_w1T"] = nc.dram_tensor(f"{t}_w1T", [1024, 4096], F32,
                                        kind="ExternalInput").ap()
        io[f"{t}_w2T"] = nc.dram_tensor(f"{t}_w2T", [4096, 1024], F32,
                                        kind="ExternalInput").ap()
    io["R2"] = nc.dram_tensor("R2", [DM, ntok], F32, kind="ExternalOutput").ap()
    em = Em(nc)
    build_stage2(nc, em, io, seq=seq, tc=tc)
    return nc, em




_S3SUB = ("q", "k", "v")
_S3NMO = 8
_S3HACK = 0


def build_stage3(nc, em, io, seq=2048, tc=512, upto=None):
    sub = _S3SUB
    stS = contextlib.ExitStack()

    def A(name, shape, dt):
        return stS.enter_context(nc.sbuf_tensor(name, shape, dt))
    ntok = 2 * seq
    nch = ntok // tc
    nst = seq // 128
    ncw = seq // 512 if seq >= 512 else 1
    cw = min(512, seq)
    nqt = cw // 128

    R2 = io["R2"]
    R3 = io["R3"]

    xbD = nc.dram_tensor("s3_xb", [DM, ntok], BF16, kind="Internal").ap()
    qroD = nc.dram_tensor("s3_qro", [DM, ntok], BF16, kind="Internal").ap()
    krD = nc.dram_tensor("s3_kr", [DM, ntok], BF16, kind="Internal").ap()
    vD3 = nc.dram_tensor("s3_v", [512, ntok], BF16, kind="Internal").ap()
    avD = nc.dram_tensor("s3_av", [512, ntok], BF16, kind="Internal").ap()

    PSALL = stS.enter_context(nc.psum_tensor("s3_psall", [128, 4, 512], F32)).ap()
    P0 = PSALL[:, 0, :]
    P1 = PSALL[:, 1, :]
    P2 = PSALL[:, 2, :]
    P3 = PSALL[:, 3, :]
    pT0 = stS.enter_context(nc.psum_tensor("s3_pT0", [128, 1024], BF16)).ap()
    pT1 = stS.enter_context(nc.psum_tensor("s3_pT1", [128, 1024], BF16)).ap()
    P6 = stS.enter_context(nc.psum_tensor("s3_p6", [128, 512], F32)).ap()
    P7 = stS.enter_context(nc.psum_tensor("s3_p7", [128, 512], F32)).ap()
    SF = PSALL.rearrange("p a b -> p (a b)")

    ones_col = A("s3_onesc", [128, 1], F32).ap()
    ones_row = A("s3_onesr", [1, 128], F32).ap()
    identf = A("s3_identf", [128, 128], F32).ap()
    ident_bf = A("s3_identb", [128, 128], BF16).ap()
    seg2c = A("s3_seg2c", [128, 8], F32).ap()
    kmask = A("s3_kmask", [128, 8], F32).ap()
    mTd = A("s3_mTd", [128, 128], F32).ap()
    mUp = A("s3_mUp", [128, 128], F32).ap()
    qg = A("s3_qg", [128, 8], F32).ap()
    kg = A("s3_kg", [128, 8], F32).ap()
    wq = A("s3_wq", [128, 8, 1024], BF16).ap()
    wkv = A("s3_wkv", [128, 8, 128], BF16).ap()
    wk = A("s3_wk", [128, 1024], BF16).ap()
    wv = A("s3_wv", [128, 512], BF16).ap()
    wo = A("s3_wo", [128, 4, 1024], BF16).ap()

    t_oc = em.pool(lambda g: g.memset(ones_col, 1.0))
    t_or = em.pool(lambda g: g.memset(ones_row, 1.0))
    t_cs = [t_oc, t_or]
    t_cs.append(em.dma("sp", identf, io["ident"]))
    t_cs.append(em.dve(lambda v: v.tensor_copy(ident_bf, identf), waits=[t_cs[-1]]))
    t_s2c = em.dma("sp", seg2c, io["seg2c"])
    t_s4c = em.dma("sp", kmask, io["kmask"])
    t_cs += [t_s2c, t_s4c]
    t_cs.append(em.dma("sp", mTd, io["mTd"]))
    t_mup = em.dma("sp", mUp, io["mUp"])
    t_cs.append(t_mup)
    t_qg = em.dma("sp", qg, io["qg"])
    t_kg = em.dma("sp", kg, io["kg"])
    t_cs += [t_qg, t_kg]
    t_wq = em.dma("pool", wq, io["Wq"].rearrange("(kc p) m -> p kc m", p=128))
    t_wkv = em.dma("pool", wkv, io["Wkv"].rearrange("(kc p) m -> p kc m", p=128))
    t_wk = em.dma("pool", wk, io["WkP"])
    t_wv = em.dma("pool", wv, io["Wv"])
    t_wo = em.dma("pool", wo, io["WoS"].rearrange("(kc p) m -> p kc m", p=128))

    U = {}

    def tu(*k):
        return last(*[U.get(x) for x in k])

    st0 = contextlib.ExitStack()
    E0 = st0.enter_context
    rch = E0(nc.sbuf_tensor("s3_rch", [128, 8, tc], F32)).ap()
    sq = E0(nc.sbuf_tensor("s3_sq", [128, tc], F32)).ap()
    xnc = E0(nc.sbuf_tensor("s3_xnc", [128, 8, tc], BF16)).ap()
    rows = E0(nc.sbuf_tensor("s3_rows", [1, 4, tc], F32)).ap()
    kvc = E0(nc.sbuf_tensor("s3_kvc", [128, tc], BF16)).ap()
    qsb = E0(nc.sbuf_tensor("s3_qsbA", [128, 2, tc], F32)).ap()
    rms = E0(nc.sbuf_tensor("s3_rms", [1, 8, 2 * tc], F32)).ap()
    evb = E0(nc.sbuf_tensor("s3_evb", [128, 2, tc], BF16)).ap()

    t_proj_done = []
    for ch in range(nch):
        t0 = ch * tc
        t_r = em.dma("sp", rch,
                     R2.rearrange("(dc p) T -> p dc T", p=128)[:, :, t0:t0 + tc],
                     waits=tu("rch"))
        t_m1 = None
        t_m2 = None
        for dc in range(8):
            t_m1 = em.pe(lambda t, dc=dc: t.matmul(
                P6[0:1, 0:tc], ones_col, rch[:, dc, :],
                start=(dc == 0), stop=(dc == 7)),
                waits=last(t_r, t_oc) + (tu("ps67") if dc == 0 else []))
            t_sq = em.act(lambda s, dc=dc: s.activation(
                out=sq, in_=rch[:, dc, :], func=AF.Square),
                waits=[t_r] + tu("sq"))
            t_m2 = em.pe(lambda t, dc=dc: t.matmul(
                P7[0:1, 0:tc], ones_col, sq,
                start=(dc == 0), stop=(dc == 7)),
                waits=[t_sq] + (tu("ps67") if dc == 0 else []))
            U["sq"] = t_m2
        mu = rows[:, 0, :]
        vv = rows[:, 1, :]
        rr = rows[:, 2, :]
        bb = rows[:, 3, :]
        t_mu = em.dve(lambda v: v.tensor_scalar_mul(mu, P6[0:1, 0:tc], 1.0 / DM),
                      waits=[t_m1] + tu("rows"))
        t_q = em.act(lambda s: s.activation(out=vv, in_=mu, func=AF.Square),
                     waits=[t_mu])
        t_v = em.dve(lambda v: v.scalar_tensor_tensor(
            out=vv, in0=P7[0:1, 0:tc], scalar=1.0 / DM, in1=vv,
            op0=ALU.mult, op1=ALU.subtract), waits=[t_m2, t_q])
        t_1 = em.dve(lambda v: v.tensor_scalar_add(rr, vv, 1e-5), waits=[t_v])
        t_2 = em.act(lambda s: s.activation(out=rr, in_=rr, func=AF.Sqrt), waits=[t_1])
        t_3 = em.dve(lambda v: v.reciprocal(rr, rr), waits=[t_2])
        t_4 = em.dve(lambda v: v.tensor_tensor(out=bb, in0=mu, in1=rr, op=ALU.mult),
                     waits=[t_3])
        t_5 = em.dve(lambda v: v.tensor_scalar_mul(bb, bb, -1.0), waits=[t_4])
        U["ps67"] = [t_v, t_mu]
        U["rows"] = t_5
        t_sr = em.pe(lambda t: t.matmul(P2[:, 0:tc], ones_row, rr,
                                        start=True, stop=True),
                     waits=last(t_3, t_or) + tu("ps2"))
        t_br = em.pe(lambda t: t.matmul(P3[:, 0:tc], ones_row, bb,
                                        start=True, stop=True),
                     waits=last(t_5, t_or) + tu("ps3"))
        t_xns = []
        for dc in range(8):
            ta = em.dve(lambda v, dc=dc: v.tensor_tensor(
                out=sq, in0=rch[:, dc, :], in1=P2[:, 0:tc], op=ALU.mult),
                waits=[t_sr] + tu("sq"))
            tb = em.dve(lambda v, dc=dc: v.tensor_tensor(
                out=xnc[:, dc, :], in0=sq, in1=P3[:, 0:tc], op=ALU.add),
                waits=[t_br, ta] + tu("xnc"))
            U["sq"] = tb
            t_xns.append(tb)
        U["ps2"] = t_xns[-1]
        U["ps3"] = t_xns[-1]
        U["rch"] = [t_m1, t_xns[-1]]
        t_xd = em.dma("sp", xbD.rearrange("(dc p) T -> p dc T", p=128)[:, :, t0:t0 + tc],
                      xnc, waits=t_xns + tu("xbD"))
        U["xnc"] = t_xd
        U["xbD"] = t_xd

        for mo in (range(_S3NMO) if 'q' in sub else []):
            qb = 0 if _S3HACK == 1 else mo % 2
            bank = P6 if qb == 0 else P7
            t_mm = None
            for kc in range(8):
                t_mm = em.pe(lambda t, mo=mo, kc=kc, bank=bank: t.matmul(
                    bank[:, 0:tc], wq[:, kc, mo * 128:(mo + 1) * 128],
                    xnc[:, kc, :], start=(kc == 0), stop=(kc == 7)),
                    waits=last(t_xns[kc], t_wq) + (tu(("psq", qb)) if kc == 0 else []))
            t_ev = em.dve(lambda v, qb=qb, bank=bank: v.tensor_copy(
                qsb[:, qb, :], bank[:, 0:tc]), waits=[t_mm] + tu(("qsb", qb)))
            t_sq2 = em.act(lambda s, qb=qb, bank=bank: s.activation(
                out=sq, in_=bank[:, 0:tc], func=AF.Square),
                waits=[t_mm, t_ev] + tu("sq"))
            t_rm = None
            for hh in range(2):
                t_rm = em.pe(lambda t, hh=hh: t.matmul(
                    SF[0:1, hh * tc:(hh + 1) * tc], seg2c[:, hh:hh + 1],
                    sq, start=(hh == 0 or tc == 512),
                    stop=(hh == 1 or tc == 512)),
                    waits=([t_sq2, t_s2c] + tu("ps01")) if hh == 0 else [t_sq2])
            U[("psq", qb)] = [t_ev, t_sq2]
            t_r1 = em.dve(lambda v, mo=mo: v.tensor_scalar(
                out=rms[0:1, mo, :], in0=SF[0:1, 0:2 * tc], scalar1=1.0 / 64,
                scalar2=1e-6, op0=ALU.mult, op1=ALU.add),
                waits=[t_rm] + tu(("rms", mo)))
            t_r2 = em.act(lambda s, mo=mo: s.activation(
                out=rms[0:1, mo, :], in_=rms[0:1, mo, :], func=AF.Sqrt),
                waits=[t_r1])
            t_r3 = em.dve(lambda v, mo=mo: v.tensor_scalar_mul(
                rms[0:1, mo, :], rms[0:1, mo, :], 8.0), waits=[t_r2])
            t_r4 = em.dve(lambda v, mo=mo: v.reciprocal(
                rms[0:1, mo, :], rms[0:1, mo, :]), waits=[t_r3])
            U["ps01"] = t_r1
            U["sq"] = t_rm
            t_rpA = em.pe(lambda t, mo=mo: t.matmul(
                P2[:, 0:tc], ones_row, rms[0:1, mo, 0:tc],
                start=True, stop=True), waits=[t_r4, t_or] + tu("ps2"))
            t_rpB = em.pe(lambda t, mo=mo: t.matmul(
                P3[:, 0:tc], ones_row, rms[0:1, mo, tc:2 * tc],
                start=True, stop=True), waits=[t_r4, t_or] + tu("ps3"))
            U[("rms", mo)] = [t_rpA, t_rpB]
            t_qn0 = em.dve(lambda v, qb=qb: v.scalar_tensor_tensor(
                out=evb[0:64, qb, :], in0=qsb[0:64, qb, :], scalar=qg[0:64, 0:1],
                in1=P2[0:64, 0:tc], op0=ALU.mult, op1=ALU.mult),
                waits=[t_rpA, t_ev, t_qg] + tu(("evb", qb)))
            t_qn = em.dve(lambda v, qb=qb: v.scalar_tensor_tensor(
                out=evb[64:128, qb, :], in0=qsb[64:128, qb, :], scalar=qg[64:128, 0:1],
                in1=P3[64:128, 0:tc], op0=ALU.mult, op1=ALU.mult),
                waits=[t_rpB, t_ev, t_qg, t_qn0])
            U["ps2"] = t_qn0
            U["ps3"] = t_qn
            U[("qsb", qb)] = t_qn
            t_qd = em.dma("sp", qroD[mo * 128:(mo + 1) * 128, t0:t0 + tc],
                          evb[:, qb, :], waits=[t_qn])
            U[("evb", qb)] = t_qd
            t_proj_done.append(t_qd)

        t_kv = None
        for kc in range(8):
            t_kv = em.pe(lambda t, kc=kc: t.matmul(
                P0[:, 0:tc], wkv[:, kc, :], xnc[:, kc, :],
                start=(kc == 0), stop=(kc == 7)),
                waits=last(t_xns[kc], t_wkv) + (tu("ps0") if kc == 0 else []))
        t_kve = em.dve(lambda v: v.tensor_copy(kvc, P0[:, 0:tc]),
                       waits=[t_kv] + tu("kvc"))
        U["ps0"] = t_kve
        for mo in (range(8) if 'k' in sub else []):
            qb = mo % 2
            bank = P6 if qb == 0 else P7
            t_km = em.pe(lambda t, mo=mo, bank=bank: t.matmul(
                bank[:, 0:tc], wk[:, mo * 128:(mo + 1) * 128], kvc,
                start=True, stop=True),
                waits=last(t_kve, t_wk) + tu(("psq", qb)))
            t_ev = em.dve(lambda v, qb=qb, bank=bank: v.tensor_copy(
                qsb[:, qb, :], bank[:, 0:tc]), waits=[t_km] + tu(("qsb", qb)))
            t_sq2 = em.act(lambda s, bank=bank: s.activation(
                out=sq, in_=bank[:, 0:tc], func=AF.Square),
                waits=[t_km, t_ev] + tu("sq"))
            t_rm = None
            for hh in range(2):
                t_rm = em.pe(lambda t, hh=hh: t.matmul(
                    SF[0:1, hh * tc:(hh + 1) * tc], kmask[:, hh:hh + 1],
                    sq, start=(hh == 0 or tc == 512),
                    stop=(hh == 1 or tc == 512)),
                    waits=([t_sq2, t_s4c] + tu("ps01")) if hh == 0 else [t_sq2])
            U[("psq", qb)] = [t_ev, t_sq2]
            t_r1 = em.dve(lambda v, mo=mo: v.tensor_scalar(
                out=rms[0:1, mo, :], in0=SF[0:1, 0:2 * tc], scalar1=1.0 / 32,
                scalar2=1e-6, op0=ALU.mult, op1=ALU.add),
                waits=[t_rm] + tu(("rms", mo)))
            t_r2 = em.act(lambda s, mo=mo: s.activation(
                out=rms[0:1, mo, :], in_=rms[0:1, mo, :], func=AF.Sqrt),
                waits=[t_r1])
            t_r4 = em.dve(lambda v, mo=mo: v.reciprocal(
                rms[0:1, mo, :], rms[0:1, mo, :]), waits=[t_r2])
            U["ps01"] = t_r1
            U["sq"] = t_rm
            t_rpA = em.pe(lambda t, mo=mo: t.matmul(
                P2[:, 0:tc], ones_row, rms[0:1, mo, 0:tc],
                start=True, stop=True), waits=[t_r4, t_or] + tu("ps2"))
            t_rpB = em.pe(lambda t, mo=mo: t.matmul(
                P3[:, 0:tc], ones_row, rms[0:1, mo, tc:2 * tc],
                start=True, stop=True), waits=[t_r4, t_or] + tu("ps3"))
            U[("rms", mo)] = [t_rpA, t_rpB]
            t_kn0 = em.dve(lambda v, qb=qb: v.scalar_tensor_tensor(
                out=evb[0:64, qb, :], in0=qsb[0:64, qb, :], scalar=kg[0:64, 0:1],
                in1=P2[0:64, 0:tc], op0=ALU.mult, op1=ALU.mult),
                waits=[t_rpA, t_ev, t_kg] + tu(("evb", qb)))
            t_kn = em.dve(lambda v, qb=qb: v.scalar_tensor_tensor(
                out=evb[64:128, qb, :], in0=qsb[64:128, qb, :], scalar=kg[64:128, 0:1],
                in1=P3[64:128, 0:tc], op0=ALU.mult, op1=ALU.mult),
                waits=[t_rpB, t_ev, t_kg, t_kn0])
            U["ps2"] = t_kn0
            U["ps3"] = t_kn
            U[("qsb", qb)] = t_kn
            t_kd = em.dma("sp", krD[mo * 128:(mo + 1) * 128, t0:t0 + tc],
                          evb[:, qb, :], waits=[t_kn])
            t_proj_done.append(t_kd)
            U[("evb", qb)] = t_kd
        for mo in (range(4) if 'v' in sub else []):
            qb = mo % 2
            bank = P6 if qb == 0 else P7
            t_vm = em.pe(lambda t, mo=mo, bank=bank: t.matmul(
                bank[:, 0:tc], wv[:, mo * 128:(mo + 1) * 128], kvc,
                start=True, stop=True),
                waits=last(t_kve, t_wv) + tu(("psq", qb)))
            t_ev = em.dve(lambda v, qb=qb, bank=bank: v.tensor_copy(
                evb[:, qb, :], bank[:, 0:tc]), waits=[t_vm] + tu(("evb", qb)))
            U[("psq", qb)] = t_ev
            t_vd = em.dma("sp", vD3[mo * 128:(mo + 1) * 128, t0:t0 + tc],
                          evb[:, qb, :], waits=[t_ev])
            U[("evb", qb)] = t_vd
            t_proj_done.append(t_vd)
        U["kvc"] = t_vm if 'v' in sub else (t_km if 'k' in sub else t_kve)
    st0.close()
    t_proj_done = t_proj_done[-8:]
    if upto == "A":
        return t_proj_done

    stB = contextlib.ExitStack()
    E = stB.enter_context
    qro = E(nc.sbuf_tensor("s3_qro_sb", [128, 8, seq], BF16)).ap()
    krs = E(nc.sbuf_tensor("s3_krs", [128, 8, seq], BF16)).ap()
    vstg = E(nc.sbuf_tensor("s3_vstg", [128, 4, seq], BF16)).ap()
    vtm = E(nc.sbuf_tensor("s3_vtm", [128, nst, 512], BF16)).ap()
    Pq = E(nc.sbuf_tensor("s3_Pq", [128, seq], BF16)).ap()
    PT = E(nc.sbuf_tensor("s3_PT", [128, nst, cw], BF16)).ap()
    ncol = E(nc.sbuf_tensor("s3_ncol", [128, 16], F32)).ap()
    lcol = E(nc.sbuf_tensor("s3_lcol", [128, 16], F32)).ap()
    nlT = E(nc.sbuf_tensor("s3_nlT", [64, 128], F32)).ap()
    lrow = E(nc.sbuf_tensor("s3_lrow", [1, cw], F32)).ap()
    rl32 = E(nc.sbuf_tensor("s3_rl32", [32, cw], F32)).ap()
    oc32 = E(nc.sbuf_tensor("s3_oc32", [32, 2, cw], BF16)).ap()
    rowD = nc.dram_tensor("s3_rowD", [16, 128], F32, kind="Internal").ap()

    t_att_done = []
    prevB = t_proj_done
    for b in range(2):
        tb0 = b * seq
        t_ql = em.dma("sp", qro,
                      qroD.rearrange("(dc p) T -> p dc T", p=128)[:, :, tb0:tb0 + seq],
                      waits=prevB + tu("qro"))
        t_kl = em.dma("sp", krs,
                      krD.rearrange("(dc p) T -> p dc T", p=128)[:, :, tb0:tb0 + seq],
                      waits=prevB + tu("krs"))
        t_vl = em.dma("sp", vstg,
                      vD3.rearrange("(dc p) T -> p dc T", p=128)[:, :, tb0:tb0 + seq],
                      waits=prevB + tu("vstg"))
        prevB = []
        t_vts = []
        for dc in range(4):
            for stt in range(nst):
                pT = pT0 if (dc * nst + stt) % 2 == 0 else pT1
                t_tr = em.pe(lambda t, dc=dc, stt=stt, pT=pT: t.transpose(
                    pT[:, 0:128], vstg[:, dc, stt * 128:(stt + 1) * 128], ident_bf),
                    waits=[t_vl] + t_cs[2:4] + tu(("psT", (dc * nst + stt) % 2)))
                t_ev = em.dve(lambda v, dc=dc, stt=stt, pT=pT: v.tensor_copy(
                    vtm[:, stt, dc * 128:(dc + 1) * 128], pT[:, 0:128]),
                    waits=[t_tr] + tu("vtmw"))
                U[("psT", (dc * nst + stt) % 2)] = t_ev
                t_vts.append(t_ev)
        U["vstg"] = t_vts[-1]
        for h in range(16):
            hb = (h * 64) % 128
            hc = (h * 64) // 128
            t_av = None
            for qc in range(ncw):
                q0 = qc * cw
                stmax = (qc + 1) * nqt
                t_pts = {}
                for qt in range(qc * nqt, (qc + 1) * nqt):
                    smax = (qt + 1) * 128
                    t_s = None
                    for sc in range((smax + 511) // 512):
                        w = min(512, smax - sc * 512)
                        bank_waits = []
                        if sc == 0:
                            bank_waits = tu("psS") + tu("psS0")
                        elif sc == 2:
                            bank_waits = tu("ps2av")
                        elif sc == 3:
                            bank_waits = tu("ps3av")
                        t_s = em.pe(lambda t, qt=qt, sc=sc, w=w, hb=hb, hc=hc: t.matmul(
                            SF[:, sc * 512:sc * 512 + w],
                            qro[hb:hb + 32, hc, qt * 128:(qt + 1) * 128],
                            krs[hb:hb + 32, hc, sc * 512:sc * 512 + w],
                            start=True, stop=True),
                            waits=last(t_ql, t_kl) + bank_waits)
                    t_mk = em.dve(lambda v, qt=qt: v.tensor_tensor(
                        out=SF[:, qt * 128:(qt + 1) * 128],
                        in0=SF[:, qt * 128:(qt + 1) * 128], in1=mUp,
                        op=ALU.add), waits=[t_s, t_mup] + tu("psSm"))
                    t_nm = em.dve(lambda v, qt=qt, smax=smax: v.reduce_max(
                        out=ncol[:, qt:qt + 1], in_=SF[:, 0:smax],
                        axis=AX.X, negate=True),
                        waits=[t_mk] + tu("ncol"))
                    t_ex = em.act(lambda s, qt=qt, smax=smax: s.activation(
                        out=Pq[:, 0:smax], in_=SF[:, 0:smax],
                        func=AF.Exp, bias=ncol[:, qt:qt + 1],
                        accum_out=lcol[:, qt:qt + 1]),
                        waits=[t_nm] + tu("Pq"))
                    U["psS"] = t_ex
                    U["psSm"] = t_ex
                    U["ncol"] = t_ex
                    t_trs_q = []
                    for stt in range(qt + 1):
                        pT = pT0 if stt % 2 == 0 else pT1
                        t_tr = em.pe(lambda t, stt=stt, pT=pT: t.transpose(
                            pT[:, 0:128], Pq[:, stt * 128:(stt + 1) * 128], ident_bf),
                            waits=[t_ex] + t_cs[2:4] + tu(("psT", stt % 2)))
                        t_evp = em.dve(lambda v, qt=qt, stt=stt, pT=pT, q0=q0: v.tensor_copy(
                            PT[:, stt, qt * 128 - q0:(qt + 1) * 128 - q0], pT[:, 0:128]),
                            waits=[t_tr] + tu("PTw"))
                        U[("psT", stt % 2)] = t_evp
                        t_trs_q.append(t_evp)
                        t_pts.setdefault(stt, []).append(t_evp)
                    U["Pq"] = t_trs_q
                for stt in range(stmax):
                    rel = stt - qc * nqt
                    if rel > 0:
                        t_pz = em.dve(lambda v, stt=stt, rel=rel: v.memset(
                            PT[:, stt, 0:rel * 128], 0.0),
                            waits=tu("PTw") + tu(("pzw", stt)))
                        t_pts.setdefault(stt, []).append(t_pz)
                        U[("pzw", stt)] = t_pz
                t_t2 = em.pe(lambda t, qc=qc: t.matmul(
                    P0[0:nqt, 0:128], lcol[:, qc * nqt:(qc + 1) * nqt],
                    identf, start=True, stop=True),
                    waits=[t_ex, t_cs[2]] + tu("psS0"))
                t_e2 = em.dve(lambda v: v.tensor_copy(nlT[32:32 + nqt, :],
                                                      P0[0:nqt, 0:128]),
                              waits=[t_t2] + tu("nlT"))
                U["psS0"] = t_e2
                t_d2 = em.dma("sp", rowD[0:nqt, :], nlT[32:32 + nqt, :],
                              waits=[t_e2] + tu("rowD"))
                t_l2 = em.dma("sp", lrow[0:1, 0:cw],
                              rowD[0:nqt, :].rearrange("a b -> (a b)").unsqueeze(0),
                              waits=[t_d2] + tu("lrow"))
                t_rc = em.dve(lambda v: v.reciprocal(lrow[0:1, 0:cw],
                                                     lrow[0:1, 0:cw]),
                              waits=[t_l2])
                U["nlT"] = t_rc
                U["rowD"] = t_rc
                for stt in range(stmax):
                    t_av = em.pe(lambda t, stt=stt, h=h, stmax=stmax: t.matmul(
                        P2[0:32, 0:cw], vtm[:, stt, h * 32:(h + 1) * 32],
                        PT[:, stt, 0:cw],
                        start=(stt == 0), stop=(stt == stmax - 1)),
                        waits=last(*t_pts.get(stt, [])) + [t_vts[-1]]
                        + (tu("ps2av") if stt == 0 else []))
                t_rr = em.pe(lambda t: t.matmul(
                    P3[0:32, 0:cw], ones_row[0:1, 0:32], lrow[0:1, 0:cw],
                    start=True, stop=True),
                    waits=[t_rc, t_or] + tu("ps3av"))
                t_rle = em.act(lambda s: s.activation(
                    out=rl32, in_=P3[0:32, 0:cw], func=AF.Copy),
                    waits=[t_rr] + tu("rl32"))
                U["ps3av"] = t_rle
                ob = qc % 2
                t_o = em.dve(lambda v, ob=ob: v.tensor_tensor(
                    out=oc32[:, ob, :], in0=P2[0:32, 0:cw], in1=rl32,
                    op=ALU.mult), waits=[t_av, t_rle] + tu(("oc", ob)))
                U["ps2av"] = t_o
                U["rl32"] = t_o
                t_od = em.dma("sp", avD[h * 32:(h + 1) * 32, tb0 + q0:tb0 + q0 + cw],
                              oc32[:, ob, :], waits=[t_o])
                U[("oc", ob)] = t_od
                t_att_done.append(t_od)
                U["lrow"] = [t_rr]
                U["PTw"] = t_av
        U["qro"] = t_av
        U["krs"] = t_av
        U["vtmw"] = t_av
        t_att_done = t_att_done[-2:]
    stB.close()

    stC = contextlib.ExitStack()
    E = stC.enter_context
    avc = E(nc.sbuf_tensor("s3_avc", [128, 4, tc], BF16)).ap()
    outE = E(nc.sbuf_tensor("s3_outE", [128, 8, tc], F32)).ap()
    r2c = E(nc.sbuf_tensor("s3_r2c", [128, 8, tc], F32)).ap()
    sqc = E(nc.sbuf_tensor("s3_sqc", [128, tc], F32)).ap()
    rowc = E(nc.sbuf_tensor("s3_rowc", [1, 4, tc], F32)).ap()
    erot = E(nc.sbuf_tensor("s3_erot", [128, 2, tc], F32)).ap()
    t_ret = []
    for ch in range(nch):
        t0 = ch * tc
        t_al = em.dma("sp", avc,
                      avD.rearrange("(dc p) T -> p dc T", p=128)[:, :, t0:t0 + tc],
                      waits=(t_att_done if ch == 0 else []) + tu("avc"))
        t_rl = em.dma("sp", r2c,
                      R2.rearrange("(dc p) T -> p dc T", p=128)[:, :, t0:t0 + tc],
                      waits=tu("r2c"))
        t_m1 = None
        t_m2 = None
        t_os = []
        for mo in range(8):
            qb = mo % 2
            bank = P6 if qb == 0 else P7
            t_mm = None
            for kc in range(4):
                t_mm = em.pe(lambda t, mo=mo, kc=kc, bank=bank: t.matmul(
                    bank[:, 0:tc], wo[:, kc, mo * 128:(mo + 1) * 128],
                    avc[:, kc, :], start=(kc == 0), stop=(kc == 3)),
                    waits=last(t_al, t_wo) + (tu(("psq", qb)) if kc == 0 else []))
            t_o = em.dve(lambda v, mo=mo, bank=bank: v.tensor_copy(
                outE[:, mo, :], bank[:, 0:tc]),
                waits=[t_mm] + tu(("oE", mo)))
            t_os.append(t_o)
            U[("psq", qb)] = t_o
            t_sq = em.act(lambda s, mo=mo: s.activation(
                out=sqc, in_=outE[:, mo, :], func=AF.Square),
                waits=[t_o] + tu("sqc"))
            t_m1 = em.pe(lambda t, mo=mo: t.matmul(
                P0[0:1, 0:tc], ones_col, outE[:, mo, :],
                start=(mo == 0), stop=(mo == 7)),
                waits=[t_o, t_oc] + (tu("psr0") if mo == 0 else []))
            t_m2 = em.pe(lambda t, mo=mo: t.matmul(
                P1[0:1, 0:tc], ones_col, sqc,
                start=(mo == 0), stop=(mo == 7)),
                waits=[t_sq] + (tu("psr1") if mo == 0 else []))
            U["sqc"] = t_m2
        mu = rowc[:, 0, :]
        vv = rowc[:, 1, :]
        rr = rowc[:, 2, :]
        bb = rowc[:, 3, :]
        t_mu = em.dve(lambda v: v.tensor_scalar_mul(mu, P0[0:1, 0:tc], 1.0 / DM),
                      waits=[t_m1] + tu("rowc"))
        t_q = em.act(lambda s: s.activation(out=vv, in_=mu, func=AF.Square),
                     waits=[t_mu])
        t_v = em.dve(lambda v: v.scalar_tensor_tensor(
            out=vv, in0=P1[0:1, 0:tc], scalar=1.0 / DM, in1=vv,
            op0=ALU.mult, op1=ALU.subtract), waits=[t_m2, t_q])
        t_1 = em.dve(lambda v: v.tensor_scalar_add(rr, vv, 1e-5), waits=[t_v])
        t_2 = em.act(lambda s: s.activation(out=rr, in_=rr, func=AF.Sqrt), waits=[t_1])
        t_3 = em.dve(lambda v: v.reciprocal(rr, rr), waits=[t_2])
        t_4 = em.dve(lambda v: v.tensor_tensor(out=bb, in0=mu, in1=rr, op=ALU.mult),
                     waits=[t_3])
        t_5 = em.dve(lambda v: v.tensor_scalar_mul(bb, bb, -1.0), waits=[t_4])
        U["psr0"] = [t_v, t_mu]
        U["psr1"] = [t_v]
        U["rowc"] = t_5
        t_rr2 = em.pe(lambda t: t.matmul(P2[:, 0:tc], ones_row, rr,
                                         start=True, stop=True),
                      waits=last(t_3, t_or) + tu("psr2"))
        t_br = em.pe(lambda t: t.matmul(P3[:, 0:tc], ones_row, bb,
                                        start=True, stop=True),
                     waits=last(t_5, t_or) + tu("psr3"))
        t_last = None
        ta = None
        tb = None
        for mo in range(8):
            rb = mo % 2
            ta = em.dve(lambda v, mo=mo, rb=rb: v.tensor_tensor(
                out=erot[:, rb, :], in0=outE[:, mo, :], in1=P2[:, 0:tc],
                op=ALU.mult), waits=[t_rr2, t_os[mo]] + tu(("ero", rb)))
            tb = em.dve(lambda v, rb=rb: v.tensor_tensor(
                out=erot[:, rb, :], in0=erot[:, rb, :], in1=P3[:, 0:tc],
                op=ALU.add), waits=[t_br, ta])
            tcv = em.dve(lambda v, mo=mo, rb=rb: v.tensor_tensor(
                out=erot[:, rb, :], in0=erot[:, rb, :], in1=r2c[:, mo, :],
                op=ALU.add), waits=[tb, t_rl])
            U[("oE", mo)] = [ta, t_m1, t_m2]
            t_st = em.dma("sp", R3[mo * 128:(mo + 1) * 128, t0:t0 + tc],
                          erot[:, rb, :], waits=[tcv])
            U[("ero", rb)] = t_st
            t_last = t_st
        U["psr2"] = ta
        U["psr3"] = tb
        U["avc"] = t_mm
        U["r2c"] = tcv
        t_ret = [t_last]
    stC.close()
    stS.close()
    return t_ret


def host_prep_stage3(z, core, seq=2048):
    mTd = np.zeros((128, 128), np.float32)
    for p in range(128):
        mTd[p, :p] = -1e30
    mUp = np.ascontiguousarray(mTd.T)
    seg2c = np.zeros((128, 8), np.float32)
    seg2c[:64, 0] = 1; seg2c[64:, 1] = 1
    kmask = np.zeros((128, 8), np.float32)
    kmask[0:32, 0] = 1; kmask[64:96, 1] = 1
    Wk = z["L1_attn_Wk"].astype(np.float32)
    WkP = np.zeros((128, 1024), np.float32)
    for h in range(16):
        WkP[:, h * 64:h * 64 + 32] = Wk[:, h * 32:(h + 1) * 32]
    kgp = np.zeros(128, np.float32)
    kgp[0:32] = z["L1_attn_k_g"].astype(np.float32)
    kgp[64:96] = z["L1_attn_k_g"].astype(np.float32)
    Wo = z["L1_attn_Wo"].astype(np.float32)
    WoS = Wo.reshape(16, 64, 1024)[:, :32, :].reshape(512, 1024)
    return {
        "ident": np.eye(128, dtype=np.float32),
        "seg2c": seg2c, "kmask": kmask,
        "mTd": mTd, "mUp": mUp,
        "qg": np.ascontiguousarray(np.tile(np.tile(z["L1_attn_q_g"].astype(np.float32), 2)[:, None], (1, 8))),
        "kg": np.ascontiguousarray(np.tile(kgp[:, None], (1, 8))),
        "Wq": np.ascontiguousarray(z["L1_attn_Wq"].astype(np.float32)),
        "Wkv": np.ascontiguousarray(z["L1_attn_Wkv"].astype(np.float32)),
        "WkP": np.ascontiguousarray(WkP),
        "Wv": np.ascontiguousarray(z["L1_attn_Wv"].astype(np.float32)),
        "WoS": np.ascontiguousarray(WoS),
    }


def build_stage3_program(seq=2048, tc=512, upto=None):
    nc = bass.Bass("TRN2", target_bir_lowering=False)
    ntok = 2 * seq
    io = {}
    io["R2"] = nc.dram_tensor("R2", [DM, ntok], F32, kind="ExternalInput").ap()
    for name, shape in [
        ("ident", [128, 128]), ("seg2c", [128, 8]),
        ("kmask", [128, 8]), ("mTd", [128, 128]), ("mUp", [128, 128]),
        ("qg", [128, 8]), ("kg", [128, 8]),
        ("Wq", [1024, 1024]), ("Wkv", [1024, 128]), ("WkP", [128, 1024]),
        ("Wv", [128, 512]), ("WoS", [512, 1024]),
    ]:
        io[name] = nc.dram_tensor(name, shape, F32, kind="ExternalInput").ap()
    io["R3"] = nc.dram_tensor("R3", [DM, ntok], F32, kind="ExternalOutput").ap()
    em = Em(nc)
    build_stage3(nc, em, io, seq=seq, tc=tc, upto=upto)
    return nc, em




def build_stage4(nc, em, io, seq=2048, tc=512):
    stS = contextlib.ExitStack()

    def A(name, shape, dt):
        return stS.enter_context(nc.sbuf_tensor(name, shape, dt))
    ntok = 2 * seq
    nch = ntok // tc

    R3 = io["R3"]
    OUT = io["OUT"]

    xbD = nc.dram_tensor("s4_xb", [DM, ntok], BF16, kind="Internal").ap()
    combD = nc.dram_tensor("s4_comb", [8, ntok], F32, kind="Internal").ap()
    accD = nc.dram_tensor("s4_acc", [DM, ntok], F32, kind="Internal").ap()

    PSALL = stS.enter_context(nc.psum_tensor("s4_psall", [128, 4, 512], F32)).ap()
    P0 = PSALL[:, 0, :]
    P1 = PSALL[:, 1, :]
    P2 = PSALL[:, 2, :]
    P3 = PSALL[:, 3, :]
    PA = stS.enter_context(nc.psum_tensor("s4_pa", [128, 512], F32)).ap()
    PB = stS.enter_context(nc.psum_tensor("s4_pb", [128, 512], F32)).ap()
    P6 = stS.enter_context(nc.psum_tensor("s4_p6", [128, 512], F32)).ap()
    P7 = stS.enter_context(nc.psum_tensor("s4_p7", [128, 512], F32)).ap()

    ones_col = A("s4_onesc", [128, 1], F32).ap()
    ones_row = A("s4_onesr", [1, 128], F32).ap()
    ones_bf = A("s4_onesb", [1, 512], BF16).ap()
    identf = A("s4_identf", [128, 128], F32).ap()
    iota8 = A("s4_iota8", [128, 8], F32).ap()
    rbt = A("s4_rbt", [128, 8], F32).ap()
    rwT = A("s4_rwT", [128, 8, 8], F32).ap()
    b1sb = A("s4_b1", [128, 9, 32], F32).ap()
    b2sp = A("s4_b2sp", [8, 1024], BF16).ap()
    sb2 = A("s4_sb2", [1, 1024], BF16).ap()

    t_oc = em.pool(lambda g: g.memset(ones_col, 1.0))
    t_or = em.pool(lambda g: g.memset(ones_row, 1.0))
    t_ob = em.pool(lambda g: g.memset(ones_bf, 1.0))
    t_cs = [t_oc, t_or, t_ob]
    t_cs.append(em.dma("sp", identf, io["ident"]))
    t_id = t_cs[-1]
    t_cs.append(em.dma("sp", iota8, io["iota8"]))
    t_io8 = t_cs[-1]
    t_cs.append(em.dma("sp", rbt, io["rb_tiled"]))
    t_rbt = t_cs[-1]
    t_cs.append(em.dma("sp", rwT, io["rwT"].rearrange("(kc p) m -> p kc m", p=128)))
    t_rw = t_cs[-1]
    t_cs.append(em.dma("sp", b1sb.rearrange("p e mc -> p (e mc)"), io["b1all"]))
    t_b1 = t_cs[-1]
    t_cs.append(em.dma("pool", b2sp, io["spec_b2"]))
    t_b2 = t_cs[-1]
    t_cs.append(em.dma("pool", sb2, io["shared_b2"]))
    t_sb2 = t_cs[-1]

    U = {}

    def tu(*k):
        return last(*[U.get(x) for x in k])

    st0 = contextlib.ExitStack()
    E0 = st0.enter_context
    rch = E0(nc.sbuf_tensor("s4_rch", [128, 8, tc], F32)).ap()
    sq = E0(nc.sbuf_tensor("s4_sq", [128, tc], F32)).ap()
    xnf = E0(nc.sbuf_tensor("s4_xnf", [128, 8, tc], F32)).ap()
    xnb = E0(nc.sbuf_tensor("s4_xnb", [128, 8, tc], BF16)).ap()
    rows = E0(nc.sbuf_tensor("s4_rows", [1, 4, tc], F32)).ap()
    lgsb = E0(nc.sbuf_tensor("s4_lgsb", [8, tc], F32)).ap()
    ltm = E0(nc.sbuf_tensor("s4_ltm", [128, 10, 8], F32)).ap()
    idxf = E0(nc.sbuf_tensor("s4_idxf", [128, 8], mybir.dt.uint32)).ap()
    m8 = E0(nc.sbuf_tensor("s4_m8", [128, 8], F32)).ap()
    rr2 = E0(nc.sbuf_tensor("s4_rr2", [128, 4], F32)).ap()
    combtm = E0(nc.sbuf_tensor("s4_combtm", [128, 8], F32)).ap()
    combf = E0(nc.sbuf_tensor("s4_combf", [8, tc], F32)).ap()

    t_ln_done = []
    for ch in range(nch):
        t0 = ch * tc
        t_r = em.dma("sp", rch,
                     R3.rearrange("(dc p) T -> p dc T", p=128)[:, :, t0:t0 + tc],
                     waits=tu("rch"))
        t_m1 = None
        t_m2 = None
        for dc in range(8):
            t_m1 = em.pe(lambda t, dc=dc: t.matmul(
                P6[0:1, 0:tc], ones_col, rch[:, dc, :],
                start=(dc == 0), stop=(dc == 7)),
                waits=last(t_r, t_oc) + (tu("ps67") if dc == 0 else []))
            t_sq = em.act(lambda s, dc=dc: s.activation(
                out=sq, in_=rch[:, dc, :], func=AF.Square),
                waits=[t_r] + tu("sq"))
            t_m2 = em.pe(lambda t, dc=dc: t.matmul(
                P7[0:1, 0:tc], ones_col, sq,
                start=(dc == 0), stop=(dc == 7)),
                waits=[t_sq] + (tu("ps67") if dc == 0 else []))
            U["sq"] = t_m2
        mu = rows[:, 0, :]
        vv = rows[:, 1, :]
        rrw = rows[:, 2, :]
        bb = rows[:, 3, :]
        t_mu = em.dve(lambda v, mu=mu: v.tensor_scalar_mul(
            mu, P6[0:1, 0:tc], 1.0 / DM), waits=[t_m1] + tu("rows"))
        t_q = em.act(lambda s, mu=mu, vv=vv: s.activation(
            out=vv, in_=mu, func=AF.Square), waits=[t_mu])
        t_v = em.dve(lambda v, vv=vv: v.scalar_tensor_tensor(
            out=vv, in0=P7[0:1, 0:tc], scalar=1.0 / DM, in1=vv,
            op0=ALU.mult, op1=ALU.subtract), waits=[t_m2, t_q])
        t_1 = em.dve(lambda v, vv=vv, rrw=rrw: v.tensor_scalar_add(rrw, vv, 1e-5),
                     waits=[t_v])
        t_2 = em.act(lambda s, rrw=rrw: s.activation(
            out=rrw, in_=rrw, func=AF.Sqrt), waits=[t_1])
        t_3 = em.dve(lambda v, rrw=rrw: v.reciprocal(rrw, rrw), waits=[t_2])
        t_4 = em.dve(lambda v, mu=mu, rrw=rrw, bb=bb: v.tensor_tensor(
            out=bb, in0=mu, in1=rrw, op=ALU.mult), waits=[t_3])
        t_5 = em.dve(lambda v, bb=bb: v.tensor_scalar_mul(bb, bb, -1.0), waits=[t_4])
        U["ps67"] = [t_v, t_mu]
        U["rows"] = t_5
        t_sr = em.pe(lambda t, rrw=rrw: t.matmul(P2[:, 0:tc], ones_row, rrw,
                                                 start=True, stop=True),
                     waits=last(t_3, t_or) + tu("ps2"))
        t_br = em.pe(lambda t, bb=bb: t.matmul(P3[:, 0:tc], ones_row, bb,
                                               start=True, stop=True),
                     waits=last(t_5, t_or) + tu("ps3"))
        t_xns = []
        for dc in range(8):
            ta = em.dve(lambda v, dc=dc: v.tensor_tensor(
                out=xnf[:, dc, :], in0=rch[:, dc, :], in1=P2[:, 0:tc],
                op=ALU.mult), waits=[t_sr] + tu("xnf"))
            tb = em.dve(lambda v, dc=dc: v.tensor_tensor(
                out=xnf[:, dc, :], in0=xnf[:, dc, :], in1=P3[:, 0:tc],
                op=ALU.add), waits=[t_br, ta])
            tcv = em.dve(lambda v, dc=dc: v.tensor_copy(
                xnb[:, dc, :], xnf[:, dc, :]), waits=[tb] + tu("xnb"))
            t_xns.append(tcv)
        U["ps2"] = t_xns[-1]
        U["ps3"] = t_xns[-1]
        U["rch"] = [t_m1, t_xns[-1]]
        t_xd = em.dma("sp", xbD.rearrange("(dc p) T -> p dc T", p=128)[:, :, t0:t0 + tc],
                      xnb, waits=t_xns + tu("xbd"))
        U["xnb"] = t_xd
        U["xbd"] = t_xd
        t_ln_done = [t_xd]
        t_lg = None
        for kc in range(8):
            t_lg = em.pe(lambda t, kc=kc: t.matmul(
                P6[0:8, 0:tc], rwT[:, kc, :], xnf[:, kc, :],
                start=(kc == 0), stop=(kc == 7)),
                waits=last(t_xns[kc], t_rw) + (tu("ps67") if kc == 0 else []))
        t_le = em.dve(lambda v: v.tensor_copy(lgsb, P6[0:8, 0:tc]),
                      waits=[t_lg] + tu("lgsb"))
        U["ps67"] = t_le
        U["xnf"] = t_lg
        t_cd = None
        for qq in range(tc // 128):
            t_t = em.pe(lambda t, qq=qq: t.matmul(
                P0[:, 0:8], lgsb[0:8, qq * 128:(qq + 1) * 128], identf[0:8, 0:8],
                start=True, stop=True),
                waits=[t_le, t_id] + tu("ps0r"))
            t_lt = em.dve(lambda v, qq=qq: v.tensor_copy(
                ltm[:, qq, :], P0[:, 0:8]), waits=[t_t] + tu(("ltm", qq)))
            U["ps0r"] = t_lt
            t_ab = em.dve(lambda v, qq=qq: v.tensor_tensor(
                out=ltm[:, qq, :], in0=ltm[:, qq, :], in1=rbt, op=ALU.add),
                waits=[t_lt, t_rbt])
            t_mx = em.dve(lambda v, qq=qq: v.reduce_max(
                out=rr2[:, 0:1], in_=ltm[:, qq, :], axis=AX.X, negate=True),
                waits=[t_ab] + tu("rr2"))
            t_ex = em.act(lambda s, qq=qq: s.activation(
                out=ltm[:, qq, :], in_=ltm[:, qq, :], func=AF.Exp,
                bias=rr2[:, 0:1], accum_out=rr2[:, 1:2]), waits=[t_mx])
            t_rc = em.dve(lambda v: v.reciprocal(rr2[:, 1:2], rr2[:, 1:2]),
                          waits=[t_ex])
            t_sm = em.dve(lambda v, qq=qq: v.tensor_scalar(
                out=ltm[:, qq, :], in0=ltm[:, qq, :], scalar1=rr2[:, 1:2],
                scalar2=None, op0=ALU.mult), waits=[t_rc])
            t_m8 = em.dve(lambda v, qq=qq: v.max(m8, ltm[:, qq, :]),
                          waits=[t_sm] + tu("m8"))
            t_ix = em.dve(lambda v, qq=qq: v.max_index(idxf, m8, ltm[:, qq, :]),
                          waits=[t_m8] + tu("idxf"))
            t_s1 = em.dve(lambda v: v.tensor_tensor(
                out=rr2[:, 2:3], in0=m8[:, 0:1], in1=m8[:, 1:2], op=ALU.add),
                waits=[t_m8])
            t_s2 = em.dve(lambda v: v.reciprocal(rr2[:, 2:3], rr2[:, 2:3]),
                          waits=[t_s1])
            t_w1 = em.dve(lambda v: v.tensor_tensor(
                out=rr2[:, 0:1], in0=m8[:, 0:1], in1=rr2[:, 2:3], op=ALU.mult),
                waits=[t_s2])
            t_w2 = em.dve(lambda v: v.tensor_tensor(
                out=rr2[:, 1:2], in0=m8[:, 1:2], in1=rr2[:, 2:3], op=ALU.mult),
                waits=[t_s2, t_w1])
            t_if = em.dve(lambda v: v.tensor_copy(m8[:, 2:4], idxf[:, 0:2]),
                          waits=[t_ix, t_w2])
            U["idxf"] = t_if
            t_c1 = em.dve(lambda v, qq=qq: v.tensor_scalar(
                out=combtm, in0=iota8, scalar1=m8[:, 2:3], scalar2=rr2[:, 0:1],
                op0=ALU.is_equal, op1=ALU.mult), waits=[t_if, t_io8] + tu("combtm"))
            t_c2 = em.dve(lambda v, qq=qq: v.tensor_scalar(
                out=ltm[:, 8, :], in0=iota8, scalar1=m8[:, 3:4],
                scalar2=rr2[:, 1:2], op0=ALU.is_equal, op1=ALU.mult),
                waits=[t_if, t_c1] + tu("ltm8"))
            t_c3 = em.dve(lambda v, qq=qq: v.tensor_tensor(
                out=combtm, in0=combtm, in1=ltm[:, 8, :], op=ALU.add),
                waits=[t_c2])
            U["ltm8"] = t_c3
            U["m8"] = t_c3
            U["rr2"] = t_c3
            t_tb = em.pe(lambda t, qq=qq: t.matmul(
                P1[0:8, 0:128], combtm, identf, start=True, stop=True),
                waits=[t_c3, t_id] + tu("ps1r"))
            t_ce = em.dve(lambda v, qq=qq: v.tensor_copy(
                combf[0:8, qq * 128:(qq + 1) * 128], P1[0:8, 0:128]),
                waits=[t_tb] + tu("combf"))
            U["ps1r"] = t_ce
            U[("ltm", qq)] = t_c1
            U["combtm"] = t_tb
            t_cd = t_ce
        t_cD = em.dma("sp", combD[:, t0:t0 + tc], combf[0:8, 0:tc], waits=[t_cd])
        U["combf"] = t_cD
        U["lgsb"] = t_lg if False else t_ab
        t_ln_done.append(t_cD)
    st0.close()
    F1 = em.frontier()

    stB = contextlib.ExitStack()
    E = stB.enter_context
    w1sb = E(nc.sbuf_tensor("s4_w1", [128, 8, 4096], BF16)).ap()
    w2sb = E(nc.sbuf_tensor("s4_w2", [128, 32, 1024], BF16)).ap()
    xbc = E(nc.sbuf_tensor("s4_xbc", [128, 8, tc], BF16)).ap()
    Hsb = E(nc.sbuf_tensor("s4_H", [128, 32, tc], BF16)).ap()
    crow = E(nc.sbuf_tensor("s4_crow", [1, tc], F32)).ap()
    cb8 = E(nc.sbuf_tensor("s4_cb8", [8, tc], BF16)).ap()
    crep = E(nc.sbuf_tensor("s4_crep", [128, tc], F32)).ap()
    otile = E(nc.sbuf_tensor("s4_ot", [128, 2, tc], F32)).ap()

    prevA = t_ln_done
    t_4a = t_ln_done
    acc_fin = {}
    for ei in range(9):
        e = ei - 1
        if ei == 0:
            t_w1 = em.dma("pool", w1sb,
                          io["sw1T"].rearrange("(kc p) m -> p kc m", p=128),
                          waits=F1 + tu("w1sb"))
            t_w2 = em.dma("pool", w2sb,
                          io["sw2T"].rearrange("(kc p) m -> p kc m", p=128),
                          waits=tu("w2sb"))
        else:
            t_w1 = em.dma("pool", w1sb,
                          io["sp1T"][e].rearrange("(kc p) m -> p kc m", p=128),
                          waits=tu("w1sb"))
            t_w2 = em.dma("pool", w2sb,
                          io["sp2T"][e].rearrange("(kc p) m -> p kc m", p=128),
                          waits=tu("w2sb"))
        for ch in range(nch):
            t0 = ch * tc
            t_x = em.dma("sp", xbc,
                         xbD.rearrange("(dc p) T -> p dc T", p=128)[:, :, t0:t0 + tc],
                         waits=(F1 if ei == 0 and ch == 0 else []) + tu("xbc"))
            if ei > 0:
                t_cr = em.dma("sp", crow, combD[e:e + 1, t0:t0 + tc],
                              waits=(t_4a if ei == 1 and ch == 0 else []) + tu("crow"))
                t_rp = em.pe(lambda t: t.matmul(P3[:, 0:tc], ones_row, crow,
                                                start=True, stop=True),
                             waits=[t_cr, t_or] + tu("ps3b"))
                t_cre = em.act(lambda s: s.activation(
                    out=crep, in_=P3[:, 0:tc], func=AF.Copy),
                    waits=[t_rp] + tu("crep"))
                U["ps3b"] = t_cre
            if ei == 0:
                t_c8 = em.dma("pool", cb8, combD[:, t0:t0 + tc],
                              waits=prevA + tu("cb8"))
            t_hs = []
            for m in range(32):
                bank = P6 if m % 2 == 0 else P7
                t_mm = None
                for kc in range(8):
                    t_mm = em.pe(lambda t, m=m, kc=kc, bank=bank: t.matmul(
                        bank[:, 0:tc], w1sb[:, kc, m * 128:(m + 1) * 128],
                        xbc[:, kc, :], start=(kc == 0), stop=(kc == 7)),
                        waits=last(t_x, t_w1) + (tu(("psh", m % 2)) if kc == 0 else [])
                        + (F1 if ei == 0 and ch == 0 and m == 0 and kc == 0 else []))
                t_g = em.act(lambda s, m=m, ei=ei, bank=bank: s.activation(
                    out=Hsb[:, m, :], in_=bank[:, 0:tc], func=AF.Gelu_apprx_tanh,
                    bias=b1sb[:, ei, m:m + 1]),
                    waits=last(t_mm, t_b1) + tu(("H", m))
                    + (F1 if ei == 0 and ch == 0 and m == 0 else []))
                U[("psh", m % 2)] = t_g
                if ei > 0:
                    t_sc = em.dve(lambda v, m=m: v.tensor_tensor(
                        out=Hsb[:, m, :], in0=Hsb[:, m, :], in1=crep,
                        op=ALU.mult), waits=[t_g, t_cre])
                    t_hs.append(t_sc)
                else:
                    t_hs.append(t_g)
            U["crep"] = t_hs[-1]
            for mo in range(8):
                ob = mo % 2
                bank = PA if ob == 0 else PB
                t_mo = None
                for kc in range(32):
                    t_mo = em.pe(lambda t, mo=mo, kc=kc, bank=bank, ei=ei: t.matmul(
                        bank[:, 0:tc], w2sb[:, kc, mo * 128:(mo + 1) * 128],
                        Hsb[:, kc, :], start=(kc == 0),
                        stop=(ei > 0 and kc == 31)),
                        waits=last(t_hs[kc], t_w2) + (tu(("pso", ob)) if kc == 0 else []))
                if ei == 0:
                    t_mo = em.pe(lambda t, mo=mo, bank=bank: t.matmul(
                        bank[:, 0:tc], b2sp[:, mo * 128:(mo + 1) * 128],
                        cb8, start=False, stop=False),
                        waits=last(t_c8, t_b2))
                    t_mo = em.pe(lambda t, mo=mo, bank=bank: t.matmul(
                        bank[:, 0:tc], sb2[0:1, mo * 128:(mo + 1) * 128],
                        ones_bf[0:1, 0:tc], start=False, stop=True),
                        waits=last(t_sb2, t_ob))
                t_oe = em.dve(lambda v, mo=mo, ob=ob, bank=bank: v.tensor_copy(
                    otile[:, ob, :], bank[:, 0:tc]),
                    waits=[t_mo] + tu(("ot", ob))
                    + (F1 if ei == 0 and ch == 0 and mo == 0 else []))
                U[("pso", ob)] = t_oe
                t_ad = em.dma("pool", accD[mo * 128:(mo + 1) * 128, t0:t0 + tc],
                              otile[:, ob, :], waits=[t_oe] + tu(("accw", mo, ch)),
                              accum_op=(ALU.bypass if ei == 0 else ALU.add))
                U[("ot", ob)] = t_ad
                U[("accw", mo, ch)] = t_ad
                acc_fin.setdefault(ch, []).append(t_ad)
                if len(acc_fin[ch]) > 8:
                    acc_fin[ch] = acc_fin[ch][-8:]
            U["xbc"] = t_mo
            U["cb8"] = t_mo
            U["crow"] = t_rp if ei > 0 else None
            for m in range(32):
                U[("H", m)] = t_mo
        U["w1sb"] = t_mo
        U["w2sb"] = t_mo
        prevA = []
    stB.close()
    F2 = em.frontier()

    stC = contextlib.ExitStack()
    E = stC.enter_context
    acc = E(nc.sbuf_tensor("s4_accc", [128, 8, tc], F32)).ap()
    r3c = E(nc.sbuf_tensor("s4_r3c", [128, 8, tc], F32)).ap()
    sqc = E(nc.sbuf_tensor("s4_sqc", [128, tc], F32)).ap()
    rowc = E(nc.sbuf_tensor("s4_rowc", [1, 4, tc], F32)).ap()
    erot = E(nc.sbuf_tensor("s4_erot", [128, 8, tc], F32)).ap()
    trE = E(nc.sbuf_tensor("s4_trE", [128, 2, 128], F32)).ap()
    OUTf = OUT.rearrange("b t d -> (b t) d")
    t_ret = []
    for ch in range(nch):
        t0 = ch * tc
        t_al = em.dma("sp", acc,
                      accD.rearrange("(dc p) T -> p dc T", p=128)[:, :, t0:t0 + tc],
                      waits=(F2 if ch == 0 else []) + tu("accc"))
        t_rl = em.dma("sp", r3c,
                      R3.rearrange("(dc p) T -> p dc T", p=128)[:, :, t0:t0 + tc],
                      waits=tu("r3c"))
        t_m1 = None
        t_m2 = None
        for dc in range(8):
            t_m1 = em.pe(lambda t, dc=dc: t.matmul(
                P6[0:1, 0:tc], ones_col, acc[:, dc, :],
                start=(dc == 0), stop=(dc == 7)),
                waits=last(t_al, t_oc) + (tu("psr67") if dc == 0 else [])
                + (F2 if ch == 0 and dc == 0 else []))
            t_sq = em.act(lambda s, dc=dc: s.activation(
                out=sqc, in_=acc[:, dc, :], func=AF.Square),
                waits=[t_al] + tu("sqc")
                + (F2 if ch == 0 and dc == 0 else []))
            t_m2 = em.pe(lambda t, dc=dc: t.matmul(
                P7[0:1, 0:tc], ones_col, sqc,
                start=(dc == 0), stop=(dc == 7)),
                waits=[t_sq] + (tu("psr67") if dc == 0 else []))
            U["sqc"] = t_m2
        mu = rowc[:, 0, :]
        vv = rowc[:, 1, :]
        rrw = rowc[:, 2, :]
        bb = rowc[:, 3, :]
        t_mu = em.dve(lambda v, mu=mu: v.tensor_scalar_mul(
            mu, P6[0:1, 0:tc], 1.0 / DM),
            waits=[t_m1] + tu("rowc") + (F2 if ch == 0 else []))
        t_q = em.act(lambda s, mu=mu, vv=vv: s.activation(
            out=vv, in_=mu, func=AF.Square), waits=[t_mu])
        t_v = em.dve(lambda v, vv=vv: v.scalar_tensor_tensor(
            out=vv, in0=P7[0:1, 0:tc], scalar=1.0 / DM, in1=vv,
            op0=ALU.mult, op1=ALU.subtract), waits=[t_m2, t_q])
        t_1 = em.dve(lambda v, vv=vv, rrw=rrw: v.tensor_scalar_add(rrw, vv, 1e-5),
                     waits=[t_v])
        t_2 = em.act(lambda s, rrw=rrw: s.activation(
            out=rrw, in_=rrw, func=AF.Sqrt), waits=[t_1])
        t_3 = em.dve(lambda v, rrw=rrw: v.reciprocal(rrw, rrw), waits=[t_2])
        t_4 = em.dve(lambda v, mu=mu, rrw=rrw, bb=bb: v.tensor_tensor(
            out=bb, in0=mu, in1=rrw, op=ALU.mult), waits=[t_3])
        t_5 = em.dve(lambda v, bb=bb: v.tensor_scalar_mul(bb, bb, -1.0), waits=[t_4])
        U["psr67"] = [t_v, t_mu]
        U["rowc"] = t_5
        t_sr = em.pe(lambda t, rrw=rrw: t.matmul(P2[:, 0:tc], ones_row, rrw,
                                                 start=True, stop=True),
                     waits=last(t_3, t_or) + tu("psr2"))
        t_br = em.pe(lambda t, bb=bb: t.matmul(P3[:, 0:tc], ones_row, bb,
                                               start=True, stop=True),
                     waits=last(t_5, t_or) + tu("psr3"))
        t_es = []
        ta = None
        tb = None
        for dc in range(8):
            ta = em.dve(lambda v, dc=dc: v.tensor_tensor(
                out=erot[:, dc, :], in0=acc[:, dc, :], in1=P2[:, 0:tc],
                op=ALU.mult), waits=[t_sr] + tu(("ero", dc)))
            tb = em.dve(lambda v, dc=dc: v.tensor_tensor(
                out=erot[:, dc, :], in0=erot[:, dc, :], in1=P3[:, 0:tc],
                op=ALU.add), waits=[t_br, ta])
            tcv = em.dve(lambda v, dc=dc: v.tensor_tensor(
                out=erot[:, dc, :], in0=erot[:, dc, :], in1=r3c[:, dc, :],
                op=ALU.add), waits=[tb, t_rl])
            t_es.append(tcv)
        U["psr2"] = ta
        U["psr3"] = tb
        U["accc"] = [t_m1, t_es[-1]]
        U["r3c"] = t_es[-1]
        for tt in range(tc // 128):
            for dc in range(8):
                bank = P0 if dc % 2 == 0 else P1
                t_tr = em.pe(lambda t, tt=tt, dc=dc, bank=bank: t.matmul(
                    bank[:, 0:128], erot[:, dc, tt * 128:(tt + 1) * 128], identf,
                    start=True, stop=True),
                    waits=[t_es[dc], t_id] + tu(("psf", dc % 2)))
                t_te = em.dve(lambda v, dc=dc, bank=bank: v.tensor_copy(
                    trE[:, dc % 2, :], bank[:, 0:128]),
                    waits=[t_tr] + tu(("trE", dc % 2)))
                U[("psf", dc % 2)] = t_te
                t_td = em.dma("sp", OUTf[t0 + tt * 128:t0 + (tt + 1) * 128,
                                         dc * 128:(dc + 1) * 128],
                              trE[:, dc % 2, :], waits=[t_te])
                U[("trE", dc % 2)] = t_td
                t_ret.append(t_td)
        for dc in range(8):
            U[("ero", dc)] = t_ret[-1]
        t_ret = t_ret[-2:]
    stC.close()
    stS.close()
    return t_ret


def host_prep_stage4(z, core, seq=2048):
    b1all = np.concatenate([z["L1_ffn_shared_b1"][None].astype(np.float32),
                            z["L1_ffn_spec_b1"].astype(np.float32)], axis=0)
    b1all = np.ascontiguousarray(b1all.reshape(9, 32, 128).transpose(2, 0, 1).reshape(128, 288))
    return {
        "ident": np.eye(128, dtype=np.float32),
        "iota8": np.ascontiguousarray(
            np.tile(np.arange(8, dtype=np.float32)[None, :], (128, 1))),
        "rb_tiled": np.ascontiguousarray(
            np.tile(z["L1_ffn_router_b"].astype(np.float32)[None, :], (128, 1))),
        "rwT": np.ascontiguousarray(z["L1_ffn_router_w"].astype(np.float32).T),
        "b1all": b1all,
        "spec_b2": np.ascontiguousarray(z["L1_ffn_spec_b2"].astype(np.float32)),
        "shared_b2": np.ascontiguousarray(z["L1_ffn_shared_b2"].astype(np.float32)[None, :]),
        "sw1T": np.ascontiguousarray(z["L1_ffn_shared_w1"].astype(np.float32).T),
        "sw2T": np.ascontiguousarray(z["L1_ffn_shared_w2"].astype(np.float32).T),
        "sp1T": np.ascontiguousarray(z["L1_ffn_spec_w1"].astype(np.float32).transpose(0, 2, 1)),
        "sp2T": np.ascontiguousarray(z["L1_ffn_spec_w2"].astype(np.float32).transpose(0, 2, 1)),
    }


def build_stage4_program(seq=2048, tc=512):
    nc = bass.Bass("TRN2", target_bir_lowering=False)
    ntok = 2 * seq
    io = {}
    io["R3"] = nc.dram_tensor("R3", [DM, ntok], F32, kind="ExternalInput").ap()
    for name, shape in [
        ("ident", [128, 128]), ("iota8", [128, 8]), ("rb_tiled", [128, 8]),
        ("rwT", [1024, 8]), ("b1all", [128, 288]), ("spec_b2", [8, 1024]),
        ("shared_b2", [1, 1024]), ("sw1T", [1024, 4096]), ("sw2T", [4096, 1024]),
        ("sp1T", [8, 1024, 4096]), ("sp2T", [8, 4096, 1024]),
    ]:
        io[name] = nc.dram_tensor(name, shape, F32, kind="ExternalInput").ap()
    io["OUT"] = nc.dram_tensor("OUT", [2, seq, 1024], F32, kind="ExternalOutput").ap()
    em = Em(nc)
    build_stage4(nc, em, io, seq=seq, tc=tc)
    return nc, em


def host_prep_stage1(z, core, seq=2048):
    def pcol(v, c):
        return np.ascontiguousarray(np.asarray(v, np.float32).reshape(c, 128).T)

    def pcol2(v, c, w):
        return np.ascontiguousarray(
            np.asarray(v, np.float32).reshape(c, 128, w).transpose(1, 0, 2))

    x = z["L0_attn_x"][2 * core:2 * core + 2, :seq]
    return {
        "x": np.ascontiguousarray(x),
        "ident": np.eye(128, dtype=np.float32),
        "ln_w": pcol(z["L0_attn_ln_w"], 8),
        "ln_b": pcol(z["L0_attn_ln_b"], 8),
        "conv_w": pcol2(z["L0_attn_conv_w"], 16, 4),
        "conv_b": pcol(z["L0_attn_conv_b"], 16),
        "dt_proj_b": pcol(z["L0_attn_dt_proj_b"], 16),
        "D_param": pcol(z["L0_attn_D_param"], 16),
        "A_log": pcol2(z["L0_attn_A_log"], 16, 16),
        "in_proj_wT": np.ascontiguousarray(z["L0_attn_in_proj_w"].T),
        "x_proj_wT": np.ascontiguousarray(z["L0_attn_x_proj_w"].T),
        "dt_proj_wT": np.ascontiguousarray(z["L0_attn_dt_proj_w"].T),
        "out_proj_wT": np.ascontiguousarray(z["L0_attn_out_proj_w"].T),
    }


def build_stage1_program(seq=2048, tc=256):
    nc = bass.Bass("TRN2", target_bir_lowering=False)
    ntok = 2 * seq
    io = {}
    io["x"] = nc.dram_tensor("x", [2, seq, 1024], F32, kind="ExternalInput").ap()
    io["ident"] = nc.dram_tensor("ident", [128, 128], F32, kind="ExternalInput").ap()
    for name, shape in [
        ("ln_w", [128, 8]), ("ln_b", [128, 8]), ("conv_w", [128, 16, 4]),
        ("conv_b", [128, 16]), ("dt_proj_b", [128, 16]), ("D_param", [128, 16]),
        ("A_log", [128, 16, 16]), ("in_proj_wT", [1024, 4096]),
        ("x_proj_wT", [2048, 96]), ("dt_proj_wT", [64, 2048]),
        ("out_proj_wT", [2048, 1024]),
    ]:
        io[name] = nc.dram_tensor(name, shape, F32, kind="ExternalInput").ap()
    io["R1"] = nc.dram_tensor("R1", [DM, ntok], F32, kind="ExternalOutput").ap()
    em = Em(nc)
    build_stage1(nc, em, io, seq=seq, tc=tc)
    return nc, em


def build_fused_program(seq=2048):
    nc = bass.Bass("TRN2", target_bir_lowering=False)
    ntok = 2 * seq
    io = {}
    io["x"] = nc.dram_tensor("x", [2, seq, 1024], F32, kind="ExternalInput").ap()
    names = [
        ("ident", [128, 128]),
        ("ln_w", [128, 8]), ("ln_b", [128, 8]), ("conv_w", [128, 16, 4]),
        ("conv_b", [128, 16]), ("dt_proj_b", [128, 16]), ("D_param", [128, 16]),
        ("A_log", [128, 16, 16]), ("in_proj_wT", [1024, 4096]),
        ("x_proj_wT", [2048, 96]), ("dt_proj_wT", [64, 2048]),
        ("out_proj_wT", [2048, 1024]),
        ("scm", [128, 8, 2, 256]), ("Mg", [128, 16, 128]),
        ("sel16T", [16, 128]), ("sel128", [128, 16]),
        ("nwT4", [1024, 4, 255]), ("nb4", [1, 4, 255]),
        ("b14", [4, 256, 16]), ("b24", [4, 256, 1024]),
        ("q_w1T", [1024, 4096]), ("q_w2T", [4096, 1024]),
        ("k_w1T", [1024, 4096]), ("k_w2T", [4096, 1024]),
        ("v_w1T", [1024, 4096]), ("v_w2T", [4096, 1024]),
        ("f_w1T", [1024, 4096]), ("f_w2T", [4096, 1024]),
        ("seg2c", [128, 8]), ("kmask", [128, 8]),
        ("mTd", [128, 128]), ("mUp", [128, 128]),
        ("qg", [128, 8]), ("kg", [128, 8]),
        ("Wq", [1024, 1024]), ("Wkv", [1024, 128]), ("WkP", [128, 1024]),
        ("Wv", [128, 512]), ("WoS", [512, 1024]),
        ("iota8", [128, 8]), ("rb_tiled", [128, 8]),
        ("rwT", [1024, 8]), ("b1all", [128, 288]), ("spec_b2", [8, 1024]),
        ("shared_b2", [1, 1024]), ("sw1T", [1024, 4096]), ("sw2T", [4096, 1024]),
        ("sp1T", [8, 1024, 4096]), ("sp2T", [8, 4096, 1024]),
    ]
    for name, shape in names:
        io[name] = nc.dram_tensor(name, shape, F32, kind="ExternalInput").ap()
    io["R1"] = nc.dram_tensor("R1", [DM, ntok], F32, kind="Internal").ap()
    io["R2"] = nc.dram_tensor("R2", [DM, ntok], F32, kind="Internal").ap()
    io["R3"] = nc.dram_tensor("R3", [DM, ntok], F32, kind="Internal").ap()
    io["OUT"] = nc.dram_tensor("OUT", [2, seq, 1024], F32, kind="ExternalOutput").ap()
    em = Em(nc)
    build_stage1(nc, em, io, seq=seq, tc=256)
    em.set_phase(em.frontier())
    build_stage2(nc, em, io, seq=seq, tc=512)
    em.set_phase(em.frontier())
    build_stage3(nc, em, io, seq=seq, tc=512)
    em.set_phase(em.frontier())
    build_stage4(nc, em, io, seq=seq, tc=512)
    return nc, em


def host_prep_fused(z, core, seq=2048):
    d = {}
    for f in (host_prep_stage1, host_prep_stage2, host_prep_stage3,
              host_prep_stage4):
        for k, v in f(z, core, seq=seq).items():
            if k in d:
                assert d[k].shape == v.shape, k
            else:
                d[k] = v
    return d


def kernel(**inputs):
    ncores = 8
    seq = 2048
    z = inputs
    nc, em = build_fused_program(seq=seq)
    em.build()
    maps = [host_prep_fused(z, c, seq=seq) for c in range(ncores)]
    res = run_bass_kernel_spmd(nc, maps, core_ids=list(range(ncores)))
    out = np.concatenate(
        [np.asarray(res.results[c]["OUT"]) for c in range(ncores)], axis=0)
    return out.astype(np.float32, copy=False)


def kernel_unfused(**inputs):
    ncores = 8
    seq = 2048
    z = inputs

    nc1, em1 = build_stage1_program(seq=seq, tc=256)
    em1.build()
    maps1 = [host_prep_stage1(z, c, seq=seq) for c in range(ncores)]
    res1 = run_bass_kernel_spmd(nc1, maps1, core_ids=list(range(ncores)))

    nc2, em2 = build_stage2_program(seq=seq, tc=512)
    em2.build()
    maps2 = [host_prep_stage2(z, c, seq=seq) for c in range(ncores)]
    for c in range(ncores):
        maps2[c]["R1"] = np.ascontiguousarray(np.asarray(res1.results[c]["R1"]))
    res2 = run_bass_kernel_spmd(nc2, maps2, core_ids=list(range(ncores)))

    nc3, em3 = build_stage3_program(seq=seq, tc=512)
    em3.build()
    maps3 = [host_prep_stage3(z, c, seq=seq) for c in range(ncores)]
    for c in range(ncores):
        maps3[c]["R2"] = np.ascontiguousarray(np.asarray(res2.results[c]["R2"]))
    res3 = run_bass_kernel_spmd(nc3, maps3, core_ids=list(range(ncores)))

    nc4, em4 = build_stage4_program(seq=seq, tc=512)
    em4.build()
    maps4 = [host_prep_stage4(z, c, seq=seq) for c in range(ncores)]
    for c in range(ncores):
        maps4[c]["R3"] = np.ascontiguousarray(np.asarray(res3.results[c]["R3"]))
    res4 = run_bass_kernel_spmd(nc4, maps4, core_ids=list(range(ncores)))

    out = np.concatenate(
        [np.asarray(res4.results[c]["OUT"]) for c in range(ncores)], axis=0)
    return out.astype(np.float32, copy=False)


# version 07 (final)
# speedup vs baseline: 1.0829x; 1.0829x over previous
import contextlib
import numpy as np
import concourse.bass as bass
import concourse.mybir as mybir
from concourse.bass_utils import run_bass_kernel_spmd

F32 = mybir.dt.float32
BF16 = mybir.dt.bfloat16
AF = mybir.ActivationFunctionType
ALU = mybir.AluOpType
AX = mybir.AxisListType

ENGINES = ("pe", "dve", "act", "pool", "sp")
EM_NSLOT = {"sp": 12, "pool": 6, "act": 2, "dve": 1, "pe": 1}
DM = 1024


class Em:

    def __init__(self, nc):
        self.nc = nc
        self.recs = {e: [] for e in ENGINES}
        self.gid = 0
        self.phase = {}

    def set_phase(self, toks):
        self.phase = {e: list(toks) for e in ENGINES}

    def _tok(self, eng, fn, waits, dma=False, serialize=False):
        extra = self.phase.pop(eng, None)
        if extra:
            waits = list(waits or ()) + extra
        ws = []
        for w in waits or ():
            if w is None:
                continue
            assert w["gid"] < self.gid, "wait on later instruction (cycle)"
            if not w["dma"]:
                w["marked"] = True
            ws.append(w)
        rec = dict(eng=eng, fn=fn, waits=ws, dma=dma, marked=dma,
                   serialize=serialize, gid=self.gid)
        self.gid += 1
        self.recs[eng].append(rec)
        return rec

    def pe(self, fn, waits=()):
        return self._tok("pe", fn, waits)

    def dve(self, fn, waits=()):
        return self._tok("dve", fn, waits)

    def act(self, fn, waits=()):
        return self._tok("act", fn, waits)

    def pool(self, fn, waits=()):
        return self._tok("pool", fn, waits)

    def frontier(self):
        toks = []
        for e in ENGINES:
            rs = self.recs[e]
            dmas = [r for r in rs if r["dma"]]
            toks += dmas[-EM_NSLOT[e]:]
            comp = [r for r in rs if not r["dma"]]
            if comp:
                toks.append(comp[-1])
        return toks

    def dma(self, queue, out, in_, waits=(), serialize=False, **kw):
        if out.dtype != in_.dtype:
            assert queue == "pool", "casting DMA must use gpsimd (SWDGE)"

        def fn(eng, out=out, in_=in_, kw=kw):
            return eng.dma_start(out=out, in_=in_, **kw)

        return self._tok(queue, fn, waits, dma=True, serialize=serialize)

    def build(self, serial=False, max_gid=None):
        nc = self.nc
        if max_gid is not None:
            for e in ENGINES:
                self.recs[e] = [r for r in self.recs[e] if r["gid"] <= max_gid]
        if serial:
            prev = None
            for e in ENGINES:
                pass
            order = sorted((r for e in ENGINES for r in self.recs[e]),
                           key=lambda r: r["gid"])
            for r in order:
                if prev is not None and prev not in r["waits"]:
                    r["waits"] = list(r["waits"]) + [prev]
                    if not prev["dma"]:
                        prev["marked"] = True
                r["marked"] = True
                prev = r
        NSLOT = EM_NSLOT
        for e in ENGINES:
            cc = 0
            dseq = 0
            for r in self.recs[e]:
                if r["dma"]:
                    r["slot"] = dseq % NSLOT[e]
                    r["use"] = dseq // NSLOT[e] + 1
                    r["count"] = 16 * r["use"]
                    dseq += 1
                elif r["marked"]:
                    cc += 1
                    r["count"] = cc
                else:
                    r["count"] = None

        stack = contextlib.ExitStack()
        sems = {}
        for e in ENGINES:
            sems[(e, -1)] = stack.enter_context(nc.semaphore(f"qc_{e}"))
            for s in range(NSLOT[e]):
                sems[(e, s)] = stack.enter_context(nc.semaphore(f"qd_{e}{s}"))

        def replay(eng_obj, e):
            seen = {}
            for r in self.recs[e]:
                if r["dma"] and r["use"] > 1:
                    key = (e, r["slot"])
                    v = 16 * (r["use"] - 1)
                    if seen.get(key, -1) < v:
                        eng_obj.wait_ge(sems[key], v)
                        seen[key] = v
                for w in r["waits"]:
                    key = (w["eng"], w["slot"] if w["dma"] else -1)
                    wv = w["count"]
                    assert wv is not None, "wait on unmarked instruction"
                    if seen.get(key, -1) < wv:
                        eng_obj.wait_ge(sems[key], wv)
                        seen[key] = wv
                inst = r["fn"](eng_obj)
                if r["dma"]:
                    inst.then_inc(sems[(e, r["slot"])], 16)
                elif r["marked"]:
                    inst.then_inc(sems[(e, -1)], 1)
            finals = {}
            for r in self.recs[e]:
                if r["dma"]:
                    finals[r["slot"]] = 16 * r["use"]
            for s, v in sorted(finals.items()):
                if seen.get((e, s), -1) < v:
                    eng_obj.wait_ge(sems[(e, s)], v)

        with nc.Block() as block:
            @block.tensor
            def _(t):
                replay(t, "pe")

            @block.vector
            def _(v):
                replay(v, "dve")

            @block.scalar
            def _(s):
                replay(s, "act")

            @block.gpsimd
            def _(g):
                replay(g, "pool")

            @block.sync
            def _(s):
                replay(s, "sp")
        stack.close()


def last(*toks):
    out = []
    for t in toks:
        if t is None:
            continue
        if isinstance(t, (list, tuple)):
            out.extend(x for x in t if x is not None)
        else:
            out.append(t)
    return out



def build_stage1(nc, em, io, seq=2048, tc=256):
    stS = contextlib.ExitStack()

    def A(name, shape, dt):
        return stS.enter_context(nc.sbuf_tensor(name, shape, dt))
    ntok = 2 * seq
    nch = ntok // tc
    cps = seq // tc

    x = io["x"]
    R1 = io["R1"]

    R0 = nc.dram_tensor("s1_R0", [DM, ntok], F32, kind="Internal").ap()
    xcD = nc.dram_tensor("s1_xc", [2048, ntok], F32, kind="Internal").ap()
    gD = nc.dram_tensor("s1_g", [2048, ntok], F32, kind="Internal").ap()
    bcD = nc.dram_tensor("s1_bc", [nch, 32, tc], F32, kind="Internal").ap()

    ident = A("s1_identS", [128, 128], F32).ap()
    ident_bf = A("ident_bf", [128, 128], BF16).ap()
    ones_col = A("ones_col", [128, 1], F32).ap()
    ones_row = A("ones_row", [1, 128], F32).ap()
    lnw = A("lnw", [128, 8], F32).ap()
    lnb = A("lnb", [128, 8], F32).ap()
    convw = A("convw", [128, 16, 4], F32).ap()
    convb = A("convb", [128, 16], F32).ap()
    dtb = A("dtb", [128, 16], F32).ap()
    Dp = A("Dp", [128, 16], F32).ap()
    Asb = A("Asb", [128, 16, 16], F32).ap()
    xpw = A("xpw", [128, 16, 96], BF16).ap()
    dtw = A("dtw", [64, 2048], BF16).ap()
    opw = A("opw", [128, 16, 1024], BF16).ap()

    t_id = em.dma("sp", ident, io["ident"])
    t_idb = em.dve(lambda v: v.tensor_copy(ident_bf, ident), waits=[t_id])
    t_oc = em.pool(lambda g: g.memset(ones_col, 1.0))
    t_or = em.pool(lambda g: g.memset(ones_row, 1.0))
    w_lnw = em.dma("sp", lnw, io["ln_w"])
    w_lnb = em.dma("sp", lnb, io["ln_b"])
    w_cw = em.dma("sp", convw, io["conv_w"])
    w_cb = em.dma("sp", convb, io["conv_b"])
    w_dtb = em.dma("sp", dtb, io["dt_proj_b"])
    w_D = em.dma("sp", Dp, io["D_param"])
    w_Al = em.dma("sp", Asb, io["A_log"])
    t_Ae = em.act(lambda s: s.activation(out=Asb, in_=Asb, func=AF.Exp),
                  waits=[w_Al])
    t_An = em.dve(lambda v: v.tensor_scalar_mul(Asb, Asb, -1.0), waits=[t_Ae])
    w_xp = em.dma("pool", xpw, io["x_proj_wT"].rearrange("(kc p) m -> p kc m", p=128))
    w_dt = em.dma("pool", dtw, io["dt_proj_wT"])
    w_op = em.dma("pool", opw, io["out_proj_wT"].rearrange("(kc p) m -> p kc m", p=128))

    PS = [stS.enter_context(nc.psum_tensor(f"s1_ps{i}", [128, 512], F32)).ap()
          for i in range(8)]

    stackA = contextlib.ExitStack()
    trT = stackA.enter_context(nc.sbuf_tensor("trT", [128, 1024], F32)).ap()
    trE = stackA.enter_context(nc.sbuf_tensor("trE", [128, 2, 128], F32)).ap()
    ipw = stackA.enter_context(nc.sbuf_tensor("ipw", [128, 8, 4096], BF16)).ap()
    rch = stackA.enter_context(nc.sbuf_tensor("rch", [128, 8, tc], F32)).ap()
    sqA = stackA.enter_context(nc.sbuf_tensor("sqA", [128, tc], F32)).ap()
    xnch = stackA.enter_context(nc.sbuf_tensor("xnch", [128, 8, tc], BF16)).ap()
    xinb = stackA.enter_context(nc.sbuf_tensor("xinb", [128, 16, tc + 3], F32)).ap()
    xcr = stackA.enter_context(nc.sbuf_tensor("xcr", [128, 2, tc], F32)).ap()
    gr = stackA.enter_context(nc.sbuf_tensor("gr", [128, 2, tc], F32)).ap()
    rows = stackA.enter_context(nc.sbuf_tensor("s1rows", [1, 4, tc], F32)).ap()

    xf = x.rearrange("b t d -> (b t) d")
    t_mm_last = None
    tok_evac = [None, None]
    tok_dma = [None, None]
    for tt in range(ntok // 128):
        t_load = em.dma("sp", trT, xf[tt * 128:(tt + 1) * 128, :],
                        waits=last(t_mm_last, t_id))
        for dc in range(8):
            bank = dc % 2
            tmm = em.pe(lambda t, dc=dc, bank=bank: t.matmul(
                PS[bank][:, 0:128], trT[:, dc * 128:(dc + 1) * 128], ident,
                start=True, stop=True),
                waits=last(t_load, tok_evac[bank], t_id))
            t_mm_last = tmm
            tev = em.dve(lambda v, bank=bank: v.tensor_copy(
                trE[:, bank, :], PS[bank][:, 0:128]),
                waits=last(tmm, tok_dma[bank]))
            tok_evac[bank] = tev
            td = em.dma("sp", R0[dc * 128:(dc + 1) * 128, tt * 128:(tt + 1) * 128],
                        trE[:, bank, :], waits=[tev])
            tok_dma[bank] = td

    w_ip = em.dma("pool", ipw, io["in_proj_wT"].rearrange("(kc p) m -> p kc m", p=128))

    phaseA_stores = [tok_dma[0], tok_dma[1]]
    t_hzero = None
    tok_sq_user = None
    tok_xc = [None, None]
    tok_g = [None, None]
    tok_xin = {}
    tok_ps = {}
    t_stats = None
    t_ipmm_last = None
    t_rch_last = None

    for ch in range(nch):
        t0 = ch * tc
        t_r = em.dma("sp", rch,
                     R0.rearrange("(dc p) T -> p dc T", p=128)[:, :, t0:t0 + tc],
                     waits=last(t_rch_last, tok_dma[0], tok_dma[1]))
        if ch % cps == 0:
            t_hzero = em.pool(lambda g: g.memset(xinb[:, :, 0:3], 0.0),
                              waits=last(*[tok_xin.get(("conv", m)) for m in range(16)]))
        t_mm1 = None
        t_mm2 = None
        for dc in range(8):
            t_mm1 = em.pe(lambda t, dc=dc: t.matmul(
                PS[2][0:1, 0:tc], ones_col, rch[:, dc, :],
                start=(dc == 0), stop=(dc == 7)),
                waits=last(t_r, t_oc) + (last(t_stats) if dc == 0 else []))
            t_sqv = em.act(lambda s, dc=dc: s.activation(
                out=sqA, in_=rch[:, dc, :], func=AF.Square),
                waits=last(t_r, tok_sq_user))
            t_mm2 = em.pe(lambda t, dc=dc: t.matmul(
                PS[3][0:1, 0:tc], ones_col, sqA,
                start=(dc == 0), stop=(dc == 7)),
                waits=last(t_sqv) + (last(t_stats) if dc == 0 else []))
            tok_sq_user = t_mm2
        mu = rows[:, 0, :]
        v1 = rows[:, 1, :]
        sc = rows[:, 2, :]
        tq = rows[:, 3, :]
        t_mu = em.dve(lambda v: v.tensor_scalar_mul(mu, PS[2][0:1, 0:tc], 1.0 / DM),
                      waits=last(t_mm1, t_stats))
        t_msq = em.act(lambda s: s.activation(out=tq, in_=mu, func=AF.Square),
                       waits=[t_mu])
        t_v1 = em.dve(lambda v: v.scalar_tensor_tensor(
            out=v1, in0=PS[3][0:1, 0:tc], scalar=1.0 / DM, in1=tq,
            op0=ALU.mult, op1=ALU.subtract), waits=[t_mm2, t_msq])
        t_a = em.dve(lambda v: v.tensor_scalar_add(tq, v1, 1e-5), waits=[t_v1, t_msq])
        t_rcp = em.dve(lambda v: v.reciprocal(sc, tq), waits=[t_a])
        t_b = em.dve(lambda v: v.tensor_tensor(out=sc, in0=v1, in1=sc, op=ALU.mult),
                     waits=[t_rcp])
        t_b2 = em.dve(lambda v: v.tensor_scalar_add(sc, sc, 1e-5), waits=[t_b])
        t_c = em.dve(lambda v: v.tensor_tensor(out=sc, in0=sc, in1=tq, op=ALU.mult),
                     waits=[t_b2])
        t_sd = em.act(lambda s: s.activation(out=sc, in_=sc, func=AF.Sqrt),
                      waits=[t_c])
        t_sc = em.dve(lambda v: v.reciprocal(sc, sc), waits=[t_sd])
        t_nb = em.dve(lambda v: v.tensor_tensor(out=tq, in0=mu, in1=sc, op=ALU.mult),
                      waits=[t_sc])
        t_nb2 = em.dve(lambda v: v.tensor_scalar_mul(tq, tq, -1.0), waits=[t_nb])
        t_stats = t_nb2
        t_srep = em.pe(lambda t: t.matmul(PS[4][:, 0:tc], ones_row, sc,
                                          start=True, stop=True),
                       waits=last(t_sc, t_or, tok_ps.get(4)))
        t_brep = em.pe(lambda t: t.matmul(PS[5][:, 0:tc], ones_row, tq,
                                          start=True, stop=True),
                       waits=last(t_nb2, t_or, tok_ps.get(5)))
        t_xns = []
        for dc in range(8):
            t1 = em.dve(lambda v, dc=dc: v.tensor_tensor(
                out=sqA, in0=rch[:, dc, :], in1=PS[4][:, 0:tc], op=ALU.mult),
                waits=last(t_srep, tok_sq_user))
            t2 = em.dve(lambda v, dc=dc: v.tensor_tensor(
                out=sqA, in0=sqA, in1=PS[5][:, 0:tc], op=ALU.add),
                waits=[t_brep, t1])
            t3 = em.dve(lambda v, dc=dc: v.tensor_scalar(
                out=xnch[:, dc, :], in0=sqA, scalar1=lnw[:, dc:dc + 1],
                scalar2=lnb[:, dc:dc + 1], op0=ALU.mult, op1=ALU.add),
                waits=last(t2, w_lnw, w_lnb, t_ipmm_last))
            tok_sq_user = t3
            t_xns.append(t3)
        tok_ps[4] = t_xns[-1]
        tok_ps[5] = t_xns[-1]
        t_rch_last = t_xns[-1]

        for m in range(32):
            bank = m % 2
            t_mm = None
            for kc in range(8):
                t_mm = em.pe(lambda t, m=m, kc=kc, bank=bank: t.matmul(
                    PS[bank][:, 0:tc], ipw[:, kc, m * 128:(m + 1) * 128],
                    xnch[:, kc, :], start=(kc == 0), stop=(kc == 7)),
                    waits=last(t_xns[kc], w_ip) + (last(tok_ps.get(bank)) if kc == 0 else []))
            if m < 16:
                t_ev = em.dve(lambda v, m=m, bank=bank: v.tensor_copy(
                    xinb[:, m, 3:3 + tc], PS[bank][:, 0:tc]),
                    waits=last(t_mm, t_hzero, tok_xin.get(("conv", m))))
                tok_ps[bank] = t_ev
                tok_xin[("xin", m)] = t_ev
            else:
                bankg = m % 2
                t_g = em.act(lambda s, bankg=bankg, bank=bank: s.activation(
                    out=gr[:, bankg, :], in_=PS[bank][:, 0:tc], func=AF.Silu),
                    waits=last(t_mm, tok_g[bankg]))
                tok_ps[bank] = t_g
                t_gd = em.dma("sp", gD[(m - 16) * 128:(m - 15) * 128, t0:t0 + tc],
                              gr[:, bankg, :], waits=[t_g])
                tok_g[bankg] = t_gd
        t_ipmm_last = t_mm

        for m in range(16):
            bank = m % 2
            tx = tok_xin[("xin", m)]
            tc1 = em.dve(lambda v, m=m: v.tensor_scalar(
                out=sqA, in0=xinb[:, m, 0:tc], scalar1=convw[:, m, 0:1],
                scalar2=None, op0=ALU.mult),
                waits=last(tx, w_cw, tok_sq_user))
            for k in (1, 2, 3):
                tc1 = em.dve(lambda v, m=m, k=k: v.scalar_tensor_tensor(
                    out=sqA, in0=xinb[:, m, k:k + tc], scalar=convw[:, m, k:k + 1],
                    in1=sqA, op0=ALU.mult, op1=ALU.add),
                    waits=[tc1])
            t_halo = em.dve(lambda v, m=m: v.tensor_copy(
                xinb[:, m, 0:3], xinb[:, m, tc:tc + 3]), waits=[tc1])
            tok_xin[("conv", m)] = t_halo
            t_xcv = em.act(lambda s, m=m, bank=bank: s.activation(
                out=xcr[:, bank, :], in_=sqA, func=AF.Silu,
                bias=convb[:, m:m + 1]),
                waits=last(tc1, w_cb, tok_xc[bank]))
            tok_sq_user = t_xcv
            t_xcd = em.dma("sp", xcD[m * 128:(m + 1) * 128, t0:t0 + tc],
                           xcr[:, bank, :], waits=[t_xcv])
            tok_xc[bank] = t_xcd
        phaseA_stores = [tok_xc[0], tok_xc[1], tok_g[0], tok_g[1]]

    stackA.close()

    stackB = contextlib.ExitStack()
    E = stackB.enter_context
    xc2 = E(nc.sbuf_tensor("xc2", [128, 16, tc], F32)).ap()
    xcb2 = E(nc.sbuf_tensor("xcb2", [128, 16, tc], BF16)).ap()
    dbc_bf = E(nc.sbuf_tensor("dbc_bf", [64, tc], BF16)).ap()
    bc_sb = E(nc.sbuf_tensor("bc_sb", [32, tc], F32)).ap()
    dtch = E(nc.sbuf_tensor("dtch", [128, 16, tc], F32)).ap()
    u_rot = E(nc.sbuf_tensor("u_rot", [128, 2, tc], F32)).ap()
    sA = E(nc.sbuf_tensor("sA", [128, tc], F32)).ap()
    sL = E(nc.sbuf_tensor("sL", [128, tc], F32)).ap()
    sR = E(nc.sbuf_tensor("sR", [128, tc], F32)).ap()
    sq = E(nc.sbuf_tensor("sqB", [128, tc], F32)).ap()
    Brep = E(nc.sbuf_tensor("Brep", [128, 16, tc], F32)).ap()
    Crep = E(nc.sbuf_tensor("Crep", [128, 16, tc], F32)).ap()
    a_rot = E(nc.sbuf_tensor("a_rot", [128, 2, tc], F32)).ap()
    w_all = E(nc.sbuf_tensor("w_all", [128, 16, tc], F32)).ap()
    hbuf = E(nc.sbuf_tensor("hbuf", [128, 16, tc], F32)).ap()
    hc_all = E(nc.sbuf_tensor("hc_all", [128, 16, tc], BF16)).ap()
    carry = E(nc.sbuf_tensor("carry", [128, 16, 16], F32)).ap()
    y3 = E(nc.sbuf_tensor("y3", [128, 16, tc], BF16)).ap()
    outE = E(nc.sbuf_tensor("outE", [128, 8, tc], F32)).ap()
    g_rot2 = E(nc.sbuf_tensor("g_rot2", [128, 2, tc], F32)).ap()
    r0ch = E(nc.sbuf_tensor("r0ch", [128, 8, tc], F32)).ap()
    e_rot = E(nc.sbuf_tensor("e_rot", [128, 2, tc], F32)).ap()
    rows2 = E(nc.sbuf_tensor("s1rows2", [1, 8, tc], F32)).ap()

    prevA = last(*phaseA_stores)
    U = {}

    def tu(*k):
        return last(*[U.get(x) for x in k])

    t_carry_w = []
    t_store_last = []
    for ch in range(nch):
        t0 = ch * tc
        t_xc2 = em.dma("sp", xc2,
                       xcD.rearrange("(mc p) T -> p mc T", p=128)[:, :, t0:t0 + tc],
                       waits=prevA + tu("xc2"))
        t_xcb2 = em.dma("pool", xcb2,
                        xcD.rearrange("(mc p) T -> p mc T", p=128)[:, :, t0:t0 + tc],
                        waits=prevA + tu("xcb2"))
        t_mm = None
        for kc in range(16):
            t_mm = em.pe(lambda t, kc=kc: t.matmul(
                PS[0][0:96, 0:tc], xpw[:, kc, :], xcb2[:, kc, :],
                start=(kc == 0), stop=(kc == 15)),
                waits=last(t_xcb2, w_xp) + (tu("ps0") if kc == 0 else []))
        t_dbc1 = em.dve(lambda v: v.tensor_copy(dbc_bf, PS[0][0:64, 0:tc]),
                        waits=last(t_mm) + tu("dbc_bf"))
        t_dbc2 = em.dve(lambda v: v.tensor_copy(bc_sb, PS[0][64:96, 0:tc]),
                        waits=last(t_mm) + tu("bc_sb"))
        U["ps0"] = t_dbc2
        t_bcd = em.dma("sp", bcD[ch], bc_sb, waits=[t_dbc2])
        U["bc_sb"] = t_bcd
        t_brepL = em.dma("sp", Brep,
                         bcD[ch, 0:16, :].unsqueeze(0).partition_broadcast(128)
                         .rearrange("p a s T -> p (a s) T"),
                         waits=[t_bcd] + tu("Brep"))
        t_crepL = em.dma("sp", Crep,
                         bcD[ch, 16:32, :].unsqueeze(0).partition_broadcast(128)
                         .rearrange("p a s T -> p (a s) T"),
                         waits=[t_bcd] + tu("Crep"))
        t_dts = []
        for m in range(16):
            bank = 1 + m % 2
            t_mmd = em.pe(lambda t, m=m, bank=bank: t.matmul(
                PS[bank][:, 0:tc], dtw[:, m * 128:(m + 1) * 128], dbc_bf,
                start=True, stop=True),
                waits=last(t_dbc1, w_dt) + tu(("psdt", bank)))
            t_ab = em.act(lambda s, m=m, bank=bank: s.activation(
                out=sA, in_=PS[bank][:, 0:tc], func=AF.Abs,
                bias=dtb[:, m:m + 1]), waits=last(t_mmd, w_dtb) + tu("sA"))
            t_ex = em.act(lambda s: s.activation(
                out=sA, in_=sA, func=AF.Exp, scale=-1.0), waits=[t_ab])
            t_ln = em.act(lambda s: s.activation(
                out=sL, in_=sA, func=AF.Ln, bias=1.0), waits=last(t_ex) + tu("sL"))
            U["sA"] = t_ln
            t_rl = em.act(lambda s, m=m, bank=bank: s.activation(
                out=sR, in_=PS[bank][:, 0:tc], func=AF.Relu,
                bias=dtb[:, m:m + 1]), waits=last(t_mmd) + tu("sR"))
            U[("psdt", bank)] = t_rl
            t_dt = em.dve(lambda v, m=m: v.tensor_tensor(
                out=dtch[:, m, :], in0=sL, in1=sR, op=ALU.add),
                waits=last(t_ln, t_rl) + tu(("dt", m)))
            U["sL"] = t_dt
            U["sR"] = t_dt
            t_dts.append(t_dt)
        if ch % cps == 0:
            t_cz = em.pool(lambda g: g.memset(carry, 0.0), waits=t_carry_w)
            t_carry_w = [t_cz]
        t_y3s = []
        t_ylast = None
        for m in range(16):
            ub = m % 2
            t_u = em.dve(lambda v, m=m, ub=ub: v.tensor_tensor(
                out=u_rot[:, ub, :], in0=dtch[:, m, :], in1=xc2[:, m, :],
                op=ALU.mult),
                waits=last(t_dts[m], t_xc2) + tu(("u", ub)))
            t_wall = em.dve(lambda v, ub=ub: v.tensor_tensor(
                out=w_all, in0=u_rot[:, ub, :].unsqueeze(1)
                .broadcast_to([128, 16, tc]), in1=Brep, op=ALU.mult),
                waits=last(t_u, t_brepL) + tu("w_all"))
            t_h = None
            for s in range(16):
                ab = s % 2
                t_aa = em.act(lambda sc_, m=m, s=s, ab=ab: sc_.activation(
                    out=a_rot[:, ab, :], in_=dtch[:, m, :], func=AF.Exp,
                    scale=Asb[:, m, s:s + 1]),
                    waits=last(t_dts[m], t_An) + tu(("a", ab)))
                t_h = em.dve(lambda v, m=m, s=s, ab=ab: v.tensor_tensor_scan(
                    out=hbuf[:, s, :], data0=a_rot[:, ab, :],
                    data1=w_all[:, s, :], initial=carry[:, m, s:s + 1],
                    op0=ALU.mult, op1=ALU.add),
                    waits=last(t_aa, t_wall) + tu(("h", s)) + t_carry_w[-1:])
                U[("a", ab)] = t_h
            U["w_all"] = t_h
            t_hcall = em.dve(lambda v: v.tensor_tensor(
                out=hc_all, in0=hbuf, in1=Crep, op=ALU.mult),
                waits=last(t_h, t_crepL) + tu("hc_all"))
            t_cc = em.dve(lambda v, m=m: v.tensor_copy(
                carry[:, m, :], hbuf[:, :, tc - 1]), waits=[t_h])
            t_carry_w.append(t_cc)
            for ss in range(16):
                U[("h", ss)] = t_cc
            t_ylast = None
            for s in range(16):
                t_ylast = em.pe(lambda t, s=s: t.matmul(
                    PS[3][:, 0:tc], ident_bf, hc_all[:, s, :],
                    start=(s == 0), stop=(s == 15)),
                    waits=last(t_hcall, t_idb) + (tu("psy") if s == 0 else []))
            U["hc_all"] = t_ylast
            U[("u", ub)] = t_ylast
            gb = m % 2
            t_g2 = em.dma("sp", g_rot2[:, gb, :],
                          gD[m * 128:(m + 1) * 128, t0:t0 + tc],
                          waits=(prevA if ch == 0 and m < 2 else []) + tu(("g2", gb)))
            t_e1 = em.dve(lambda v, m=m: v.scalar_tensor_tensor(
                out=sq, in0=xc2[:, m, :], scalar=Dp[:, m:m + 1],
                in1=PS[3][:, 0:tc], op0=ALU.mult, op1=ALU.add),
                waits=last(t_ylast, w_D, t_xc2) + tu("sqB"))
            U["psy"] = t_e1
            t_e2 = em.dve(lambda v, m=m, gb=gb: v.tensor_tensor(
                out=y3[:, m, :], in0=sq, in1=g_rot2[:, gb, :], op=ALU.mult),
                waits=last(t_e1, t_g2) + tu(("y3", m)))
            U["sqB"] = t_e2
            U[("g2", gb)] = t_e2
            t_y3s.append(t_e2)
        U["xc2"] = t_y3s[-1]
        U["xcb2"] = t_mm
        U["Brep"] = t_y3s[-1]
        U["Crep"] = t_y3s[-1]
        U["dbc_bf"] = t_dts[-1]
        for m in range(16):
            U[("dt", m)] = t_y3s[-1]

        t_r0 = em.dma("sp", r0ch,
                      R0.rearrange("(dc p) T -> p dc T", p=128)[:, :, t0:t0 + tc],
                      waits=tu("r0ch"))
        t_ms1 = None
        t_ms2 = None
        for dc in range(8):
            t_ms1 = em.pe(lambda t, dc=dc: t.matmul(
                PS[6][0:1, 0:tc], ones_col, r0ch[:, dc, :],
                start=(dc == 0), stop=(dc == 7)),
                waits=last(t_r0, t_oc) + (tu("ps67") if dc == 0 else []))
            t_sq1 = em.act(lambda s, dc=dc: s.activation(
                out=sq, in_=r0ch[:, dc, :], func=AF.Square),
                waits=[t_r0] + tu("sqB"))
            t_ms2 = em.pe(lambda t, dc=dc: t.matmul(
                PS[7][0:1, 0:tc], ones_col, sq,
                start=(dc == 0), stop=(dc == 7)),
                waits=[t_sq1] + (tu("ps67") if dc == 0 else []))
            U["sqB"] = t_ms2
        mu1 = rows2[:, 4, :]
        rs1 = rows2[:, 5, :]
        bb1 = rows2[:, 6, :]
        tq2 = rows2[:, 7, :]
        t_m1 = em.dve(lambda v: v.tensor_scalar_mul(mu1, PS[6][0:1, 0:tc], 1.0 / DM),
                      waits=[t_ms1] + tu("rows2b"))
        t_q1 = em.act(lambda s: s.activation(out=tq2, in_=mu1, func=AF.Square),
                      waits=[t_m1])
        t_v1b = em.dve(lambda v: v.scalar_tensor_tensor(
            out=rs1, in0=PS[7][0:1, 0:tc], scalar=1.0 / DM, in1=tq2,
            op0=ALU.mult, op1=ALU.subtract), waits=[t_ms2, t_q1])
        t_v1c = em.dve(lambda v: v.tensor_scalar_add(rs1, rs1, 1e-5), waits=[t_v1b])
        t_s1b = em.act(lambda s: s.activation(out=rs1, in_=rs1, func=AF.Sqrt),
                       waits=[t_v1c])
        t_rs1 = em.dve(lambda v: v.reciprocal(rs1, rs1), waits=[t_s1b])
        t_bb1 = em.dve(lambda v: v.tensor_tensor(out=bb1, in0=mu1, in1=rs1,
                                                 op=ALU.mult), waits=[t_rs1])
        t_bb1b = em.dve(lambda v: v.tensor_scalar_mul(bb1, bb1, -1.0), waits=[t_bb1])
        U["ps67a"] = [t_v1b]
        t_rs1rep = em.pe(lambda t: t.matmul(PS[2][:, 0:tc], ones_row, rs1,
                                            start=True, stop=True),
                         waits=last(t_rs1, t_or) + tu(("psdt", 2)))
        t_bb1rep = em.pe(lambda t: t.matmul(PS[1][:, 0:tc], ones_row, bb1,
                                            start=True, stop=True),
                         waits=last(t_bb1b, t_or) + tu(("psdt", 1)))
        U["rows2b"] = t_bb1b

        t_mo1 = None
        t_mo2 = None
        t_mmo = None
        t_adds = []
        for mo in range(8):
            bank = 4 + mo % 2
            for kc in range(16):
                t_mmo = em.pe(lambda t, mo=mo, kc=kc, bank=bank: t.matmul(
                    PS[bank][:, 0:tc], opw[:, kc, mo * 128:(mo + 1) * 128],
                    y3[:, kc, :], start=(kc == 0), stop=(kc == 15)),
                    waits=last(t_y3s[kc], w_op) + (tu(("pso", bank)) if kc == 0 else []))
            t_xa = em.dve(lambda v, mo=mo: v.tensor_tensor(
                out=sq, in0=r0ch[:, mo, :], in1=PS[2][:, 0:tc], op=ALU.mult),
                waits=[t_rs1rep] + tu("sqB"))
            t_xb = em.dve(lambda v: v.tensor_tensor(
                out=sq, in0=sq, in1=PS[1][:, 0:tc], op=ALU.add),
                waits=[t_bb1rep, t_xa])
            t_evo = em.dve(lambda v, mo=mo, bank=bank: v.tensor_tensor(
                out=outE[:, mo, :], in0=sq, in1=PS[bank][:, 0:tc], op=ALU.add),
                waits=last(t_mmo, t_xb) + tu(("outE", mo)))
            t_adds.append(t_evo)
            U[("pso", bank)] = t_evo
            t_sqo = em.act(lambda s, mo=mo: s.activation(
                out=sq, in_=outE[:, mo, :], func=AF.Square),
                waits=[t_evo])
            U["sqB"] = t_sqo
            t_mo1 = em.pe(lambda t, mo=mo: t.matmul(
                PS[6][0:1, 0:tc], ones_col, outE[:, mo, :],
                start=(mo == 0), stop=(mo == 7)),
                waits=[t_evo] + (tu("ps67a") if mo == 0 else []))
            t_mo2 = em.pe(lambda t, mo=mo: t.matmul(
                PS[7][0:1, 0:tc], ones_col, sq,
                start=(mo == 0), stop=(mo == 7)),
                waits=[t_sqo] + (tu("ps67a") if mo == 0 else []))
            U["sqB"] = t_mo2
        mu2 = rows2[:, 0, :]
        vv = rows2[:, 1, :]
        rr = rows2[:, 2, :]
        bb = rows2[:, 3, :]
        t_mu = em.dve(lambda v: v.tensor_scalar_mul(mu2, PS[6][0:1, 0:tc], 1.0 / DM),
                      waits=[t_mo1] + tu("rows2"))
        t_ms = em.act(lambda s: s.activation(out=vv, in_=mu2, func=AF.Square),
                      waits=[t_mu])
        t_vv = em.dve(lambda v: v.scalar_tensor_tensor(
            out=vv, in0=PS[7][0:1, 0:tc], scalar=1.0 / DM, in1=vv,
            op0=ALU.mult, op1=ALU.subtract), waits=[t_mo2, t_ms])
        t_ve = em.dve(lambda v: v.tensor_scalar_add(rr, vv, 1e-5), waits=[t_vv])
        t_sd = em.act(lambda s: s.activation(out=rr, in_=rr, func=AF.Sqrt),
                      waits=[t_ve])
        t_rr = em.dve(lambda v: v.reciprocal(rr, rr), waits=[t_sd])
        t_nb = em.dve(lambda v: v.tensor_tensor(out=bb, in0=mu2, in1=rr,
                                                op=ALU.mult), waits=[t_rr])
        t_nb2 = em.dve(lambda v: v.tensor_scalar_mul(bb, bb, -1.0), waits=[t_nb])
        U["ps67"] = [t_vv, t_nb2]
        U["rows2"] = t_nb2
        t_rrep = em.pe(lambda t: t.matmul(PS[0][:, 0:tc], ones_row, rr,
                                          start=True, stop=True),
                       waits=last(t_rr, t_or) + tu("ps0"))
        t_brep2 = em.pe(lambda t: t.matmul(PS[3][:, 0:tc], ones_row, bb,
                                           start=True, stop=True),
                        waits=last(t_nb2, t_or) + tu("psy"))
        t_a1 = None
        t_a2 = None
        for mo in range(8):
            rb = mo % 2
            t_a1 = em.dve(lambda v, mo=mo, rb=rb: v.tensor_tensor(
                out=e_rot[:, rb, :], in0=outE[:, mo, :], in1=PS[0][:, 0:tc],
                op=ALU.mult), waits=[t_rrep, t_adds[mo]] + tu(("er", rb)))
            t_a2 = em.dve(lambda v, rb=rb: v.tensor_tensor(
                out=e_rot[:, rb, :], in0=e_rot[:, rb, :], in1=PS[3][:, 0:tc],
                op=ALU.add), waits=[t_brep2, t_a1])
            t_a3 = em.dve(lambda v, mo=mo, rb=rb: v.tensor_tensor(
                out=e_rot[:, rb, :], in0=e_rot[:, rb, :], in1=r0ch[:, mo, :],
                op=ALU.add), waits=[t_a2])
            U[("outE", mo)] = [t_a1, t_mo1, t_mo2]
            t_st = em.dma("sp", R1[mo * 128:(mo + 1) * 128, t0:t0 + tc],
                          e_rot[:, rb, :], waits=[t_a3])
            U[("er", rb)] = t_st
            t_store_last.append(t_st)
        U["r0ch"] = t_a3
        U["ps0"] = t_a1
        U["psy"] = t_a2
        U[("psdt", 1)] = t_xb
        U[("psdt", 2)] = t_xa
        prevA = []
        t_store_last = t_store_last[-2:]
    stackB.close()
    stS.close()
    return t_store_last




def build_stage2(nc, em, io, seq=2048, tc=512):
    stS = contextlib.ExitStack()

    def A(name, shape, dt):
        return stS.enter_context(nc.sbuf_tensor(name, shape, dt))
    ntok = 2 * seq
    nch = ntok // tc
    nst = seq // 128
    cw = min(512, seq)
    ncw = seq // cw

    R1 = io["R1"]
    R2 = io["R2"]

    xnB = nc.dram_tensor("s2_xnB", [DM, ntok], BF16, kind="Internal").ap()
    xnF = nc.dram_tensor("s2_xnF", [DM, ntok], F32, kind="Internal").ap()
    qD = nc.dram_tensor("s2_q", [DM, ntok], BF16, kind="Internal").ap()
    kD = nc.dram_tensor("s2_k", [DM, ntok], BF16, kind="Internal").ap()
    vD = nc.dram_tensor("s2_v", [DM, ntok], BF16, kind="Internal").ap()
    yB = nc.dram_tensor("s2_yB", [DM, ntok], BF16, kind="Internal").ap()
    yF = nc.dram_tensor("s2_yF", [DM, ntok], F32, kind="Internal").ap()
    p8D = nc.dram_tensor("s2_p8", [nch, 256, tc], BF16, kind="Internal").ap()
    hD = nc.dram_tensor("s2_h", [nch, 16, tc], BF16, kind="Internal").ap()

    PSALL = stS.enter_context(nc.psum_tensor("s2_psall", [128, 4, 512], F32)).ap()
    P0 = PSALL[:, 0, :]
    P1 = PSALL[:, 1, :]
    P2 = PSALL[:, 2, :]
    P3 = PSALL[:, 3, :]
    pT0 = stS.enter_context(nc.psum_tensor("s2_pT0", [128, 1024], BF16)).ap()
    pT1 = stS.enter_context(nc.psum_tensor("s2_pT1", [128, 1024], BF16)).ap()
    P6 = stS.enter_context(nc.psum_tensor("s2_p6", [128, 512], F32)).ap()
    P7 = stS.enter_context(nc.psum_tensor("s2_p7", [128, 512], F32)).ap()
    SF = PSALL.rearrange("p a b -> p (a b)")

    ones_col = A("s2_onesc", [128, 1], F32).ap()
    ones_row = A("s2_onesr", [1, 128], F32).ap()
    ones_bf = A("s2_onesb", [1, 512], BF16).ap()
    ident_bf = A("s2_identb", [128, 128], BF16).ap()
    identf = A("s2_identf", [128, 128], F32).ap()
    Mg = A("s2_Mg", [128, 16, 128], BF16).ap()
    sel16T = A("s2_sel16T", [16, 128], BF16).ap()
    sel128 = A("s2_sel128", [128, 16], BF16).ap()
    t_oc = em.pool(lambda g: g.memset(ones_col, 1.0))
    t_or = em.pool(lambda g: g.memset(ones_row, 1.0))
    t_ob = em.pool(lambda g: g.memset(ones_bf, 1.0))
    t_i1 = em.dma("sp", identf, io["ident"])
    t_ib = em.dve(lambda v: v.tensor_copy(ident_bf, identf), waits=[t_i1])
    t_cs = [t_ib, t_i1, t_oc, t_or, t_ob]
    t_cs.append(em.dma("pool", Mg, io["Mg"]))
    t_cs.append(em.dma("pool", sel16T, io["sel16T"]))
    t_cs.append(em.dma("pool", sel128, io["sel128"]))

    ones_f = A("s2_onesf", [1, 512], F32).ap()
    b1T = A("s2_b1T", [128, 4, 2, 16], BF16).ap()
    b2T = A("s2_b2T", [128, 4, 2, 1024], BF16).ap()
    evq = A("s2_evq", [128, 2, tc], BF16).ap()
    stW = contextlib.ExitStack()
    scm = stW.enter_context(nc.sbuf_tensor("s2_scm", [128, 8, 2, 256], BF16)).ap()
    nwT = stW.enter_context(nc.sbuf_tensor("s2_nwT", [128, 4, 8, 255], F32)).ap()
    nbT = stW.enter_context(nc.sbuf_tensor("s2_nbT", [1, 4, 255], F32)).ap()
    t_cs.append(em.dma("pool", scm, io["scm"]))
    w_nw = em.dma("sp", nwT, io["nwT4"].rearrange("(kc p) f n -> p f kc n", p=128))
    w_nb = em.dma("sp", nbT, io["nb4"])
    t_of = em.pool(lambda g: g.memset(ones_f, 1.0))
    w_b1 = em.dma("pool", b1T, io["b14"].rearrange("f (c p) l -> p f c l", p=128))
    w_b2 = em.dma("pool", b2T, io["b24"].rearrange("f (c p) m -> p f c m", p=128))

    U = {}

    def tu(*k):
        return last(*[U.get(x) for x in k])

    st0 = contextlib.ExitStack()
    E0 = st0.enter_context
    rch = E0(nc.sbuf_tensor("s2_rch", [128, 8, tc], F32)).ap()
    sq = E0(nc.sbuf_tensor("s2_sq", [128, tc], F32)).ap()
    xnc = E0(nc.sbuf_tensor("s2_xnc", [128, 8, tc], BF16)).ap()
    xfc = E0(nc.sbuf_tensor("s2_xfc", [128, 2, tc], F32)).ap()
    rows = E0(nc.sbuf_tensor("s2_rows", [1, 4, tc], F32)).ap()
    t_done_xn = []
    for ch in range(nch):
        t0 = ch * tc
        t_r = em.dma("sp", rch,
                     R1.rearrange("(dc p) T -> p dc T", p=128)[:, :, t0:t0 + tc],
                     waits=tu("rch"))
        t_m1 = None
        t_m2 = None
        for dc in range(8):
            t_m1 = em.pe(lambda t, dc=dc: t.matmul(
                P6[0:1, 0:tc], ones_col, rch[:, dc, :],
                start=(dc == 0), stop=(dc == 7)),
                waits=last(t_r, t_oc) + (tu("ps67") if dc == 0 else []))
            t_sq = em.act(lambda s, dc=dc: s.activation(
                out=sq, in_=rch[:, dc, :], func=AF.Square),
                waits=[t_r] + tu("sq"))
            t_m2 = em.pe(lambda t, dc=dc: t.matmul(
                P7[0:1, 0:tc], ones_col, sq,
                start=(dc == 0), stop=(dc == 7)),
                waits=[t_sq] + (tu("ps67") if dc == 0 else []))
            U["sq"] = t_m2
        mu = rows[:, 0, :]
        v1 = rows[:, 1, :]
        sc = rows[:, 2, :]
        tq = rows[:, 3, :]
        t_mu = em.dve(lambda v: v.tensor_scalar_mul(mu, P6[0:1, 0:tc], 1.0 / DM),
                      waits=[t_m1] + tu("rows"))
        t_q = em.act(lambda s: s.activation(out=tq, in_=mu, func=AF.Square),
                     waits=[t_mu])
        t_v = em.dve(lambda v: v.scalar_tensor_tensor(
            out=v1, in0=P7[0:1, 0:tc], scalar=1.0 / DM, in1=tq,
            op0=ALU.mult, op1=ALU.subtract), waits=[t_m2, t_q])
        U["ps67"] = [t_v, t_mu]
        t_1 = em.dve(lambda v: v.tensor_scalar_add(tq, v1, 1e-5), waits=[t_v, t_q])
        t_2 = em.dve(lambda v: v.reciprocal(sc, tq), waits=[t_1])
        t_3 = em.dve(lambda v: v.scalar_tensor_tensor(
            out=tq, in0=v1, scalar=float(DM) / (DM - 1), in1=sc,
            op0=ALU.mult, op1=ALU.mult), waits=[t_2])
        t_4 = em.act(lambda s: s.activation(out=tq, in_=tq, func=AF.Sqrt),
                     waits=[t_3])
        t_5 = em.dve(lambda v: v.tensor_scalar_add(tq, tq, 1e-5), waits=[t_4])
        t_6 = em.act(lambda s: s.activation(out=tq, in_=tq, func=AF.Sqrt),
                     waits=[t_5])
        t_7 = em.act(lambda s: s.activation(out=sc, in_=sc, func=AF.Sqrt),
                     waits=[t_3])
        t_8 = em.dve(lambda v: v.reciprocal(tq, tq), waits=[t_6])
        t_9 = em.dve(lambda v: v.tensor_tensor(out=sc, in0=sc, in1=tq, op=ALU.mult),
                     waits=[t_7, t_8])
        t_10 = em.dve(lambda v: v.tensor_tensor(out=tq, in0=mu, in1=sc, op=ALU.mult),
                      waits=[t_9])
        t_11 = em.dve(lambda v: v.tensor_scalar_mul(tq, tq, -1.0), waits=[t_10])
        U["rows"] = t_11
        t_sr = em.pe(lambda t: t.matmul(P2[:, 0:tc], ones_row, sc,
                                        start=True, stop=True),
                     waits=last(t_9, t_or) + tu("ps2"))
        t_br = em.pe(lambda t: t.matmul(P3[:, 0:tc], ones_row, tq,
                                        start=True, stop=True),
                     waits=last(t_11, t_or) + tu("ps3"))
        t_l = []
        ta = None
        tb = None
        for dc in range(8):
            fb = dc % 2
            ta = em.dve(lambda v, dc=dc, fb=fb: v.tensor_tensor(
                out=xfc[:, fb, :], in0=rch[:, dc, :], in1=P2[:, 0:tc],
                op=ALU.mult),
                waits=[t_sr] + tu(("xf", fb)))
            tb = em.dve(lambda v, fb=fb: v.tensor_tensor(
                out=xfc[:, fb, :], in0=xfc[:, fb, :], in1=P3[:, 0:tc],
                op=ALU.add), waits=[t_br, ta])
            tcv = em.dve(lambda v, dc=dc, fb=fb: v.tensor_copy(
                xnc[:, dc, :], xfc[:, fb, :]), waits=[tb] + tu("xnc"))
            t_d1 = em.dma("sp", xnF[dc * 128:(dc + 1) * 128, t0:t0 + tc],
                          xfc[:, fb, :], waits=[tcv])
            U[("xf", fb)] = t_d1
            t_l.append(tcv)
        t_d2 = em.dma("sp", xnB.rearrange("(dc p) T -> p dc T", p=128)[:, :, t0:t0 + tc],
                      xnc, waits=t_l)
        U["xnc"] = t_d2
        U["rch"] = t_l + [t_m1]
        U["ps2"] = [ta]
        U["ps3"] = [tb]
        t_done_xn = [t_d2] + tu(("xf", 0)) + tu(("xf", 1))
    st0.close()

    def tree_L1(tree_idx, src_bf, src_f32, act_fn, wname, t_pre):
        st = contextlib.ExitStack()
        E = st.enter_context
        sfx = f"t{tree_idx}a"
        w1 = E(nc.sbuf_tensor("s2_w1_"+sfx, [128, 8, 4096], BF16)).ap()
        xb = E(nc.sbuf_tensor("s2_xb_"+sfx, [128, 8, tc], BF16)).ap()
        xf = E(nc.sbuf_tensor("s2_xf_"+sfx, [128, 8, tc], F32)).ap()
        lgk = E(nc.sbuf_tensor("s2_lgk_"+sfx, [128, 2, tc], F32)).ap()
        p_cur = E(nc.sbuf_tensor("s2_pcur_"+sfx, [128, tc], BF16)).ap()
        g0 = E(nc.sbuf_tensor("s2_g0_"+sfx, [128, tc], BF16)).ap()
        g1 = E(nc.sbuf_tensor("s2_g1_"+sfx, [128, tc], BF16)).ap()
        p8 = E(nc.sbuf_tensor("s2_p8sb_"+sfx, [128, 2, tc], BF16)).ap()
        prep = E(nc.sbuf_tensor("s2_prep_"+sfx, [128, 2, tc], BF16)).ap()
        pF = E(nc.sbuf_tensor("s2_pF_"+sfx, [128, 2, tc], BF16)).ap()
        hsb = E(nc.sbuf_tensor("s2_hsb_"+sfx, [16, tc], BF16)).ap()
        w_w1 = em.dma("pool", w1, io[wname].rearrange("(kc p) m -> p kc m", p=128),
                      waits=t_pre)
        V = {}

        def tv(*k):
            return last(*[V.get(x) for x in k])

        t_ret = []
        for ch in range(nch):
            t0 = ch * tc
            t_x = em.dma("sp", xb,
                         src_bf.rearrange("(dc p) T -> p dc T", p=128)[:, :, t0:t0 + tc],
                         waits=(t_pre if ch == 0 else []) + tv("xb"))
            t_x2 = em.dma("sp", xf,
                          src_f32.rearrange("(dc p) T -> p dc T", p=128)[:, :, t0:t0 + tc],
                          waits=(t_pre if ch == 0 else []) + tv("xf"))
            t_pc = em.dve(lambda v: v.memset(p_cur[0:1, :], 1.0),
                          waits=tv("p_cur") + t_cs + (t_pre if ch == 0 else []))
            t_p8 = []
            for k in range(8):
                wk = 2 ** k
                off = wk - 1
                lb = k % 2
                t_lg = None
                for kc in range(8):
                    t_lg = em.pe(lambda t, kc=kc, wk=wk, off=off: t.matmul(
                        P0[0:wk, 0:tc], nwT[:, tree_idx, kc, off:off + wk],
                        xf[:, kc, :], start=(kc == 0), stop=False),
                        waits=last(t_x2, w_nw) + (tv("ps0") if kc == 0 else []))
                t_lg = em.pe(lambda t, wk=wk, off=off: t.matmul(
                    P0[0:wk, 0:tc], nbT[0:1, tree_idx, off:off + wk],
                    ones_f[0:1, 0:tc], start=False, stop=True),
                    waits=last(w_nb, t_of))
                t_le = em.dve(lambda v, wk=wk, lb=lb: v.tensor_copy(
                    lgk[0:wk, lb, :], P0[0:wk, 0:tc]),
                    waits=[t_lg] + tv(("lgk", lb)))
                V["ps0"] = t_le
                t_g1 = em.dve(lambda v, wk=wk, lb=lb: v.scalar_tensor_tensor(
                    out=g1[0:wk, :], in0=lgk[0:wk, lb, :], scalar=0.0,
                    in1=p_cur[0:wk, :], op0=ALU.is_gt, op1=ALU.mult),
                    waits=[t_le, t_pc] + tv("g1"))
                V[("lgk", lb)] = t_g1
                t_g0 = em.dve(lambda v, wk=wk: v.tensor_tensor(
                    out=g0[0:wk, :], in0=p_cur[0:wk, :], in1=g1[0:wk, :],
                    op=ALU.subtract), waits=[t_g1] + tv("g0"))
                if k < 7:
                    w2 = 2 ** (k + 1)
                    em.pe(lambda t, k=k, wk=wk, w2=w2: t.matmul(
                        P1[0:w2, 0:tc], scm[0:wk, k, 0, 0:w2], g0[0:wk, :],
                        start=True, stop=False),
                        waits=[t_g0] + tv("ps1"))
                    t_c1 = em.pe(lambda t, k=k, wk=wk, w2=w2: t.matmul(
                        P1[0:w2, 0:tc], scm[0:wk, k, 1, 0:w2], g1[0:wk, :],
                        start=False, stop=True), waits=[t_g1])
                    t_pc = em.dve(lambda v, w2=w2: v.tensor_copy(
                        p_cur[0:w2, :], P1[0:w2, 0:tc]), waits=[t_c1, t_g0])
                    V["ps1"] = t_pc
                    V["g0"] = t_pc
                    V["g1"] = t_pc
                else:
                    for half in range(2):
                        em.pe(lambda t, half=half: t.matmul(
                            P1[:, 0:tc], scm[:, 7, 0, half * 128:(half + 1) * 128],
                            g0, start=True, stop=False),
                            waits=[t_g0] + tv("ps1"))
                        t_c1 = em.pe(lambda t, half=half: t.matmul(
                            P1[:, 0:tc], scm[:, 7, 1, half * 128:(half + 1) * 128],
                            g1, start=False, stop=True), waits=[t_g1])
                        t_pe = em.dve(lambda v, half=half: v.tensor_copy(
                            p8[:, half, :], P1[:, 0:tc]),
                            waits=[t_c1] + tv(("p8", half)))
                        V["ps1"] = t_pe
                        t_p8.append(t_pe)
                    V["g0"] = t_p8[-1]
                    V["g1"] = t_p8[-1]
            t_p8d = em.dma("sp", p8D[ch].rearrange("(h p) T -> p h T", p=128),
                           p8, waits=t_p8)
            t_hmm = None
            t_pf_prev = None
            fb_prev = 0
            t_pf_last_odd = None
            for gg in range(32):
                half = gg // 16
                gi = gg % 16
                fb = gg % 2
                bank3 = P3 if fb == 0 else P1
                bkey = "ps3" if fb == 0 else "ps1"
                t_pr = em.pe(lambda t, gi=gi, half=half: t.matmul(
                    P2[:, 0:tc], Mg[:, gi, :], p8[:, half, :],
                    start=True, stop=True),
                    waits=[t_p8[half]] + tv("ps2"))
                t_pv = em.act(lambda s, fb=fb: s.activation(
                    out=prep[:, fb, :], in_=P2[:, 0:tc], func=AF.Copy),
                    waits=[t_pr] + tv(("prep", fb)))
                V["ps2"] = t_pv
                t_F = None
                for kc in range(8):
                    t_F = em.pe(lambda t, gg=gg, kc=kc, bank3=bank3: t.matmul(
                        bank3[:, 0:tc], w1[:, kc, gg * 128:(gg + 1) * 128],
                        xb[:, kc, :], start=(kc == 0), stop=(kc == 7)),
                        waits=last(t_x, w_w1) + (tv(bkey) if kc == 0 else []))
                if t_pf_prev is not None:
                    t_hmm = em.pe(lambda t, fb_prev=fb_prev, gg=gg: t.matmul(
                        P6[0:16, 0:tc], sel128, pF[:, fb_prev, :],
                        start=(gg == 1), stop=False),
                        waits=[t_pf_prev] + (tv("ps6") if gg == 1 else []))
                    V[("pF", fb_prev)] = t_hmm
                t_pf = em.dve(lambda v, fb=fb, bank3=bank3: v.tensor_tensor(
                    out=pF[:, fb, :], in0=bank3[:, 0:tc], in1=prep[:, fb, :],
                    op=ALU.mult), waits=[t_F, t_pv] + tv(("pF", fb)))
                V[bkey] = t_pf
                if fb == 1:
                    t_pf_last_odd = t_pf
                V[("prep", fb)] = t_pf
                t_pf_prev = t_pf
                fb_prev = fb
            t_hmm = em.pe(lambda t: t.matmul(
                P6[0:16, 0:tc], sel128, pF[:, fb_prev, :],
                start=False, stop=False), waits=[t_pf_prev])
            V[("pF", fb_prev)] = t_hmm
            V["ps1"] = t_pf_last_odd
            for half in range(2):
                t_hmm = em.pe(lambda t, half=half: t.matmul(
                    P6[0:16, 0:tc], b1T[:, tree_idx, half, :], p8[:, half, :],
                    start=False, stop=(half == 1)), waits=last(w_b1))
            t_h = em.act(lambda s: s.activation(
                out=hsb, in_=P6[0:16, 0:tc], func=act_fn),
                waits=[t_hmm] + tv("hsb"))
            V["ps6"] = t_h
            t_hd = em.dma("sp", hD[ch], hsb, waits=[t_h])
            V["hsb"] = t_hd
            V["xb"] = [t_hmm]
            V["xf"] = [t_lg]
            V["p_cur"] = t_hmm
            V[("p8", 0)] = [t_p8d, t_hmm]
            V[("p8", 1)] = [t_p8d, t_hmm]
            t_ret = [t_hd, t_p8d]
        st.close()
        return t_ret

    def tree_L2(tree_idx, wname, t_pre, out_cb):
        st = contextlib.ExitStack()
        E = st.enter_context
        sfx = f"t{tree_idx}b"
        w2t = E(nc.sbuf_tensor("s2_w2_"+sfx, [128, 32, 1024], BF16)).ap()
        p8 = E(nc.sbuf_tensor("s2_p8b_"+sfx, [128, 2, tc], BF16)).ap()
        hsb = E(nc.sbuf_tensor("s2_hsb2_"+sfx, [16, tc], BF16)).ap()
        hrep = E(nc.sbuf_tensor("s2_hrep_"+sfx, [128, tc], BF16)).ap()
        Hbig = E(nc.sbuf_tensor("s2_Hbig_"+sfx, [128, 32, tc], BF16)).ap()
        w_w2 = em.dma("pool", w2t, io[wname].rearrange("(kc p) m -> p kc m", p=128),
                      waits=t_pre)
        V = {}

        def tv(*k):
            return last(*[V.get(x) for x in k])

        t_out = []
        for ch in range(nch):
            t_p8 = em.dma("sp", p8, p8D[ch].rearrange("(h p) T -> p h T", p=128),
                          waits=(t_pre if ch == 0 else []) + tv("p8"))
            t_hl = em.dma("sp", hsb, hD[ch],
                          waits=(t_pre if ch == 0 else []) + tv("hsb"))
            t_hr = em.pe(lambda t: t.matmul(P0[:, 0:tc], sel16T, hsb,
                                            start=True, stop=True),
                         waits=last(t_hl, *t_cs) + tv("ps0b"))
            t_hv = em.act(lambda s: s.activation(
                out=hrep, in_=P0[:, 0:tc], func=AF.Copy),
                waits=[t_hr] + tv("hrep"))
            V["ps0b"] = t_hv
            t_hbs = []
            t_hb = None
            for gg in range(32):
                half = gg // 16
                gi = gg % 16
                t_pr = em.pe(lambda t, gi=gi, half=half: t.matmul(
                    P1[:, 0:tc], Mg[:, gi, :], p8[:, half, :],
                    start=True, stop=True),
                    waits=[t_p8] + tv("ps1b") + t_cs)
                t_hb = em.dve(lambda v, gg=gg: v.tensor_tensor(
                    out=Hbig[:, gg, :], in0=P1[:, 0:tc], in1=hrep,
                    op=ALU.mult), waits=[t_pr, t_hv] + tv(("Hb", gg)))
                V["ps1b"] = t_hb
                t_hbs.append(t_hb)
            V["hrep"] = t_hb
            t_fin = None
            t_mm = None
            for mo in range(8):
                bank = P6 if mo % 2 == 0 else P7
                bkey = ("psy", mo % 2)
                for gg in range(32):
                    t_mm = em.pe(lambda t, mo=mo, gg=gg, bank=bank: t.matmul(
                        bank[:, 0:tc], w2t[:, gg, mo * 128:(mo + 1) * 128],
                        Hbig[:, gg, :], start=(gg == 0), stop=False),
                        waits=last(t_hbs[gg], w_w2) + (tv(bkey) if gg == 0 else []))
                for half in range(2):
                    t_mm = em.pe(lambda t, mo=mo, half=half, bank=bank: t.matmul(
                        bank[:, 0:tc],
                        b2T[:, tree_idx, half, mo * 128:(mo + 1) * 128],
                        p8[:, half, :], start=False, stop=(half == 1)),
                        waits=last(w_b2))
                t_fin = out_cb(ch, mo, bank[:, 0:tc], t_mm)
                V[bkey] = t_fin
                t_out.append(t_fin)
            for gg in range(32):
                V[("Hb", gg)] = t_mm
            V["p8"] = t_mm
            V["hsb"] = t_hr
        st.close()
        return t_out[-2:]


    qkv_st = []

    def mk_cb(dst, scale=None):
        def cb(ch, mo, ps, t_mm, dst=dst, scale=scale):
            fb = mo % 2
            if scale is None:
                t_e = em.act(lambda s, ch=ch, mo=mo, fb=fb: s.activation(
                    out=evq[:, fb, :], in_=ps, func=AF.Copy),
                    waits=[t_mm] + tu(("evq", fb)))
            else:
                t_e = em.dve(lambda v, fb=fb, ps=ps: v.tensor_scalar_mul(
                    evq[:, fb, :], ps, scale),
                    waits=[t_mm] + tu(("evq", fb)))
            t_d = em.dma("sp", dst[mo * 128:(mo + 1) * 128, ch * tc:(ch + 1) * tc],
                         evq[:, fb, :], waits=[t_e])
            U[("evq", fb)] = t_d
            qkv_st.append(t_d)
            return t_e
        return cb

    t_qh = tree_L1(0, xnB, xnF, AF.Relu, "q_w1T", t_done_xn + t_cs)
    t_q2 = tree_L2(0, "q_w2T", t_qh, mk_cb(qD, scale=1.0 / 32.0))
    t_kh = tree_L1(1, xnB, xnF, AF.Relu, "k_w1T", t_q2)
    t_k2 = tree_L2(1, "k_w2T", t_kh, mk_cb(kD))
    t_vh = tree_L1(2, xnB, xnF, AF.Relu, "v_w1T", t_k2)
    t_v2 = tree_L2(2, "v_w2T", t_vh, mk_cb(vD))

    stB = contextlib.ExitStack()
    E = stB.enter_context
    qsb = E(nc.sbuf_tensor("s2_qsb", [128, 8, cw], BF16)).ap()
    ksb = E(nc.sbuf_tensor("s2_ksb", [128, 8, seq], BF16)).ap()
    Pq = E(nc.sbuf_tensor("s2_Pq", [128, seq], BF16)).ap()
    PT = E(nc.sbuf_tensor("s2_PT", [128, nst, cw], BF16)).ap()
    vtm = E(nc.sbuf_tensor("s2_vtm", [128, nst, 1024], BF16)).ap()
    ncol = E(nc.sbuf_tensor("s2_ncol", [128, 4], F32)).ap()
    lcol = E(nc.sbuf_tensor("s2_lcol", [128, 4], F32)).ap()
    nlT = E(nc.sbuf_tensor("s2_nlT", [64, 128], F32)).ap()
    lrow = E(nc.sbuf_tensor("s2_lrow", [1, cw], F32)).ap()
    rl = E(nc.sbuf_tensor("s2_rl", [128, 2, cw], F32)).ap()
    xfl = E(nc.sbuf_tensor("s2_xfl", [128, 2, cw], F32)).ap()
    yfb = E(nc.sbuf_tensor("s2_yfb", [128, 2, cw], F32)).ap()
    ybb = E(nc.sbuf_tensor("s2_ybb", [128, 2, cw], BF16)).ap()
    t_att_done = []
    prevB = last(t_v2) + qkv_st
    nqt = cw // 128
    for b in range(2):
        tb0 = b * seq
        t_kl = em.dma("sp", ksb,
                      kD.rearrange("(dc p) T -> p dc T", p=128)[:, :, tb0:tb0 + seq],
                      waits=prevB + tu("ksb"))
        t_vts = []
        for dc in range(8):
            t_vl = em.dma("sp", Pq[:, 0:seq],
                          vD[dc * 128:(dc + 1) * 128, tb0:tb0 + seq],
                          waits=prevB + tu("Pq") + tu("vload"))
            for stt2 in range(nst):
                pT = pT0 if stt2 % 2 == 0 else pT1
                t_tr = em.pe(lambda t, stt2=stt2, pT=pT: t.transpose(
                    pT[:, 0:128], Pq[:, stt2 * 128:(stt2 + 1) * 128], ident_bf),
                    waits=[t_vl, t_ib] + tu(("psT", stt2 % 2)))
                t_ev = em.dve(lambda v, stt2=stt2, dc=dc, pT=pT: v.tensor_copy(
                    vtm[:, stt2, dc * 128:(dc + 1) * 128], pT[:, 0:128]),
                    waits=[t_tr] + tu("vtmw"))
                U[("psT", stt2 % 2)] = t_ev
                t_vts.append(t_ev)
            U["vload"] = t_vts[-nst:]
            U["Pq"] = t_vts[-nst:]
        prevB = []
        for qc in range(ncw):
            t_ql = em.dma("sp", qsb,
                          qD.rearrange("(dc p) T -> p dc T", p=128)[:, :, tb0 + qc * cw:tb0 + (qc + 1) * cw],
                          waits=tu("qsb"))
            t_trs = []
            t_ex = None
            for qt in range(nqt):
                t_s = None
                for stt in range(ncw):
                    for kc in range(8):
                        t_s = em.pe(lambda t, qt=qt, stt=stt, kc=kc: t.matmul(
                            SF[:, stt * cw:(stt + 1) * cw],
                            qsb[:, kc, qt * 128:(qt + 1) * 128],
                            ksb[:, kc, stt * cw:(stt + 1) * cw],
                            start=(kc == 0), stop=(kc == 7)),
                            waits=last(t_ql, t_kl) + (tu("psS") if kc == 0 and stt == 0 else []))
                t_nm = em.dve(lambda v, qt=qt: v.reduce_max(
                    out=ncol[:, qt:qt + 1], in_=SF[:, 0:seq],
                    axis=AX.X, negate=True),
                    waits=[t_s] + tu("ncol"))
                t_ex = em.act(lambda s, qt=qt: s.activation(
                    out=Pq[:, 0:seq], in_=SF[:, 0:seq],
                    func=AF.Exp, bias=ncol[:, qt:qt + 1],
                    accum_out=lcol[:, qt:qt + 1]),
                    waits=[t_nm] + tu("Pq"))
                U["psS"] = t_ex
                U["ncol"] = t_ex
                for stt2 in range(nst):
                    pT = pT0 if stt2 % 2 == 0 else pT1
                    t_tr = em.pe(lambda t, stt2=stt2, pT=pT: t.transpose(
                        pT[:, 0:128], Pq[:, stt2 * 128:(stt2 + 1) * 128], ident_bf),
                        waits=[t_ex, t_ib] + tu(("psT", stt2 % 2)))
                    t_ev = em.dve(lambda v, qt=qt, stt2=stt2, pT=pT: v.tensor_copy(
                        PT[:, stt2, qt * 128:(qt + 1) * 128], pT[:, 0:128]),
                        waits=[t_tr] + tu("PTw"))
                    U[("psT", stt2 % 2)] = t_ev
                    t_trs.append(t_ev)
                U["Pq"] = t_trs[-nst:]
            U["qsb"] = t_ex
            t_t2 = em.pe(lambda t: t.matmul(P1[0:nqt, 0:128], lcol[:, 0:nqt], identf,
                                            start=True, stop=True),
                         waits=[t_ex, t_i1] + tu("psS1"))
            t_e2 = em.dve(lambda v: v.tensor_copy(nlT[32:32 + nqt, :], P1[0:nqt, 0:128]),
                          waits=[t_t2] + tu("nlT"))
            U["psS1"] = t_e2
            t_r2 = em.dma("sp", lrow[0:1, 0:cw], nlT[32:32 + nqt, :],
                          waits=[t_e2] + tu("lrow"))
            t_rc = em.dve(lambda v: v.reciprocal(lrow[0:1, 0:cw], lrow[0:1, 0:cw]),
                          waits=[t_r2])
            U["nlT"] = [t_rc]
            fb = qc % 2
            t_rlrep = em.pe(lambda t: t.matmul(
                P2[:, 0:cw], ones_row, lrow[0:1, 0:cw],
                start=True, stop=True), waits=[t_rc, t_or] + tu("ps2c"))
            t_rle = em.act(lambda s, fb=fb: s.activation(
                out=rl[:, fb, :], in_=P2[:, 0:cw], func=AF.Copy),
                waits=[t_rlrep] + tu(("rl", fb)))
            U["ps2c"] = t_rle
            U["lrow"] = t_rle
            t_av = None
            for vd in range(8):
                bank = P6 if vd % 2 == 0 else P7
                bkey = ("psav", vd % 2)
                for stt2 in range(nst):
                    t_av = em.pe(lambda t, vd=vd, stt2=stt2, bank=bank: t.matmul(
                        bank[:, 0:cw], vtm[:, stt2, vd * 128:(vd + 1) * 128],
                        PT[:, stt2, 0:cw],
                        start=(stt2 == 0), stop=(stt2 == nst - 1)),
                        waits=last(t_vts[-1], t_trs[-1]) + (tu(bkey) if stt2 == 0 else []))
                t_xl = em.dma("sp", xfl[:, fb, :],
                              xnF[vd * 128:(vd + 1) * 128, tb0 + qc * cw:tb0 + (qc + 1) * cw],
                              waits=tu(("xfl", fb)))
                t_m = em.dve(lambda v, fb=fb, bank=bank: v.tensor_tensor(
                    out=yfb[:, fb, :], in0=bank[:, 0:cw], in1=rl[:, fb, :],
                    op=ALU.mult), waits=[t_av, t_rle] + tu(("yfb", fb)))
                U[bkey] = t_m
                t_y = em.dve(lambda v, fb=fb: v.tensor_tensor(
                    out=yfb[:, fb, :], in0=yfb[:, fb, :], in1=xfl[:, fb, :],
                    op=ALU.add), waits=[t_m, t_xl])
                U[("rl", fb)] = t_y
                U[("xfl", fb)] = t_y
                t_yc = em.dve(lambda v, fb=fb: v.tensor_copy(
                    ybb[:, fb, :], yfb[:, fb, :]), waits=[t_y] + tu(("ybb", fb)))
                t_d1 = em.dma("sp", yF[vd * 128:(vd + 1) * 128, tb0 + qc * cw:tb0 + (qc + 1) * cw],
                              yfb[:, fb, :], waits=[t_yc])
                t_d2 = em.dma("sp", yB[vd * 128:(vd + 1) * 128, tb0 + qc * cw:tb0 + (qc + 1) * cw],
                              ybb[:, fb, :], waits=[t_yc])
                U[("yfb", fb)] = t_d1
                U[("ybb", fb)] = t_d2
                t_att_done += [t_d1, t_d2]
            U["PTw"] = t_av
        U["vtmw"] = t_av
        U["ksb"] = t_av
        t_att_done = t_att_done[-8:]
    stB.close()

    t_fh = tree_L1(3, yB, yF, AF.Gelu, "f_w1T", t_att_done + t_cs)
    stW.close()

    stC = contextlib.ExitStack()
    E = stC.enter_context
    outE = E(nc.sbuf_tensor("s2_outE", [128, 8, tc], F32)).ap()
    yfc = E(nc.sbuf_tensor("s2_yfc", [128, 8, tc], F32)).ap()
    r1c = E(nc.sbuf_tensor("s2_r1c", [128, 8, tc], F32)).ap()
    sqc = E(nc.sbuf_tensor("s2_sqc", [128, tc], F32)).ap()
    rowc = E(nc.sbuf_tensor("s2_rowc", [1, 4, tc], F32)).ap()
    erot = E(nc.sbuf_tensor("s2_erot", [128, 2, tc], F32)).ap()
    tail_state = {}

    def tail_cb(ch, mo, ps, t_mm):
        t0 = ch * tc
        S = tail_state
        if mo == 0:
            S["yl"] = em.dma("sp", yfc,
                             yF.rearrange("(dc p) T -> p dc T", p=128)[:, :, t0:t0 + tc],
                             waits=tu("yfc"))
            S["rl"] = em.dma("sp", r1c,
                             R1.rearrange("(dc p) T -> p dc T", p=128)[:, :, t0:t0 + tc],
                             waits=tu("r1c"))
            S["m"] = {}
        t_o = em.dve(lambda v, mo=mo: v.tensor_tensor(
            out=outE[:, mo, :], in0=ps, in1=yfc[:, mo, :], op=ALU.add),
            waits=[t_mm, S["yl"]] + tu(("oE", mo)))
        t_sq = em.act(lambda s, mo=mo: s.activation(
            out=sqc, in_=outE[:, mo, :], func=AF.Square),
            waits=[t_o] + tu("sqc"))
        t_m1 = em.pe(lambda t, mo=mo: t.matmul(
            P2[0:1, 0:tc], ones_col, outE[:, mo, :],
            start=(mo == 0), stop=(mo == 7)),
            waits=[t_o, t_oc] + (tu("psr2") if mo == 0 else []))
        t_m2 = em.pe(lambda t, mo=mo: t.matmul(
            P3[0:1, 0:tc], ones_col, sqc,
            start=(mo == 0), stop=(mo == 7)),
            waits=[t_sq] + (tu("psr3") if mo == 0 else []))
        U["sqc"] = t_m2
        S["m"][mo] = (t_m1, t_m2, t_o)
        if mo < 7:
            return t_o
        mu = rowc[:, 0, :]
        vv = rowc[:, 1, :]
        rr = rowc[:, 2, :]
        bb = rowc[:, 3, :]
        t_mu = em.dve(lambda v: v.tensor_scalar_mul(mu, P2[0:1, 0:tc], 1.0 / DM),
                      waits=[t_m1] + tu("rowc"))
        t_q = em.act(lambda s: s.activation(out=vv, in_=mu, func=AF.Square),
                     waits=[t_mu])
        t_v = em.dve(lambda v: v.scalar_tensor_tensor(
            out=vv, in0=P3[0:1, 0:tc], scalar=1.0 / DM, in1=vv,
            op0=ALU.mult, op1=ALU.subtract), waits=[t_m2, t_q])
        t_1 = em.dve(lambda v: v.tensor_scalar_add(rr, vv, 1e-5), waits=[t_v])
        t_2 = em.act(lambda s: s.activation(out=rr, in_=rr, func=AF.Sqrt),
                     waits=[t_1])
        t_3 = em.dve(lambda v: v.reciprocal(rr, rr), waits=[t_2])
        t_4 = em.dve(lambda v: v.tensor_tensor(out=bb, in0=mu, in1=rr, op=ALU.mult),
                     waits=[t_3])
        t_5 = em.dve(lambda v: v.tensor_scalar_mul(bb, bb, -1.0), waits=[t_4])
        U["psr2"] = [t_v, t_mu]
        U["psr3"] = [t_v]
        U["rowc"] = t_5
        t_rr = em.pe(lambda t: t.matmul(P0[:, 0:tc], ones_row, rr,
                                        start=True, stop=True),
                     waits=last(t_3, t_or) + tu("psr0"))
        t_br = em.pe(lambda t: t.matmul(P1[:, 0:tc], ones_row, bb,
                                        start=True, stop=True),
                     waits=last(t_5, t_or) + tu("psr1"))
        t_last = None
        ta = None
        tb = None
        for mo2 in range(8):
            rb = mo2 % 2
            ta = em.dve(lambda v, mo2=mo2, rb=rb: v.tensor_tensor(
                out=erot[:, rb, :], in0=outE[:, mo2, :], in1=P0[:, 0:tc],
                op=ALU.mult),
                waits=[t_rr, S["m"][mo2][2]] + tu(("ero", rb)))
            tb = em.dve(lambda v, rb=rb: v.tensor_tensor(
                out=erot[:, rb, :], in0=erot[:, rb, :], in1=P1[:, 0:tc],
                op=ALU.add), waits=[t_br, ta])
            tcv = em.dve(lambda v, mo2=mo2, rb=rb: v.tensor_tensor(
                out=erot[:, rb, :], in0=erot[:, rb, :], in1=r1c[:, mo2, :],
                op=ALU.add), waits=[tb, S["rl"]])
            U[("oE", mo2)] = [ta, S["m"][mo2][0], S["m"][mo2][1]]
            t_st = em.dma("sp", R2[mo2 * 128:(mo2 + 1) * 128, t0:t0 + tc],
                          erot[:, rb, :], waits=[tcv])
            U[("ero", rb)] = t_st
            t_last = t_st
        U["psr0"] = ta
        U["psr1"] = tb
        U["yfc"] = ta
        U["r1c"] = tcv
        return t_last

    t_f2 = tree_L2(3, "f_w2T", t_fh, tail_cb)
    stC.close()
    stS.close()
    return t_f2


def host_consts_stage2():
    scm = np.zeros((128, 8, 2, 256), np.float32)
    for k in range(8):
        for c in (0, 1):
            for j in range(2 ** k):
                scm[j, k, c, 2 * j + c] = 1.0
    Mg = np.zeros((128, 16, 128), np.float32)
    for gi in range(16):
        for j in range(8):
            for l in range(16):
                Mg[8 * gi + j, gi, 16 * j + l] = 1.0
    sel16T = np.zeros((16, 128), np.float32)
    sel128 = np.zeros((128, 16), np.float32)
    for j in range(8):
        for l in range(16):
            sel16T[l, 16 * j + l] = 1.0
            sel128[16 * j + l, l] = 1.0
    return {"ident": np.eye(128, dtype=np.float32), "scm": scm, "Mg": Mg,
            "sel16T": sel16T, "sel128": sel128}


def host_prep_stage2(z, core, seq=2048):
    d = dict(host_consts_stage2())
    trees = ["q", "k", "v", "f"]
    d["nwT4"] = np.ascontiguousarray(np.stack(
        [z[f"L0_ffn_{t}_nw"].T for t in trees], axis=1).astype(np.float32))
    d["nb4"] = np.ascontiguousarray(np.stack(
        [z[f"L0_ffn_{t}_nb"] for t in trees], axis=0)[None].astype(np.float32))
    d["b14"] = np.ascontiguousarray(np.stack(
        [z[f"L0_ffn_{t}_b1"] for t in trees], axis=0).astype(np.float32))
    d["b24"] = np.ascontiguousarray(np.stack(
        [z[f"L0_ffn_{t}_b2"] for t in trees], axis=0).astype(np.float32))
    for t in trees:
        d[f"{t}_w1T"] = np.ascontiguousarray(
            z[f"L0_ffn_{t}_w1"].transpose(1, 0, 2).reshape(1024, 4096).astype(np.float32))
        d[f"{t}_w2T"] = np.ascontiguousarray(
            z[f"L0_ffn_{t}_w2"].reshape(4096, 1024).astype(np.float32))
    return d


def build_stage2_program(seq=2048, tc=512):
    nc = bass.Bass("TRN2", target_bir_lowering=False)
    ntok = 2 * seq
    io = {}
    io["R1"] = nc.dram_tensor("R1", [DM, ntok], F32, kind="ExternalInput").ap()
    io["ident"] = nc.dram_tensor("ident", [128, 128], F32, kind="ExternalInput").ap()
    io["scm"] = nc.dram_tensor("scm", [128, 8, 2, 256], F32, kind="ExternalInput").ap()
    io["Mg"] = nc.dram_tensor("Mg", [128, 16, 128], F32, kind="ExternalInput").ap()
    io["sel16T"] = nc.dram_tensor("sel16T", [16, 128], F32, kind="ExternalInput").ap()
    io["sel128"] = nc.dram_tensor("sel128", [128, 16], F32, kind="ExternalInput").ap()
    io["nwT4"] = nc.dram_tensor("nwT4", [1024, 4, 255], F32, kind="ExternalInput").ap()
    io["nb4"] = nc.dram_tensor("nb4", [1, 4, 255], F32, kind="ExternalInput").ap()
    io["b14"] = nc.dram_tensor("b14", [4, 256, 16], F32, kind="ExternalInput").ap()
    io["b24"] = nc.dram_tensor("b24", [4, 256, 1024], F32, kind="ExternalInput").ap()
    for t in ["q", "k", "v", "f"]:
        io[f"{t}_w1T"] = nc.dram_tensor(f"{t}_w1T", [1024, 4096], F32,
                                        kind="ExternalInput").ap()
        io[f"{t}_w2T"] = nc.dram_tensor(f"{t}_w2T", [4096, 1024], F32,
                                        kind="ExternalInput").ap()
    io["R2"] = nc.dram_tensor("R2", [DM, ntok], F32, kind="ExternalOutput").ap()
    em = Em(nc)
    build_stage2(nc, em, io, seq=seq, tc=tc)
    return nc, em




_S3SUB = ("q", "k", "v")
_S3NMO = 8
_S3HACK = 0


def build_stage3(nc, em, io, seq=2048, tc=512, upto=None):
    sub = _S3SUB
    stS = contextlib.ExitStack()

    def A(name, shape, dt):
        return stS.enter_context(nc.sbuf_tensor(name, shape, dt))
    ntok = 2 * seq
    nch = ntok // tc
    nst = seq // 128
    ncw = seq // 512 if seq >= 512 else 1
    cw = min(512, seq)
    nqt = cw // 128

    R2 = io["R2"]
    R3 = io["R3"]

    xbD = nc.dram_tensor("s3_xb", [DM, ntok], BF16, kind="Internal").ap()
    qroD = nc.dram_tensor("s3_qro", [DM, ntok], BF16, kind="Internal").ap()
    krD = nc.dram_tensor("s3_kr", [DM, ntok], BF16, kind="Internal").ap()
    vD3 = nc.dram_tensor("s3_v", [512, ntok], BF16, kind="Internal").ap()
    avD = nc.dram_tensor("s3_av", [512, ntok], BF16, kind="Internal").ap()

    PSALL = stS.enter_context(nc.psum_tensor("s3_psall", [128, 4, 512], F32)).ap()
    P0 = PSALL[:, 0, :]
    P1 = PSALL[:, 1, :]
    P2 = PSALL[:, 2, :]
    P3 = PSALL[:, 3, :]
    pT0 = stS.enter_context(nc.psum_tensor("s3_pT0", [128, 1024], BF16)).ap()
    pT1 = stS.enter_context(nc.psum_tensor("s3_pT1", [128, 1024], BF16)).ap()
    P6 = stS.enter_context(nc.psum_tensor("s3_p6", [128, 512], F32)).ap()
    P7 = stS.enter_context(nc.psum_tensor("s3_p7", [128, 512], F32)).ap()
    SF = PSALL.rearrange("p a b -> p (a b)")

    ones_col = A("s3_onesc", [128, 1], F32).ap()
    ones_row = A("s3_onesr", [1, 128], F32).ap()
    identf = A("s3_identf", [128, 128], F32).ap()
    ident_bf = A("s3_identb", [128, 128], BF16).ap()
    seg2c = A("s3_seg2c", [128, 8], F32).ap()
    kmask = A("s3_kmask", [128, 8], F32).ap()
    mTd = A("s3_mTd", [128, 128], F32).ap()
    mUp = A("s3_mUp", [128, 128], F32).ap()
    qg = A("s3_qg", [128, 8], F32).ap()
    kg = A("s3_kg", [128, 8], F32).ap()
    wq = A("s3_wq", [128, 8, 1024], BF16).ap()
    wkv = A("s3_wkv", [128, 8, 128], BF16).ap()
    wk = A("s3_wk", [128, 1024], BF16).ap()
    wv = A("s3_wv", [128, 512], BF16).ap()
    wo = A("s3_wo", [128, 4, 1024], BF16).ap()

    t_oc = em.pool(lambda g: g.memset(ones_col, 1.0))
    t_or = em.pool(lambda g: g.memset(ones_row, 1.0))
    t_cs = [t_oc, t_or]
    t_cs.append(em.dma("sp", identf, io["ident"]))
    t_cs.append(em.dve(lambda v: v.tensor_copy(ident_bf, identf), waits=[t_cs[-1]]))
    t_s2c = em.dma("sp", seg2c, io["seg2c"])
    t_s4c = em.dma("sp", kmask, io["kmask"])
    t_cs += [t_s2c, t_s4c]
    t_cs.append(em.dma("sp", mTd, io["mTd"]))
    t_mup = em.dma("sp", mUp, io["mUp"])
    t_cs.append(t_mup)
    t_qg = em.dma("sp", qg, io["qg"])
    t_kg = em.dma("sp", kg, io["kg"])
    t_cs += [t_qg, t_kg]
    t_wq = em.dma("pool", wq, io["Wq"].rearrange("(kc p) m -> p kc m", p=128))
    t_wkv = em.dma("pool", wkv, io["Wkv"].rearrange("(kc p) m -> p kc m", p=128))
    t_wk = em.dma("pool", wk, io["WkP"])
    t_wv = em.dma("pool", wv, io["Wv"])
    t_wo = em.dma("pool", wo, io["WoS"].rearrange("(kc p) m -> p kc m", p=128))

    U = {}

    def tu(*k):
        return last(*[U.get(x) for x in k])

    st0 = contextlib.ExitStack()
    E0 = st0.enter_context
    rch = E0(nc.sbuf_tensor("s3_rch", [128, 8, tc], F32)).ap()
    sq = E0(nc.sbuf_tensor("s3_sq", [128, tc], F32)).ap()
    xnc = E0(nc.sbuf_tensor("s3_xnc", [128, 8, tc], BF16)).ap()
    rows = E0(nc.sbuf_tensor("s3_rows", [1, 4, tc], F32)).ap()
    kvc = E0(nc.sbuf_tensor("s3_kvc", [128, tc], BF16)).ap()
    qsb = E0(nc.sbuf_tensor("s3_qsbA", [128, 2, tc], F32)).ap()
    rms = E0(nc.sbuf_tensor("s3_rms", [1, 8, 2 * tc], F32)).ap()
    evb = E0(nc.sbuf_tensor("s3_evb", [128, 2, tc], BF16)).ap()

    t_proj_done = []
    for ch in range(nch):
        t0 = ch * tc
        t_r = em.dma("sp", rch,
                     R2.rearrange("(dc p) T -> p dc T", p=128)[:, :, t0:t0 + tc],
                     waits=tu("rch"))
        t_m1 = None
        t_m2 = None
        for dc in range(8):
            t_m1 = em.pe(lambda t, dc=dc: t.matmul(
                P6[0:1, 0:tc], ones_col, rch[:, dc, :],
                start=(dc == 0), stop=(dc == 7)),
                waits=last(t_r, t_oc) + (tu("ps67") if dc == 0 else []))
            t_sq = em.act(lambda s, dc=dc: s.activation(
                out=sq, in_=rch[:, dc, :], func=AF.Square),
                waits=[t_r] + tu("sq"))
            t_m2 = em.pe(lambda t, dc=dc: t.matmul(
                P7[0:1, 0:tc], ones_col, sq,
                start=(dc == 0), stop=(dc == 7)),
                waits=[t_sq] + (tu("ps67") if dc == 0 else []))
            U["sq"] = t_m2
        mu = rows[:, 0, :]
        vv = rows[:, 1, :]
        rr = rows[:, 2, :]
        bb = rows[:, 3, :]
        t_mu = em.dve(lambda v: v.tensor_scalar_mul(mu, P6[0:1, 0:tc], 1.0 / DM),
                      waits=[t_m1] + tu("rows"))
        t_q = em.act(lambda s: s.activation(out=vv, in_=mu, func=AF.Square),
                     waits=[t_mu])
        t_v = em.dve(lambda v: v.scalar_tensor_tensor(
            out=vv, in0=P7[0:1, 0:tc], scalar=1.0 / DM, in1=vv,
            op0=ALU.mult, op1=ALU.subtract), waits=[t_m2, t_q])
        t_1 = em.dve(lambda v: v.tensor_scalar_add(rr, vv, 1e-5), waits=[t_v])
        t_2 = em.act(lambda s: s.activation(out=rr, in_=rr, func=AF.Sqrt), waits=[t_1])
        t_3 = em.dve(lambda v: v.reciprocal(rr, rr), waits=[t_2])
        t_4 = em.dve(lambda v: v.tensor_tensor(out=bb, in0=mu, in1=rr, op=ALU.mult),
                     waits=[t_3])
        t_5 = em.dve(lambda v: v.tensor_scalar_mul(bb, bb, -1.0), waits=[t_4])
        U["ps67"] = [t_v, t_mu]
        U["rows"] = t_5
        t_sr = em.pe(lambda t: t.matmul(P2[:, 0:tc], ones_row, rr,
                                        start=True, stop=True),
                     waits=last(t_3, t_or) + tu("ps2"))
        t_br = em.pe(lambda t: t.matmul(P3[:, 0:tc], ones_row, bb,
                                        start=True, stop=True),
                     waits=last(t_5, t_or) + tu("ps3"))
        t_xns = []
        for dc in range(8):
            ta = em.dve(lambda v, dc=dc: v.tensor_tensor(
                out=sq, in0=rch[:, dc, :], in1=P2[:, 0:tc], op=ALU.mult),
                waits=[t_sr] + tu("sq"))
            tb = em.dve(lambda v, dc=dc: v.tensor_tensor(
                out=xnc[:, dc, :], in0=sq, in1=P3[:, 0:tc], op=ALU.add),
                waits=[t_br, ta] + tu("xnc"))
            U["sq"] = tb
            t_xns.append(tb)
        U["ps2"] = t_xns[-1]
        U["ps3"] = t_xns[-1]
        U["rch"] = [t_m1, t_xns[-1]]
        t_xd = em.dma("sp", xbD.rearrange("(dc p) T -> p dc T", p=128)[:, :, t0:t0 + tc],
                      xnc, waits=t_xns + tu("xbD"))
        U["xnc"] = t_xd
        U["xbD"] = t_xd

        for mo in (range(_S3NMO) if 'q' in sub else []):
            qb = 0 if _S3HACK == 1 else mo % 2
            bank = P6 if qb == 0 else P7
            t_mm = None
            for kc in range(8):
                t_mm = em.pe(lambda t, mo=mo, kc=kc, bank=bank: t.matmul(
                    bank[:, 0:tc], wq[:, kc, mo * 128:(mo + 1) * 128],
                    xnc[:, kc, :], start=(kc == 0), stop=(kc == 7)),
                    waits=last(t_xns[kc], t_wq) + (tu(("psq", qb)) if kc == 0 else []))
            t_ev = em.dve(lambda v, qb=qb, bank=bank: v.tensor_copy(
                qsb[:, qb, :], bank[:, 0:tc]), waits=[t_mm] + tu(("qsb", qb)))
            t_sq2 = em.act(lambda s, qb=qb, bank=bank: s.activation(
                out=sq, in_=bank[:, 0:tc], func=AF.Square),
                waits=[t_mm, t_ev] + tu("sq"))
            t_rm = None
            for hh in range(2):
                t_rm = em.pe(lambda t, hh=hh: t.matmul(
                    SF[0:1, hh * tc:(hh + 1) * tc], seg2c[:, hh:hh + 1],
                    sq, start=(hh == 0 or tc == 512),
                    stop=(hh == 1 or tc == 512)),
                    waits=([t_sq2, t_s2c] + tu("ps01")) if hh == 0 else [t_sq2])
            U[("psq", qb)] = [t_ev, t_sq2]
            t_r1 = em.dve(lambda v, mo=mo: v.tensor_scalar(
                out=rms[0:1, mo, :], in0=SF[0:1, 0:2 * tc], scalar1=1.0 / 64,
                scalar2=1e-6, op0=ALU.mult, op1=ALU.add),
                waits=[t_rm] + tu(("rms", mo)))
            t_r2 = em.act(lambda s, mo=mo: s.activation(
                out=rms[0:1, mo, :], in_=rms[0:1, mo, :], func=AF.Sqrt),
                waits=[t_r1])
            t_r3 = em.dve(lambda v, mo=mo: v.tensor_scalar_mul(
                rms[0:1, mo, :], rms[0:1, mo, :], 8.0), waits=[t_r2])
            t_r4 = em.dve(lambda v, mo=mo: v.reciprocal(
                rms[0:1, mo, :], rms[0:1, mo, :]), waits=[t_r3])
            U["ps01"] = t_r1
            U["sq"] = t_rm
            t_rpA = em.pe(lambda t, mo=mo: t.matmul(
                P2[:, 0:tc], ones_row, rms[0:1, mo, 0:tc],
                start=True, stop=True), waits=[t_r4, t_or] + tu("ps2"))
            t_rpB = em.pe(lambda t, mo=mo: t.matmul(
                P3[:, 0:tc], ones_row, rms[0:1, mo, tc:2 * tc],
                start=True, stop=True), waits=[t_r4, t_or] + tu("ps3"))
            U[("rms", mo)] = [t_rpA, t_rpB]
            t_qn0 = em.dve(lambda v, qb=qb: v.scalar_tensor_tensor(
                out=evb[0:64, qb, :], in0=qsb[0:64, qb, :], scalar=qg[0:64, 0:1],
                in1=P2[0:64, 0:tc], op0=ALU.mult, op1=ALU.mult),
                waits=[t_rpA, t_ev, t_qg] + tu(("evb", qb)))
            t_qn = em.dve(lambda v, qb=qb: v.scalar_tensor_tensor(
                out=evb[64:128, qb, :], in0=qsb[64:128, qb, :], scalar=qg[64:128, 0:1],
                in1=P3[64:128, 0:tc], op0=ALU.mult, op1=ALU.mult),
                waits=[t_rpB, t_ev, t_qg, t_qn0])
            U["ps2"] = t_qn0
            U["ps3"] = t_qn
            U[("qsb", qb)] = t_qn
            t_qd = em.dma("sp", qroD[mo * 128:(mo + 1) * 128, t0:t0 + tc],
                          evb[:, qb, :], waits=[t_qn])
            U[("evb", qb)] = t_qd
            t_proj_done.append(t_qd)

        t_kv = None
        for kc in range(8):
            t_kv = em.pe(lambda t, kc=kc: t.matmul(
                P0[:, 0:tc], wkv[:, kc, :], xnc[:, kc, :],
                start=(kc == 0), stop=(kc == 7)),
                waits=last(t_xns[kc], t_wkv) + (tu("ps0") if kc == 0 else []))
        t_kve = em.dve(lambda v: v.tensor_copy(kvc, P0[:, 0:tc]),
                       waits=[t_kv] + tu("kvc"))
        U["ps0"] = t_kve
        for mo in (range(8) if 'k' in sub else []):
            qb = mo % 2
            bank = P6 if qb == 0 else P7
            t_km = em.pe(lambda t, mo=mo, bank=bank: t.matmul(
                bank[:, 0:tc], wk[:, mo * 128:(mo + 1) * 128], kvc,
                start=True, stop=True),
                waits=last(t_kve, t_wk) + tu(("psq", qb)))
            t_ev = em.dve(lambda v, qb=qb, bank=bank: v.tensor_copy(
                qsb[:, qb, :], bank[:, 0:tc]), waits=[t_km] + tu(("qsb", qb)))
            t_sq2 = em.act(lambda s, bank=bank: s.activation(
                out=sq, in_=bank[:, 0:tc], func=AF.Square),
                waits=[t_km, t_ev] + tu("sq"))
            t_rm = None
            for hh in range(2):
                t_rm = em.pe(lambda t, hh=hh: t.matmul(
                    SF[0:1, hh * tc:(hh + 1) * tc], kmask[:, hh:hh + 1],
                    sq, start=(hh == 0 or tc == 512),
                    stop=(hh == 1 or tc == 512)),
                    waits=([t_sq2, t_s4c] + tu("ps01")) if hh == 0 else [t_sq2])
            U[("psq", qb)] = [t_ev, t_sq2]
            t_r1 = em.dve(lambda v, mo=mo: v.tensor_scalar(
                out=rms[0:1, mo, :], in0=SF[0:1, 0:2 * tc], scalar1=1.0 / 32,
                scalar2=1e-6, op0=ALU.mult, op1=ALU.add),
                waits=[t_rm] + tu(("rms", mo)))
            t_r2 = em.act(lambda s, mo=mo: s.activation(
                out=rms[0:1, mo, :], in_=rms[0:1, mo, :], func=AF.Sqrt),
                waits=[t_r1])
            t_r4 = em.dve(lambda v, mo=mo: v.reciprocal(
                rms[0:1, mo, :], rms[0:1, mo, :]), waits=[t_r2])
            U["ps01"] = t_r1
            U["sq"] = t_rm
            t_rpA = em.pe(lambda t, mo=mo: t.matmul(
                P2[:, 0:tc], ones_row, rms[0:1, mo, 0:tc],
                start=True, stop=True), waits=[t_r4, t_or] + tu("ps2"))
            t_rpB = em.pe(lambda t, mo=mo: t.matmul(
                P3[:, 0:tc], ones_row, rms[0:1, mo, tc:2 * tc],
                start=True, stop=True), waits=[t_r4, t_or] + tu("ps3"))
            U[("rms", mo)] = [t_rpA, t_rpB]
            t_kn0 = em.dve(lambda v, qb=qb: v.scalar_tensor_tensor(
                out=evb[0:64, qb, :], in0=qsb[0:64, qb, :], scalar=kg[0:64, 0:1],
                in1=P2[0:64, 0:tc], op0=ALU.mult, op1=ALU.mult),
                waits=[t_rpA, t_ev, t_kg] + tu(("evb", qb)))
            t_kn = em.dve(lambda v, qb=qb: v.scalar_tensor_tensor(
                out=evb[64:128, qb, :], in0=qsb[64:128, qb, :], scalar=kg[64:128, 0:1],
                in1=P3[64:128, 0:tc], op0=ALU.mult, op1=ALU.mult),
                waits=[t_rpB, t_ev, t_kg, t_kn0])
            U["ps2"] = t_kn0
            U["ps3"] = t_kn
            U[("qsb", qb)] = t_kn
            t_kd = em.dma("sp", krD[mo * 128:(mo + 1) * 128, t0:t0 + tc],
                          evb[:, qb, :], waits=[t_kn])
            t_proj_done.append(t_kd)
            U[("evb", qb)] = t_kd
        for mo in (range(4) if 'v' in sub else []):
            qb = mo % 2
            bank = P6 if qb == 0 else P7
            t_vm = em.pe(lambda t, mo=mo, bank=bank: t.matmul(
                bank[:, 0:tc], wv[:, mo * 128:(mo + 1) * 128], kvc,
                start=True, stop=True),
                waits=last(t_kve, t_wv) + tu(("psq", qb)))
            t_ev = em.dve(lambda v, qb=qb, bank=bank: v.tensor_copy(
                evb[:, qb, :], bank[:, 0:tc]), waits=[t_vm] + tu(("evb", qb)))
            U[("psq", qb)] = t_ev
            t_vd = em.dma("sp", vD3[mo * 128:(mo + 1) * 128, t0:t0 + tc],
                          evb[:, qb, :], waits=[t_ev])
            U[("evb", qb)] = t_vd
            t_proj_done.append(t_vd)
        U["kvc"] = t_vm if 'v' in sub else (t_km if 'k' in sub else t_kve)
    st0.close()
    t_proj_done = t_proj_done[-8:]
    if upto == "A":
        return t_proj_done

    stB = contextlib.ExitStack()
    E = stB.enter_context
    qro = E(nc.sbuf_tensor("s3_qro_sb", [128, 8, seq], BF16)).ap()
    krs = E(nc.sbuf_tensor("s3_krs", [128, 8, seq], BF16)).ap()
    vstg = E(nc.sbuf_tensor("s3_vstg", [128, 4, seq], BF16)).ap()
    vtm = E(nc.sbuf_tensor("s3_vtm", [128, nst, 512], BF16)).ap()
    Pq = E(nc.sbuf_tensor("s3_Pq", [128, 2, seq], BF16)).ap()
    PT = E(nc.sbuf_tensor("s3_PT", [128, 2, nst, cw], BF16)).ap()
    ncol = E(nc.sbuf_tensor("s3_ncol", [128, 2, 16], F32)).ap()
    lcol = E(nc.sbuf_tensor("s3_lcol", [128, 2, 16], F32)).ap()
    nlT = E(nc.sbuf_tensor("s3_nlT", [64, 128], F32)).ap()
    lrow = E(nc.sbuf_tensor("s3_lrow", [1, cw], F32)).ap()
    rl32 = E(nc.sbuf_tensor("s3_rl32", [32, cw], F32)).ap()
    oc32 = E(nc.sbuf_tensor("s3_oc32", [32, 2, cw], BF16)).ap()
    rowD = nc.dram_tensor("s3_rowD", [16, 128], F32, kind="Internal").ap()

    t_att_done = []
    prevB = t_proj_done
    for b in range(2):
        tb0 = b * seq
        t_ql = em.dma("sp", qro,
                      qroD.rearrange("(dc p) T -> p dc T", p=128)[:, :, tb0:tb0 + seq],
                      waits=prevB + tu("qro"))
        t_kl = em.dma("sp", krs,
                      krD.rearrange("(dc p) T -> p dc T", p=128)[:, :, tb0:tb0 + seq],
                      waits=prevB + tu("krs"))
        t_vl = em.dma("sp", vstg,
                      vD3.rearrange("(dc p) T -> p dc T", p=128)[:, :, tb0:tb0 + seq],
                      waits=prevB + tu("vstg"))
        prevB = []
        t_vts = []
        for dc in range(4):
            for stt in range(nst):
                pT = pT0 if (dc * nst + stt) % 2 == 0 else pT1
                t_tr = em.pe(lambda t, dc=dc, stt=stt, pT=pT: t.transpose(
                    pT[:, 0:128], vstg[:, dc, stt * 128:(stt + 1) * 128], ident_bf),
                    waits=[t_vl] + t_cs[2:4] + tu(("psT", (dc * nst + stt) % 2)))
                t_ev = em.dve(lambda v, dc=dc, stt=stt, pT=pT: v.tensor_copy(
                    vtm[:, stt, dc * 128:(dc + 1) * 128], pT[:, 0:128]),
                    waits=[t_tr] + tu("vtmw"))
                U[("psT", (dc * nst + stt) % 2)] = t_ev
                t_vts.append(t_ev)
        U["vstg"] = t_vts[-1]
        if b == 0:
            U[("psav", 0)] = U.get(("psq", 0))
            U[("psav", 1)] = U.get(("psq", 1))
        for h in range(16):
            hb = (h * 64) % 128
            hc = (h * 64) // 128
            t_av = None
            for qc in range(ncw):
                pp = (h * ncw + qc) % 2
                bank6 = P6 if pp == 0 else P7
                q0 = qc * cw
                stmax = (qc + 1) * nqt
                t_pts = {}
                for qt in range(qc * nqt, (qc + 1) * nqt):
                    smax = (qt + 1) * 128
                    t_s = None
                    for sc in range((smax + 511) // 512):
                        w = min(512, smax - sc * 512)
                        bank_waits = []
                        if sc == 0:
                            bank_waits = tu("psS") + tu("psS0")
                        elif sc == 2:
                            bank_waits = tu("psS")
                        elif sc == 3:
                            bank_waits = tu("ps3av")
                        t_s = em.pe(lambda t, qt=qt, sc=sc, w=w, hb=hb, hc=hc: t.matmul(
                            SF[:, sc * 512:sc * 512 + w],
                            qro[hb:hb + 32, hc, qt * 128:(qt + 1) * 128],
                            krs[hb:hb + 32, hc, sc * 512:sc * 512 + w],
                            start=True, stop=True),
                            waits=last(t_ql, t_kl) + bank_waits)
                    t_mk = em.dve(lambda v, qt=qt: v.tensor_tensor(
                        out=SF[:, qt * 128:(qt + 1) * 128],
                        in0=SF[:, qt * 128:(qt + 1) * 128], in1=mUp,
                        op=ALU.add), waits=[t_s, t_mup] + tu("psSm"))
                    t_nm = em.dve(lambda v, qt=qt, smax=smax, pp=pp: v.reduce_max(
                        out=ncol[:, pp, qt:qt + 1], in_=SF[:, 0:smax],
                        axis=AX.X, negate=True),
                        waits=[t_mk] + tu(("ncol", pp)))
                    t_ex = em.act(lambda s, qt=qt, smax=smax, pp=pp: s.activation(
                        out=Pq[:, pp, 0:smax], in_=SF[:, 0:smax],
                        func=AF.Exp, bias=ncol[:, pp, qt:qt + 1],
                        accum_out=lcol[:, pp, qt:qt + 1]),
                        waits=[t_nm] + tu(("Pq", pp)) + tu(("lcolr", pp)))
                    U["psS"] = t_ex
                    U["psSm"] = t_ex
                    U[("ncol", pp)] = t_ex
                    t_trs_q = []
                    for stt in range(qt + 1):
                        pT = pT0 if stt % 2 == 0 else pT1
                        t_tr = em.pe(lambda t, stt=stt, pT=pT, pp=pp: t.transpose(
                            pT[:, 0:128], Pq[:, pp, stt * 128:(stt + 1) * 128], ident_bf),
                            waits=[t_ex] + t_cs[2:4] + tu(("psT", stt % 2)))
                        t_evp = em.dve(lambda v, qt=qt, stt=stt, pT=pT, q0=q0, pp=pp: v.tensor_copy(
                            PT[:, pp, stt, qt * 128 - q0:(qt + 1) * 128 - q0], pT[:, 0:128]),
                            waits=[t_tr] + tu(("PTw", pp)))
                        U[("psT", stt % 2)] = t_evp
                        t_trs_q.append(t_evp)
                        t_pts.setdefault(stt, []).append(t_evp)
                    U[("Pq", pp)] = t_trs_q
                for stt in range(stmax):
                    rel = stt - qc * nqt
                    if rel > 0:
                        t_pz = em.dve(lambda v, stt=stt, rel=rel, pp=pp: v.memset(
                            PT[:, pp, stt, 0:rel * 128], 0.0),
                            waits=tu(("PTw", pp)) + tu(("pzw", stt, pp)))
                        t_pts.setdefault(stt, []).append(t_pz)
                        U[("pzw", stt, pp)] = t_pz
                t_t2 = em.pe(lambda t, qc=qc, pp=pp: t.matmul(
                    P0[0:nqt, 0:128], lcol[:, pp, qc * nqt:(qc + 1) * nqt],
                    identf, start=True, stop=True),
                    waits=[t_ex, t_cs[2]] + tu("psS0"))
                U[("lcolr", pp)] = t_t2
                t_e2 = em.dve(lambda v: v.tensor_copy(nlT[32:32 + nqt, :],
                                                      P0[0:nqt, 0:128]),
                              waits=[t_t2] + tu("nlT"))
                U["psS0"] = t_e2
                t_d2 = em.dma("sp", rowD[0:nqt, :], nlT[32:32 + nqt, :],
                              waits=[t_e2] + tu("rowD"))
                t_l2 = em.dma("sp", lrow[0:1, 0:cw],
                              rowD[0:nqt, :].rearrange("a b -> (a b)").unsqueeze(0),
                              waits=[t_d2] + tu("lrow"))
                t_rc = em.dve(lambda v: v.reciprocal(lrow[0:1, 0:cw],
                                                     lrow[0:1, 0:cw]),
                              waits=[t_l2])
                U["nlT"] = t_rc
                U["rowD"] = t_rc
                for stt in range(stmax):
                    t_av = em.pe(lambda t, stt=stt, h=h, stmax=stmax, pp=pp, bank6=bank6: t.matmul(
                        bank6[0:32, 0:cw], vtm[:, stt, h * 32:(h + 1) * 32],
                        PT[:, pp, stt, 0:cw],
                        start=(stt == 0), stop=(stt == stmax - 1)),
                        waits=last(*t_pts.get(stt, [])) + [t_vts[-1]]
                        + (tu(("psav", pp)) if stt == 0 else []))
                t_rr = em.pe(lambda t: t.matmul(
                    P3[0:32, 0:cw], ones_row[0:1, 0:32], lrow[0:1, 0:cw],
                    start=True, stop=True),
                    waits=[t_rc, t_or] + tu("ps3av"))
                t_rle = em.act(lambda s: s.activation(
                    out=rl32, in_=P3[0:32, 0:cw], func=AF.Copy),
                    waits=[t_rr] + tu("rl32"))
                U["ps3av"] = t_rle
                ob = pp
                t_o = em.dve(lambda v, ob=ob, bank6=bank6: v.tensor_tensor(
                    out=oc32[:, ob, :], in0=bank6[0:32, 0:cw], in1=rl32,
                    op=ALU.mult), waits=[t_av, t_rle] + tu(("oc", ob)))
                U[("psav", pp)] = t_o
                U["rl32"] = t_o
                t_od = em.dma("sp", avD[h * 32:(h + 1) * 32, tb0 + q0:tb0 + q0 + cw],
                              oc32[:, ob, :], waits=[t_o])
                U[("oc", ob)] = t_od
                t_att_done.append(t_od)
                U["lrow"] = [t_rr]
                U[("PTw", pp)] = t_av
        U["qro"] = t_av
        U["krs"] = t_av
        U["vtmw"] = t_av
        t_att_done = t_att_done[-2:]
    U[("psq", 0)] = U.get(("psav", 0))
    U[("psq", 1)] = U.get(("psav", 1))
    stB.close()

    stC = contextlib.ExitStack()
    E = stC.enter_context
    avc = E(nc.sbuf_tensor("s3_avc", [128, 4, tc], BF16)).ap()
    outE = E(nc.sbuf_tensor("s3_outE", [128, 8, tc], F32)).ap()
    r2c = E(nc.sbuf_tensor("s3_r2c", [128, 8, tc], F32)).ap()
    sqc = E(nc.sbuf_tensor("s3_sqc", [128, tc], F32)).ap()
    rowc = E(nc.sbuf_tensor("s3_rowc", [1, 4, tc], F32)).ap()
    erot = E(nc.sbuf_tensor("s3_erot", [128, 2, tc], F32)).ap()
    t_ret = []
    for ch in range(nch):
        t0 = ch * tc
        t_al = em.dma("sp", avc,
                      avD.rearrange("(dc p) T -> p dc T", p=128)[:, :, t0:t0 + tc],
                      waits=(t_att_done if ch == 0 else []) + tu("avc"))
        t_rl = em.dma("sp", r2c,
                      R2.rearrange("(dc p) T -> p dc T", p=128)[:, :, t0:t0 + tc],
                      waits=tu("r2c"))
        t_m1 = None
        t_m2 = None
        t_os = []
        for mo in range(8):
            qb = mo % 2
            bank = P6 if qb == 0 else P7
            t_mm = None
            for kc in range(4):
                t_mm = em.pe(lambda t, mo=mo, kc=kc, bank=bank: t.matmul(
                    bank[:, 0:tc], wo[:, kc, mo * 128:(mo + 1) * 128],
                    avc[:, kc, :], start=(kc == 0), stop=(kc == 3)),
                    waits=last(t_al, t_wo) + (tu(("psq", qb)) if kc == 0 else []))
            t_o = em.dve(lambda v, mo=mo, bank=bank: v.tensor_copy(
                outE[:, mo, :], bank[:, 0:tc]),
                waits=[t_mm] + tu(("oE", mo)))
            t_os.append(t_o)
            U[("psq", qb)] = t_o
            t_sq = em.act(lambda s, mo=mo: s.activation(
                out=sqc, in_=outE[:, mo, :], func=AF.Square),
                waits=[t_o] + tu("sqc"))
            t_m1 = em.pe(lambda t, mo=mo: t.matmul(
                P0[0:1, 0:tc], ones_col, outE[:, mo, :],
                start=(mo == 0), stop=(mo == 7)),
                waits=[t_o, t_oc] + (tu("psr0") if mo == 0 else []))
            t_m2 = em.pe(lambda t, mo=mo: t.matmul(
                P1[0:1, 0:tc], ones_col, sqc,
                start=(mo == 0), stop=(mo == 7)),
                waits=[t_sq] + (tu("psr1") if mo == 0 else []))
            U["sqc"] = t_m2
        mu = rowc[:, 0, :]
        vv = rowc[:, 1, :]
        rr = rowc[:, 2, :]
        bb = rowc[:, 3, :]
        t_mu = em.dve(lambda v: v.tensor_scalar_mul(mu, P0[0:1, 0:tc], 1.0 / DM),
                      waits=[t_m1] + tu("rowc"))
        t_q = em.act(lambda s: s.activation(out=vv, in_=mu, func=AF.Square),
                     waits=[t_mu])
        t_v = em.dve(lambda v: v.scalar_tensor_tensor(
            out=vv, in0=P1[0:1, 0:tc], scalar=1.0 / DM, in1=vv,
            op0=ALU.mult, op1=ALU.subtract), waits=[t_m2, t_q])
        t_1 = em.dve(lambda v: v.tensor_scalar_add(rr, vv, 1e-5), waits=[t_v])
        t_2 = em.act(lambda s: s.activation(out=rr, in_=rr, func=AF.Sqrt), waits=[t_1])
        t_3 = em.dve(lambda v: v.reciprocal(rr, rr), waits=[t_2])
        t_4 = em.dve(lambda v: v.tensor_tensor(out=bb, in0=mu, in1=rr, op=ALU.mult),
                     waits=[t_3])
        t_5 = em.dve(lambda v: v.tensor_scalar_mul(bb, bb, -1.0), waits=[t_4])
        U["psr0"] = [t_v, t_mu]
        U["psr1"] = [t_v]
        U["rowc"] = t_5
        t_rr2 = em.pe(lambda t: t.matmul(P2[:, 0:tc], ones_row, rr,
                                         start=True, stop=True),
                      waits=last(t_3, t_or) + tu("psr2"))
        t_br = em.pe(lambda t: t.matmul(P3[:, 0:tc], ones_row, bb,
                                        start=True, stop=True),
                     waits=last(t_5, t_or) + tu("psr3"))
        t_last = None
        ta = None
        tb = None
        for mo in range(8):
            rb = mo % 2
            ta = em.dve(lambda v, mo=mo, rb=rb: v.tensor_tensor(
                out=erot[:, rb, :], in0=outE[:, mo, :], in1=P2[:, 0:tc],
                op=ALU.mult), waits=[t_rr2, t_os[mo]] + tu(("ero", rb)))
            tb = em.dve(lambda v, rb=rb: v.tensor_tensor(
                out=erot[:, rb, :], in0=erot[:, rb, :], in1=P3[:, 0:tc],
                op=ALU.add), waits=[t_br, ta])
            tcv = em.dve(lambda v, mo=mo, rb=rb: v.tensor_tensor(
                out=erot[:, rb, :], in0=erot[:, rb, :], in1=r2c[:, mo, :],
                op=ALU.add), waits=[tb, t_rl])
            U[("oE", mo)] = [ta, t_m1, t_m2]
            t_st = em.dma("sp", R3[mo * 128:(mo + 1) * 128, t0:t0 + tc],
                          erot[:, rb, :], waits=[tcv])
            U[("ero", rb)] = t_st
            t_last = t_st
        U["psr2"] = ta
        U["psr3"] = tb
        U["avc"] = t_mm
        U["r2c"] = tcv
        t_ret = [t_last]
    stC.close()
    stS.close()
    return t_ret


def host_prep_stage3(z, core, seq=2048):
    mTd = np.zeros((128, 128), np.float32)
    for p in range(128):
        mTd[p, :p] = -1e30
    mUp = np.ascontiguousarray(mTd.T)
    seg2c = np.zeros((128, 8), np.float32)
    seg2c[:64, 0] = 1; seg2c[64:, 1] = 1
    kmask = np.zeros((128, 8), np.float32)
    kmask[0:32, 0] = 1; kmask[64:96, 1] = 1
    Wk = z["L1_attn_Wk"].astype(np.float32)
    WkP = np.zeros((128, 1024), np.float32)
    for h in range(16):
        WkP[:, h * 64:h * 64 + 32] = Wk[:, h * 32:(h + 1) * 32]
    kgp = np.zeros(128, np.float32)
    kgp[0:32] = z["L1_attn_k_g"].astype(np.float32)
    kgp[64:96] = z["L1_attn_k_g"].astype(np.float32)
    Wo = z["L1_attn_Wo"].astype(np.float32)
    WoS = Wo.reshape(16, 64, 1024)[:, :32, :].reshape(512, 1024)
    return {
        "ident": np.eye(128, dtype=np.float32),
        "seg2c": seg2c, "kmask": kmask,
        "mTd": mTd, "mUp": mUp,
        "qg": np.ascontiguousarray(np.tile(np.tile(z["L1_attn_q_g"].astype(np.float32), 2)[:, None], (1, 8))),
        "kg": np.ascontiguousarray(np.tile(kgp[:, None], (1, 8))),
        "Wq": np.ascontiguousarray(z["L1_attn_Wq"].astype(np.float32)),
        "Wkv": np.ascontiguousarray(z["L1_attn_Wkv"].astype(np.float32)),
        "WkP": np.ascontiguousarray(WkP),
        "Wv": np.ascontiguousarray(z["L1_attn_Wv"].astype(np.float32)),
        "WoS": np.ascontiguousarray(WoS),
    }


def build_stage3_program(seq=2048, tc=512, upto=None):
    nc = bass.Bass("TRN2", target_bir_lowering=False)
    ntok = 2 * seq
    io = {}
    io["R2"] = nc.dram_tensor("R2", [DM, ntok], F32, kind="ExternalInput").ap()
    for name, shape in [
        ("ident", [128, 128]), ("seg2c", [128, 8]),
        ("kmask", [128, 8]), ("mTd", [128, 128]), ("mUp", [128, 128]),
        ("qg", [128, 8]), ("kg", [128, 8]),
        ("Wq", [1024, 1024]), ("Wkv", [1024, 128]), ("WkP", [128, 1024]),
        ("Wv", [128, 512]), ("WoS", [512, 1024]),
    ]:
        io[name] = nc.dram_tensor(name, shape, F32, kind="ExternalInput").ap()
    io["R3"] = nc.dram_tensor("R3", [DM, ntok], F32, kind="ExternalOutput").ap()
    em = Em(nc)
    build_stage3(nc, em, io, seq=seq, tc=tc, upto=upto)
    return nc, em




def build_stage4(nc, em, io, seq=2048, tc=512):
    stS = contextlib.ExitStack()

    def A(name, shape, dt):
        return stS.enter_context(nc.sbuf_tensor(name, shape, dt))
    ntok = 2 * seq
    nch = ntok // tc

    R3 = io["R3"]
    OUT = io["OUT"]

    xbD = nc.dram_tensor("s4_xb", [DM, ntok], BF16, kind="Internal").ap()
    combD = nc.dram_tensor("s4_comb", [8, ntok], F32, kind="Internal").ap()
    accD = nc.dram_tensor("s4_acc", [DM, ntok], F32, kind="Internal").ap()

    PSALL = stS.enter_context(nc.psum_tensor("s4_psall", [128, 4, 512], F32)).ap()
    P0 = PSALL[:, 0, :]
    P1 = PSALL[:, 1, :]
    P2 = PSALL[:, 2, :]
    P3 = PSALL[:, 3, :]
    PA = stS.enter_context(nc.psum_tensor("s4_pa", [128, 512], F32)).ap()
    PB = stS.enter_context(nc.psum_tensor("s4_pb", [128, 512], F32)).ap()
    P6 = stS.enter_context(nc.psum_tensor("s4_p6", [128, 512], F32)).ap()
    P7 = stS.enter_context(nc.psum_tensor("s4_p7", [128, 512], F32)).ap()

    ones_col = A("s4_onesc", [128, 1], F32).ap()
    ones_row = A("s4_onesr", [1, 128], F32).ap()
    ones_bf = A("s4_onesb", [1, 512], BF16).ap()
    identf = A("s4_identf", [128, 128], F32).ap()
    iota8 = A("s4_iota8", [128, 8], F32).ap()
    rbt = A("s4_rbt", [128, 8], F32).ap()
    rwT = A("s4_rwT", [128, 8, 8], F32).ap()
    b1sb = A("s4_b1", [128, 9, 32], F32).ap()
    b2sp = A("s4_b2sp", [8, 1024], BF16).ap()
    sb2 = A("s4_sb2", [1, 1024], BF16).ap()

    t_oc = em.pool(lambda g: g.memset(ones_col, 1.0))
    t_or = em.pool(lambda g: g.memset(ones_row, 1.0))
    t_ob = em.pool(lambda g: g.memset(ones_bf, 1.0))
    t_cs = [t_oc, t_or, t_ob]
    t_cs.append(em.dma("sp", identf, io["ident"]))
    t_id = t_cs[-1]
    t_cs.append(em.dma("sp", iota8, io["iota8"]))
    t_io8 = t_cs[-1]
    t_cs.append(em.dma("sp", rbt, io["rb_tiled"]))
    t_rbt = t_cs[-1]
    t_cs.append(em.dma("sp", rwT, io["rwT"].rearrange("(kc p) m -> p kc m", p=128)))
    t_rw = t_cs[-1]
    t_cs.append(em.dma("sp", b1sb.rearrange("p e mc -> p (e mc)"), io["b1all"]))
    t_b1 = t_cs[-1]
    t_cs.append(em.dma("pool", b2sp, io["spec_b2"]))
    t_b2 = t_cs[-1]
    t_cs.append(em.dma("pool", sb2, io["shared_b2"]))
    t_sb2 = t_cs[-1]

    U = {}

    def tu(*k):
        return last(*[U.get(x) for x in k])

    st0 = contextlib.ExitStack()
    E0 = st0.enter_context
    rch = E0(nc.sbuf_tensor("s4_rch", [128, 8, tc], F32)).ap()
    sq = E0(nc.sbuf_tensor("s4_sq", [128, tc], F32)).ap()
    xnf = E0(nc.sbuf_tensor("s4_xnf", [128, 8, tc], F32)).ap()
    xnb = E0(nc.sbuf_tensor("s4_xnb", [128, 8, tc], BF16)).ap()
    rows = E0(nc.sbuf_tensor("s4_rows", [1, 4, tc], F32)).ap()
    lgsb = E0(nc.sbuf_tensor("s4_lgsb", [8, tc], F32)).ap()
    ltm = E0(nc.sbuf_tensor("s4_ltm", [128, 10, 8], F32)).ap()
    idxf = E0(nc.sbuf_tensor("s4_idxf", [128, 8], mybir.dt.uint32)).ap()
    m8 = E0(nc.sbuf_tensor("s4_m8", [128, 8], F32)).ap()
    rr2 = E0(nc.sbuf_tensor("s4_rr2", [128, 4], F32)).ap()
    combtm = E0(nc.sbuf_tensor("s4_combtm", [128, 8], F32)).ap()
    combf = E0(nc.sbuf_tensor("s4_combf", [8, tc], F32)).ap()

    t_ln_done = []
    for ch in range(nch):
        t0 = ch * tc
        t_r = em.dma("sp", rch,
                     R3.rearrange("(dc p) T -> p dc T", p=128)[:, :, t0:t0 + tc],
                     waits=tu("rch"))
        t_m1 = None
        t_m2 = None
        for dc in range(8):
            t_m1 = em.pe(lambda t, dc=dc: t.matmul(
                P6[0:1, 0:tc], ones_col, rch[:, dc, :],
                start=(dc == 0), stop=(dc == 7)),
                waits=last(t_r, t_oc) + (tu("ps67") if dc == 0 else []))
            t_sq = em.act(lambda s, dc=dc: s.activation(
                out=sq, in_=rch[:, dc, :], func=AF.Square),
                waits=[t_r] + tu("sq"))
            t_m2 = em.pe(lambda t, dc=dc: t.matmul(
                P7[0:1, 0:tc], ones_col, sq,
                start=(dc == 0), stop=(dc == 7)),
                waits=[t_sq] + (tu("ps67") if dc == 0 else []))
            U["sq"] = t_m2
        mu = rows[:, 0, :]
        vv = rows[:, 1, :]
        rrw = rows[:, 2, :]
        bb = rows[:, 3, :]
        t_mu = em.dve(lambda v, mu=mu: v.tensor_scalar_mul(
            mu, P6[0:1, 0:tc], 1.0 / DM), waits=[t_m1] + tu("rows"))
        t_q = em.act(lambda s, mu=mu, vv=vv: s.activation(
            out=vv, in_=mu, func=AF.Square), waits=[t_mu])
        t_v = em.dve(lambda v, vv=vv: v.scalar_tensor_tensor(
            out=vv, in0=P7[0:1, 0:tc], scalar=1.0 / DM, in1=vv,
            op0=ALU.mult, op1=ALU.subtract), waits=[t_m2, t_q])
        t_1 = em.dve(lambda v, vv=vv, rrw=rrw: v.tensor_scalar_add(rrw, vv, 1e-5),
                     waits=[t_v])
        t_2 = em.act(lambda s, rrw=rrw: s.activation(
            out=rrw, in_=rrw, func=AF.Sqrt), waits=[t_1])
        t_3 = em.dve(lambda v, rrw=rrw: v.reciprocal(rrw, rrw), waits=[t_2])
        t_4 = em.dve(lambda v, mu=mu, rrw=rrw, bb=bb: v.tensor_tensor(
            out=bb, in0=mu, in1=rrw, op=ALU.mult), waits=[t_3])
        t_5 = em.dve(lambda v, bb=bb: v.tensor_scalar_mul(bb, bb, -1.0), waits=[t_4])
        U["ps67"] = [t_v, t_mu]
        U["rows"] = t_5
        t_sr = em.pe(lambda t, rrw=rrw: t.matmul(P2[:, 0:tc], ones_row, rrw,
                                                 start=True, stop=True),
                     waits=last(t_3, t_or) + tu("ps2"))
        t_br = em.pe(lambda t, bb=bb: t.matmul(P3[:, 0:tc], ones_row, bb,
                                               start=True, stop=True),
                     waits=last(t_5, t_or) + tu("ps3"))
        t_xns = []
        for dc in range(8):
            ta = em.dve(lambda v, dc=dc: v.tensor_tensor(
                out=xnf[:, dc, :], in0=rch[:, dc, :], in1=P2[:, 0:tc],
                op=ALU.mult), waits=[t_sr] + tu("xnf"))
            tb = em.dve(lambda v, dc=dc: v.tensor_tensor(
                out=xnf[:, dc, :], in0=xnf[:, dc, :], in1=P3[:, 0:tc],
                op=ALU.add), waits=[t_br, ta])
            tcv = em.dve(lambda v, dc=dc: v.tensor_copy(
                xnb[:, dc, :], xnf[:, dc, :]), waits=[tb] + tu("xnb"))
            t_xns.append(tcv)
        U["ps2"] = t_xns[-1]
        U["ps3"] = t_xns[-1]
        U["rch"] = [t_m1, t_xns[-1]]
        t_xd = em.dma("sp", xbD.rearrange("(dc p) T -> p dc T", p=128)[:, :, t0:t0 + tc],
                      xnb, waits=t_xns + tu("xbd"))
        U["xnb"] = t_xd
        U["xbd"] = t_xd
        t_ln_done = [t_xd]
        t_lg = None
        for kc in range(8):
            t_lg = em.pe(lambda t, kc=kc: t.matmul(
                P6[0:8, 0:tc], rwT[:, kc, :], xnf[:, kc, :],
                start=(kc == 0), stop=(kc == 7)),
                waits=last(t_xns[kc], t_rw) + (tu("ps67") if kc == 0 else []))
        t_le = em.dve(lambda v: v.tensor_copy(lgsb, P6[0:8, 0:tc]),
                      waits=[t_lg] + tu("lgsb"))
        U["ps67"] = t_le
        U["xnf"] = t_lg
        t_cd = None
        for qq in range(tc // 128):
            t_t = em.pe(lambda t, qq=qq: t.matmul(
                P0[:, 0:8], lgsb[0:8, qq * 128:(qq + 1) * 128], identf[0:8, 0:8],
                start=True, stop=True),
                waits=[t_le, t_id] + tu("ps0r"))
            t_lt = em.dve(lambda v, qq=qq: v.tensor_copy(
                ltm[:, qq, :], P0[:, 0:8]), waits=[t_t] + tu(("ltm", qq)))
            U["ps0r"] = t_lt
            t_ab = em.dve(lambda v, qq=qq: v.tensor_tensor(
                out=ltm[:, qq, :], in0=ltm[:, qq, :], in1=rbt, op=ALU.add),
                waits=[t_lt, t_rbt])
            t_mx = em.dve(lambda v, qq=qq: v.reduce_max(
                out=rr2[:, 0:1], in_=ltm[:, qq, :], axis=AX.X, negate=True),
                waits=[t_ab] + tu("rr2"))
            t_ex = em.act(lambda s, qq=qq: s.activation(
                out=ltm[:, qq, :], in_=ltm[:, qq, :], func=AF.Exp,
                bias=rr2[:, 0:1], accum_out=rr2[:, 1:2]), waits=[t_mx])
            t_rc = em.dve(lambda v: v.reciprocal(rr2[:, 1:2], rr2[:, 1:2]),
                          waits=[t_ex])
            t_sm = em.dve(lambda v, qq=qq: v.tensor_scalar(
                out=ltm[:, qq, :], in0=ltm[:, qq, :], scalar1=rr2[:, 1:2],
                scalar2=None, op0=ALU.mult), waits=[t_rc])
            t_m8 = em.dve(lambda v, qq=qq: v.max(m8, ltm[:, qq, :]),
                          waits=[t_sm] + tu("m8"))
            t_ix = em.dve(lambda v, qq=qq: v.max_index(idxf, m8, ltm[:, qq, :]),
                          waits=[t_m8] + tu("idxf"))
            t_s1 = em.dve(lambda v: v.tensor_tensor(
                out=rr2[:, 2:3], in0=m8[:, 0:1], in1=m8[:, 1:2], op=ALU.add),
                waits=[t_m8])
            t_s2 = em.dve(lambda v: v.reciprocal(rr2[:, 2:3], rr2[:, 2:3]),
                          waits=[t_s1])
            t_w1 = em.dve(lambda v: v.tensor_tensor(
                out=rr2[:, 0:1], in0=m8[:, 0:1], in1=rr2[:, 2:3], op=ALU.mult),
                waits=[t_s2])
            t_w2 = em.dve(lambda v: v.tensor_tensor(
                out=rr2[:, 1:2], in0=m8[:, 1:2], in1=rr2[:, 2:3], op=ALU.mult),
                waits=[t_s2, t_w1])
            t_if = em.dve(lambda v: v.tensor_copy(m8[:, 2:4], idxf[:, 0:2]),
                          waits=[t_ix, t_w2])
            U["idxf"] = t_if
            t_c1 = em.dve(lambda v, qq=qq: v.tensor_scalar(
                out=combtm, in0=iota8, scalar1=m8[:, 2:3], scalar2=rr2[:, 0:1],
                op0=ALU.is_equal, op1=ALU.mult), waits=[t_if, t_io8] + tu("combtm"))
            t_c2 = em.dve(lambda v, qq=qq: v.tensor_scalar(
                out=ltm[:, 8, :], in0=iota8, scalar1=m8[:, 3:4],
                scalar2=rr2[:, 1:2], op0=ALU.is_equal, op1=ALU.mult),
                waits=[t_if, t_c1] + tu("ltm8"))
            t_c3 = em.dve(lambda v, qq=qq: v.tensor_tensor(
                out=combtm, in0=combtm, in1=ltm[:, 8, :], op=ALU.add),
                waits=[t_c2])
            U["ltm8"] = t_c3
            U["m8"] = t_c3
            U["rr2"] = t_c3
            t_tb = em.pe(lambda t, qq=qq: t.matmul(
                P1[0:8, 0:128], combtm, identf, start=True, stop=True),
                waits=[t_c3, t_id] + tu("ps1r"))
            t_ce = em.dve(lambda v, qq=qq: v.tensor_copy(
                combf[0:8, qq * 128:(qq + 1) * 128], P1[0:8, 0:128]),
                waits=[t_tb] + tu("combf"))
            U["ps1r"] = t_ce
            U[("ltm", qq)] = t_c1
            U["combtm"] = t_tb
            t_cd = t_ce
        t_cD = em.dma("sp", combD[:, t0:t0 + tc], combf[0:8, 0:tc], waits=[t_cd])
        U["combf"] = t_cD
        U["lgsb"] = t_lg if False else t_ab
        t_ln_done.append(t_cD)
    st0.close()
    F1 = em.frontier()

    stB = contextlib.ExitStack()
    E = stB.enter_context
    w1sb = E(nc.sbuf_tensor("s4_w1", [128, 8, 4096], BF16)).ap()
    w2sb = E(nc.sbuf_tensor("s4_w2", [128, 32, 1024], BF16)).ap()
    xbc = E(nc.sbuf_tensor("s4_xbc", [128, 8, tc], BF16)).ap()
    Hsb = E(nc.sbuf_tensor("s4_H", [128, 32, tc], BF16)).ap()
    crow = E(nc.sbuf_tensor("s4_crow", [1, tc], F32)).ap()
    cb8 = E(nc.sbuf_tensor("s4_cb8", [8, tc], BF16)).ap()
    crep = E(nc.sbuf_tensor("s4_crep", [128, tc], F32)).ap()
    otile = E(nc.sbuf_tensor("s4_ot", [128, 2, tc], F32)).ap()

    prevA = t_ln_done
    t_4a = t_ln_done
    acc_fin = {}
    for ei in range(9):
        e = ei - 1
        if ei == 0:
            t_w1 = em.dma("pool", w1sb,
                          io["sw1T"].rearrange("(kc p) m -> p kc m", p=128),
                          waits=F1 + tu("w1sb"))
            t_w2 = em.dma("pool", w2sb,
                          io["sw2T"].rearrange("(kc p) m -> p kc m", p=128),
                          waits=tu("w2sb"))
        else:
            t_w1 = em.dma("pool", w1sb,
                          io["sp1T"][e].rearrange("(kc p) m -> p kc m", p=128),
                          waits=tu("w1sb"))
            t_w2 = em.dma("pool", w2sb,
                          io["sp2T"][e].rearrange("(kc p) m -> p kc m", p=128),
                          waits=tu("w2sb"))
        for ch in range(nch):
            t0 = ch * tc
            t_x = em.dma("sp", xbc,
                         xbD.rearrange("(dc p) T -> p dc T", p=128)[:, :, t0:t0 + tc],
                         waits=(F1 if ei == 0 and ch == 0 else []) + tu("xbc"))
            if ei > 0:
                t_cr = em.dma("sp", crow, combD[e:e + 1, t0:t0 + tc],
                              waits=(t_4a if ei == 1 and ch == 0 else []) + tu("crow"))
                t_rp = em.pe(lambda t: t.matmul(P3[:, 0:tc], ones_row, crow,
                                                start=True, stop=True),
                             waits=[t_cr, t_or] + tu("ps3b"))
                t_cre = em.act(lambda s: s.activation(
                    out=crep, in_=P3[:, 0:tc], func=AF.Copy),
                    waits=[t_rp] + tu("crep"))
                U["ps3b"] = t_cre
            if ei == 0:
                t_c8 = em.dma("pool", cb8, combD[:, t0:t0 + tc],
                              waits=prevA + tu("cb8"))
            t_hs = []
            for m in range(32):
                bank = P6 if m % 2 == 0 else P7
                t_mm = None
                for kc in range(8):
                    t_mm = em.pe(lambda t, m=m, kc=kc, bank=bank: t.matmul(
                        bank[:, 0:tc], w1sb[:, kc, m * 128:(m + 1) * 128],
                        xbc[:, kc, :], start=(kc == 0), stop=(kc == 7)),
                        waits=last(t_x, t_w1) + (tu(("psh", m % 2)) if kc == 0 else [])
                        + (F1 if ei == 0 and ch == 0 and m == 0 and kc == 0 else []))
                t_g = em.act(lambda s, m=m, ei=ei, bank=bank: s.activation(
                    out=Hsb[:, m, :], in_=bank[:, 0:tc], func=AF.Gelu_apprx_tanh,
                    bias=b1sb[:, ei, m:m + 1]),
                    waits=last(t_mm, t_b1) + tu(("H", m))
                    + (F1 if ei == 0 and ch == 0 and m == 0 else []))
                U[("psh", m % 2)] = t_g
                if ei > 0:
                    t_sc = em.dve(lambda v, m=m: v.tensor_tensor(
                        out=Hsb[:, m, :], in0=Hsb[:, m, :], in1=crep,
                        op=ALU.mult), waits=[t_g, t_cre])
                    t_hs.append(t_sc)
                else:
                    t_hs.append(t_g)
            U["crep"] = t_hs[-1]
            for mo in range(8):
                ob = mo % 2
                bank = PA if ob == 0 else PB
                t_mo = None
                for kc in range(32):
                    t_mo = em.pe(lambda t, mo=mo, kc=kc, bank=bank, ei=ei: t.matmul(
                        bank[:, 0:tc], w2sb[:, kc, mo * 128:(mo + 1) * 128],
                        Hsb[:, kc, :], start=(kc == 0),
                        stop=(ei > 0 and kc == 31)),
                        waits=last(t_hs[kc], t_w2) + (tu(("pso", ob)) if kc == 0 else []))
                if ei == 0:
                    t_mo = em.pe(lambda t, mo=mo, bank=bank: t.matmul(
                        bank[:, 0:tc], b2sp[:, mo * 128:(mo + 1) * 128],
                        cb8, start=False, stop=False),
                        waits=last(t_c8, t_b2))
                    t_mo = em.pe(lambda t, mo=mo, bank=bank: t.matmul(
                        bank[:, 0:tc], sb2[0:1, mo * 128:(mo + 1) * 128],
                        ones_bf[0:1, 0:tc], start=False, stop=True),
                        waits=last(t_sb2, t_ob))
                t_oe = em.dve(lambda v, mo=mo, ob=ob, bank=bank: v.tensor_copy(
                    otile[:, ob, :], bank[:, 0:tc]),
                    waits=[t_mo] + tu(("ot", ob))
                    + (F1 if ei == 0 and ch == 0 and mo == 0 else []))
                U[("pso", ob)] = t_oe
                t_ad = em.dma("pool", accD[mo * 128:(mo + 1) * 128, t0:t0 + tc],
                              otile[:, ob, :], waits=[t_oe] + tu(("accw", mo, ch)),
                              accum_op=(ALU.bypass if ei == 0 else ALU.add))
                U[("ot", ob)] = t_ad
                U[("accw", mo, ch)] = t_ad
                acc_fin.setdefault(ch, []).append(t_ad)
                if len(acc_fin[ch]) > 8:
                    acc_fin[ch] = acc_fin[ch][-8:]
            U["xbc"] = t_mo
            U["cb8"] = t_mo
            U["crow"] = t_rp if ei > 0 else None
            for m in range(32):
                U[("H", m)] = t_mo
        U["w1sb"] = t_mo
        U["w2sb"] = t_mo
        prevA = []
    stB.close()
    F2 = em.frontier()

    stC = contextlib.ExitStack()
    E = stC.enter_context
    acc = E(nc.sbuf_tensor("s4_accc", [128, 8, tc], F32)).ap()
    r3c = E(nc.sbuf_tensor("s4_r3c", [128, 8, tc], F32)).ap()
    sqc = E(nc.sbuf_tensor("s4_sqc", [128, tc], F32)).ap()
    rowc = E(nc.sbuf_tensor("s4_rowc", [1, 4, tc], F32)).ap()
    erot = E(nc.sbuf_tensor("s4_erot", [128, 8, tc], F32)).ap()
    trE = E(nc.sbuf_tensor("s4_trE", [128, 2, 128], F32)).ap()
    OUTf = OUT.rearrange("b t d -> (b t) d")
    t_ret = []
    for ch in range(nch):
        t0 = ch * tc
        t_al = em.dma("sp", acc,
                      accD.rearrange("(dc p) T -> p dc T", p=128)[:, :, t0:t0 + tc],
                      waits=(F2 if ch == 0 else []) + tu("accc"))
        t_rl = em.dma("sp", r3c,
                      R3.rearrange("(dc p) T -> p dc T", p=128)[:, :, t0:t0 + tc],
                      waits=tu("r3c"))
        t_m1 = None
        t_m2 = None
        for dc in range(8):
            t_m1 = em.pe(lambda t, dc=dc: t.matmul(
                P6[0:1, 0:tc], ones_col, acc[:, dc, :],
                start=(dc == 0), stop=(dc == 7)),
                waits=last(t_al, t_oc) + (tu("psr67") if dc == 0 else [])
                + (F2 if ch == 0 and dc == 0 else []))
            t_sq = em.act(lambda s, dc=dc: s.activation(
                out=sqc, in_=acc[:, dc, :], func=AF.Square),
                waits=[t_al] + tu("sqc")
                + (F2 if ch == 0 and dc == 0 else []))
            t_m2 = em.pe(lambda t, dc=dc: t.matmul(
                P7[0:1, 0:tc], ones_col, sqc,
                start=(dc == 0), stop=(dc == 7)),
                waits=[t_sq] + (tu("psr67") if dc == 0 else []))
            U["sqc"] = t_m2
        mu = rowc[:, 0, :]
        vv = rowc[:, 1, :]
        rrw = rowc[:, 2, :]
        bb = rowc[:, 3, :]
        t_mu = em.dve(lambda v, mu=mu: v.tensor_scalar_mul(
            mu, P6[0:1, 0:tc], 1.0 / DM),
            waits=[t_m1] + tu("rowc") + (F2 if ch == 0 else []))
        t_q = em.act(lambda s, mu=mu, vv=vv: s.activation(
            out=vv, in_=mu, func=AF.Square), waits=[t_mu])
        t_v = em.dve(lambda v, vv=vv: v.scalar_tensor_tensor(
            out=vv, in0=P7[0:1, 0:tc], scalar=1.0 / DM, in1=vv,
            op0=ALU.mult, op1=ALU.subtract), waits=[t_m2, t_q])
        t_1 = em.dve(lambda v, vv=vv, rrw=rrw: v.tensor_scalar_add(rrw, vv, 1e-5),
                     waits=[t_v])
        t_2 = em.act(lambda s, rrw=rrw: s.activation(
            out=rrw, in_=rrw, func=AF.Sqrt), waits=[t_1])
        t_3 = em.dve(lambda v, rrw=rrw: v.reciprocal(rrw, rrw), waits=[t_2])
        t_4 = em.dve(lambda v, mu=mu, rrw=rrw, bb=bb: v.tensor_tensor(
            out=bb, in0=mu, in1=rrw, op=ALU.mult), waits=[t_3])
        t_5 = em.dve(lambda v, bb=bb: v.tensor_scalar_mul(bb, bb, -1.0), waits=[t_4])
        U["psr67"] = [t_v, t_mu]
        U["rowc"] = t_5
        t_sr = em.pe(lambda t, rrw=rrw: t.matmul(P2[:, 0:tc], ones_row, rrw,
                                                 start=True, stop=True),
                     waits=last(t_3, t_or) + tu("psr2"))
        t_br = em.pe(lambda t, bb=bb: t.matmul(P3[:, 0:tc], ones_row, bb,
                                               start=True, stop=True),
                     waits=last(t_5, t_or) + tu("psr3"))
        t_es = []
        ta = None
        tb = None
        for dc in range(8):
            ta = em.dve(lambda v, dc=dc: v.tensor_tensor(
                out=erot[:, dc, :], in0=acc[:, dc, :], in1=P2[:, 0:tc],
                op=ALU.mult), waits=[t_sr] + tu(("ero", dc)))
            tb = em.dve(lambda v, dc=dc: v.tensor_tensor(
                out=erot[:, dc, :], in0=erot[:, dc, :], in1=P3[:, 0:tc],
                op=ALU.add), waits=[t_br, ta])
            tcv = em.dve(lambda v, dc=dc: v.tensor_tensor(
                out=erot[:, dc, :], in0=erot[:, dc, :], in1=r3c[:, dc, :],
                op=ALU.add), waits=[tb, t_rl])
            t_es.append(tcv)
        U["psr2"] = ta
        U["psr3"] = tb
        U["accc"] = [t_m1, t_es[-1]]
        U["r3c"] = t_es[-1]
        for tt in range(tc // 128):
            for dc in range(8):
                bank = P0 if dc % 2 == 0 else P1
                t_tr = em.pe(lambda t, tt=tt, dc=dc, bank=bank: t.matmul(
                    bank[:, 0:128], erot[:, dc, tt * 128:(tt + 1) * 128], identf,
                    start=True, stop=True),
                    waits=[t_es[dc], t_id] + tu(("psf", dc % 2)))
                t_te = em.dve(lambda v, dc=dc, bank=bank: v.tensor_copy(
                    trE[:, dc % 2, :], bank[:, 0:128]),
                    waits=[t_tr] + tu(("trE", dc % 2)))
                U[("psf", dc % 2)] = t_te
                t_td = em.dma("sp", OUTf[t0 + tt * 128:t0 + (tt + 1) * 128,
                                         dc * 128:(dc + 1) * 128],
                              trE[:, dc % 2, :], waits=[t_te])
                U[("trE", dc % 2)] = t_td
                t_ret.append(t_td)
        for dc in range(8):
            U[("ero", dc)] = t_ret[-1]
        t_ret = t_ret[-2:]
    stC.close()
    stS.close()
    return t_ret


def host_prep_stage4(z, core, seq=2048):
    b1all = np.concatenate([z["L1_ffn_shared_b1"][None].astype(np.float32),
                            z["L1_ffn_spec_b1"].astype(np.float32)], axis=0)
    b1all = np.ascontiguousarray(b1all.reshape(9, 32, 128).transpose(2, 0, 1).reshape(128, 288))
    return {
        "ident": np.eye(128, dtype=np.float32),
        "iota8": np.ascontiguousarray(
            np.tile(np.arange(8, dtype=np.float32)[None, :], (128, 1))),
        "rb_tiled": np.ascontiguousarray(
            np.tile(z["L1_ffn_router_b"].astype(np.float32)[None, :], (128, 1))),
        "rwT": np.ascontiguousarray(z["L1_ffn_router_w"].astype(np.float32).T),
        "b1all": b1all,
        "spec_b2": np.ascontiguousarray(z["L1_ffn_spec_b2"].astype(np.float32)),
        "shared_b2": np.ascontiguousarray(z["L1_ffn_shared_b2"].astype(np.float32)[None, :]),
        "sw1T": np.ascontiguousarray(z["L1_ffn_shared_w1"].astype(np.float32).T),
        "sw2T": np.ascontiguousarray(z["L1_ffn_shared_w2"].astype(np.float32).T),
        "sp1T": np.ascontiguousarray(z["L1_ffn_spec_w1"].astype(np.float32).transpose(0, 2, 1)),
        "sp2T": np.ascontiguousarray(z["L1_ffn_spec_w2"].astype(np.float32).transpose(0, 2, 1)),
    }


def build_stage4_program(seq=2048, tc=512):
    nc = bass.Bass("TRN2", target_bir_lowering=False)
    ntok = 2 * seq
    io = {}
    io["R3"] = nc.dram_tensor("R3", [DM, ntok], F32, kind="ExternalInput").ap()
    for name, shape in [
        ("ident", [128, 128]), ("iota8", [128, 8]), ("rb_tiled", [128, 8]),
        ("rwT", [1024, 8]), ("b1all", [128, 288]), ("spec_b2", [8, 1024]),
        ("shared_b2", [1, 1024]), ("sw1T", [1024, 4096]), ("sw2T", [4096, 1024]),
        ("sp1T", [8, 1024, 4096]), ("sp2T", [8, 4096, 1024]),
    ]:
        io[name] = nc.dram_tensor(name, shape, F32, kind="ExternalInput").ap()
    io["OUT"] = nc.dram_tensor("OUT", [2, seq, 1024], F32, kind="ExternalOutput").ap()
    em = Em(nc)
    build_stage4(nc, em, io, seq=seq, tc=tc)
    return nc, em


def host_prep_stage1(z, core, seq=2048):
    def pcol(v, c):
        return np.ascontiguousarray(np.asarray(v, np.float32).reshape(c, 128).T)

    def pcol2(v, c, w):
        return np.ascontiguousarray(
            np.asarray(v, np.float32).reshape(c, 128, w).transpose(1, 0, 2))

    x = z["L0_attn_x"][2 * core:2 * core + 2, :seq]
    return {
        "x": np.ascontiguousarray(x),
        "ident": np.eye(128, dtype=np.float32),
        "ln_w": pcol(z["L0_attn_ln_w"], 8),
        "ln_b": pcol(z["L0_attn_ln_b"], 8),
        "conv_w": pcol2(z["L0_attn_conv_w"], 16, 4),
        "conv_b": pcol(z["L0_attn_conv_b"], 16),
        "dt_proj_b": pcol(z["L0_attn_dt_proj_b"], 16),
        "D_param": pcol(z["L0_attn_D_param"], 16),
        "A_log": pcol2(z["L0_attn_A_log"], 16, 16),
        "in_proj_wT": np.ascontiguousarray(z["L0_attn_in_proj_w"].T),
        "x_proj_wT": np.ascontiguousarray(z["L0_attn_x_proj_w"].T),
        "dt_proj_wT": np.ascontiguousarray(z["L0_attn_dt_proj_w"].T),
        "out_proj_wT": np.ascontiguousarray(z["L0_attn_out_proj_w"].T),
    }


def build_stage1_program(seq=2048, tc=256):
    nc = bass.Bass("TRN2", target_bir_lowering=False)
    ntok = 2 * seq
    io = {}
    io["x"] = nc.dram_tensor("x", [2, seq, 1024], F32, kind="ExternalInput").ap()
    io["ident"] = nc.dram_tensor("ident", [128, 128], F32, kind="ExternalInput").ap()
    for name, shape in [
        ("ln_w", [128, 8]), ("ln_b", [128, 8]), ("conv_w", [128, 16, 4]),
        ("conv_b", [128, 16]), ("dt_proj_b", [128, 16]), ("D_param", [128, 16]),
        ("A_log", [128, 16, 16]), ("in_proj_wT", [1024, 4096]),
        ("x_proj_wT", [2048, 96]), ("dt_proj_wT", [64, 2048]),
        ("out_proj_wT", [2048, 1024]),
    ]:
        io[name] = nc.dram_tensor(name, shape, F32, kind="ExternalInput").ap()
    io["R1"] = nc.dram_tensor("R1", [DM, ntok], F32, kind="ExternalOutput").ap()
    em = Em(nc)
    build_stage1(nc, em, io, seq=seq, tc=tc)
    return nc, em


def build_fused_program(seq=2048):
    nc = bass.Bass("TRN2", target_bir_lowering=False)
    ntok = 2 * seq
    io = {}
    io["x"] = nc.dram_tensor("x", [2, seq, 1024], F32, kind="ExternalInput").ap()
    names = [
        ("ident", [128, 128]),
        ("ln_w", [128, 8]), ("ln_b", [128, 8]), ("conv_w", [128, 16, 4]),
        ("conv_b", [128, 16]), ("dt_proj_b", [128, 16]), ("D_param", [128, 16]),
        ("A_log", [128, 16, 16]), ("in_proj_wT", [1024, 4096]),
        ("x_proj_wT", [2048, 96]), ("dt_proj_wT", [64, 2048]),
        ("out_proj_wT", [2048, 1024]),
        ("scm", [128, 8, 2, 256]), ("Mg", [128, 16, 128]),
        ("sel16T", [16, 128]), ("sel128", [128, 16]),
        ("nwT4", [1024, 4, 255]), ("nb4", [1, 4, 255]),
        ("b14", [4, 256, 16]), ("b24", [4, 256, 1024]),
        ("q_w1T", [1024, 4096]), ("q_w2T", [4096, 1024]),
        ("k_w1T", [1024, 4096]), ("k_w2T", [4096, 1024]),
        ("v_w1T", [1024, 4096]), ("v_w2T", [4096, 1024]),
        ("f_w1T", [1024, 4096]), ("f_w2T", [4096, 1024]),
        ("seg2c", [128, 8]), ("kmask", [128, 8]),
        ("mTd", [128, 128]), ("mUp", [128, 128]),
        ("qg", [128, 8]), ("kg", [128, 8]),
        ("Wq", [1024, 1024]), ("Wkv", [1024, 128]), ("WkP", [128, 1024]),
        ("Wv", [128, 512]), ("WoS", [512, 1024]),
        ("iota8", [128, 8]), ("rb_tiled", [128, 8]),
        ("rwT", [1024, 8]), ("b1all", [128, 288]), ("spec_b2", [8, 1024]),
        ("shared_b2", [1, 1024]), ("sw1T", [1024, 4096]), ("sw2T", [4096, 1024]),
        ("sp1T", [8, 1024, 4096]), ("sp2T", [8, 4096, 1024]),
    ]
    for name, shape in names:
        io[name] = nc.dram_tensor(name, shape, F32, kind="ExternalInput").ap()
    io["R1"] = nc.dram_tensor("R1", [DM, ntok], F32, kind="Internal").ap()
    io["R2"] = nc.dram_tensor("R2", [DM, ntok], F32, kind="Internal").ap()
    io["R3"] = nc.dram_tensor("R3", [DM, ntok], F32, kind="Internal").ap()
    io["OUT"] = nc.dram_tensor("OUT", [2, seq, 1024], F32, kind="ExternalOutput").ap()
    em = Em(nc)
    build_stage1(nc, em, io, seq=seq, tc=256)
    em.set_phase(em.frontier())
    build_stage2(nc, em, io, seq=seq, tc=512)
    em.set_phase(em.frontier())
    build_stage3(nc, em, io, seq=seq, tc=512)
    em.set_phase(em.frontier())
    build_stage4(nc, em, io, seq=seq, tc=512)
    return nc, em


def host_prep_fused(z, core, seq=2048):
    d = {}
    for f in (host_prep_stage1, host_prep_stage2, host_prep_stage3,
              host_prep_stage4):
        for k, v in f(z, core, seq=seq).items():
            if k in d:
                assert d[k].shape == v.shape, k
            else:
                d[k] = v
    return d


def kernel(**inputs):
    ncores = 8
    seq = 2048
    z = inputs
    nc, em = build_fused_program(seq=seq)
    em.build()
    maps = [host_prep_fused(z, c, seq=seq) for c in range(ncores)]
    res = run_bass_kernel_spmd(nc, maps, core_ids=list(range(ncores)))
    out = np.concatenate(
        [np.asarray(res.results[c]["OUT"]) for c in range(ncores)], axis=0)
    return out.astype(np.float32, copy=False)


def kernel_unfused(**inputs):
    ncores = 8
    seq = 2048
    z = inputs

    nc1, em1 = build_stage1_program(seq=seq, tc=256)
    em1.build()
    maps1 = [host_prep_stage1(z, c, seq=seq) for c in range(ncores)]
    res1 = run_bass_kernel_spmd(nc1, maps1, core_ids=list(range(ncores)))

    nc2, em2 = build_stage2_program(seq=seq, tc=512)
    em2.build()
    maps2 = [host_prep_stage2(z, c, seq=seq) for c in range(ncores)]
    for c in range(ncores):
        maps2[c]["R1"] = np.ascontiguousarray(np.asarray(res1.results[c]["R1"]))
    res2 = run_bass_kernel_spmd(nc2, maps2, core_ids=list(range(ncores)))

    nc3, em3 = build_stage3_program(seq=seq, tc=512)
    em3.build()
    maps3 = [host_prep_stage3(z, c, seq=seq) for c in range(ncores)]
    for c in range(ncores):
        maps3[c]["R2"] = np.ascontiguousarray(np.asarray(res2.results[c]["R2"]))
    res3 = run_bass_kernel_spmd(nc3, maps3, core_ids=list(range(ncores)))

    nc4, em4 = build_stage4_program(seq=seq, tc=512)
    em4.build()
    maps4 = [host_prep_stage4(z, c, seq=seq) for c in range(ncores)]
    for c in range(ncores):
        maps4[c]["R3"] = np.ascontiguousarray(np.asarray(res3.results[c]["R3"]))
    res4 = run_bass_kernel_spmd(nc4, maps4, core_ids=list(range(ncores)))

    out = np.concatenate(
        [np.asarray(res4.results[c]["OUT"]) for c in range(ncores)], axis=0)
    return out.astype(np.float32, copy=False)
